# version 05
# speedup vs baseline: 1.4399x; 1.0100x over previous
import numpy as np
import ml_dtypes
import concourse.bass as bass
import concourse.mybir as mybir
from concourse.bass_utils import run_bass_kernel_spmd

F32 = mybir.dt.float32
BF16 = mybir.dt.bfloat16
AF = mybir.ActivationFunctionType
ALU = mybir.AluOpType
AX = mybir.AxisListType
BF = ml_dtypes.bfloat16

ENGINES = ("tensor", "vector", "scalar", "gpsimd", "sync")


class Op:
    __slots__ = ("eng", "fn", "deps", "inc", "cum", "name", "collective")

    def __init__(self, eng, fn, deps, inc, name, collective=False):
        self.eng = eng
        self.fn = fn
        self.deps = [d for d in deps if d is not None]
        self.inc = inc
        self.cum = None
        self.name = name
        self.collective = collective


class Prog:

    def __init__(self, nc):
        self.nc = nc
        self.ops = []

    def op(self, eng, fn, deps=(), name=""):
        o = Op(eng, fn, deps, 1, name)
        self.ops.append(o)
        return o

    def dma(self, eng, fn, deps=(), name=""):
        o = Op(eng, fn, deps, 16, name)
        self.ops.append(o)
        return o

    def cc(self, fn, deps=(), name="cc"):
        o = Op("gpsimd", fn, deps, 1, name, collective=True)
        self.ops.append(o)
        return o

    def marker(self, eng, deps, name="mark"):
        return self.op(eng, lambda e: e.nop(), deps=deps, name=name)

    def phase_wrap(self, start, gate, name="pg"):
        ops = self.ops[start:]
        if gate is not None:
            seen = set()
            for o in ops:
                if o.eng not in seen:
                    seen.add(o.eng)
                    o.deps.append(gate)
                    if len(seen) == len(ENGINES):
                        break
        return self.marker("sync", list(ops), name=name)

    def emit(self, block, sems, dma_pools):
        needed = set()
        for o in self.ops:
            for d in o.deps:
                needed.add(id(d))
        per_eng = {e: [] for e in ENGINES}
        for o in self.ops:
            per_eng[o.eng].append(o)
        cum = {e: 0 for e in ENGINES}
        dma_ct = {e: 0 for e in ENGINES}
        cc_ct = 0
        for e in ENGINES:
            for o in per_eng[e]:
                if id(o) not in needed:
                    continue
                if o.collective:
                    o.cum = ("cc", cc_ct, 1)
                    cc_ct += 1
                elif o.inc == 16:
                    pool = dma_pools[e]
                    i = dma_ct[e]
                    o.cum = (e, i % len(pool), 16 * (i // len(pool) + 1))
                    dma_ct[e] += 1
                else:
                    cum[e] += 1
                    o.cum = (e, None, cum[e])
        self.n_cc = cc_ct

        def sem_of(tag):
            e, slot, val = tag
            if e == "cc":
                return dma_pools["cc"][slot], val
            if slot is None:
                return sems[e], val
            return dma_pools[e][slot], val

        def runner(eng_name):
            def run(eng):
                watermark = {}
                for o in per_eng[eng_name]:
                    waits = {}
                    for d in o.deps:
                        assert d.cum is not None, f"dep {d.name} of {o.name} unmarked"
                        sem, val = sem_of(d.cum)
                        key = id(sem)
                        if val > waits.get(key, (None, 0))[1]:
                            waits[key] = (sem, val)
                    for key in sorted(waits):
                        sem, val = waits[key]
                        if val > watermark.get(key, 0):
                            eng.wait_ge(sem, val)
                            watermark[key] = val
                    if o.cum is not None and o.inc == 16 and not o.collective:
                        sem, val = sem_of(o.cum)
                        if val > 16:
                            eng.wait_ge(sem, val - 16)
                            watermark[id(sem)] = max(watermark.get(id(sem), 0), val - 16)
                    inst = o.fn(_TeProxy(eng) if eng_name == "tensor" else eng)
                    if o.cum is not None:
                        sem, val = sem_of(o.cum)
                        if o.collective:
                            inst.then_inc(sem)
                        else:
                            inst.then_inc(sem, o.inc)
            return run

        block.tensor(runner("tensor"))
        block.vector(runner("vector"))
        block.scalar(runner("scalar"))
        block.gpsimd(runner("gpsimd"))
        block.sync(runner("sync"))


class _TeProxy:

    def __init__(self, te):
        self._te = te

    def matmul(self, *a, **k):
        k.setdefault("skip_group_check", True)
        return self._te.matmul(*a, **k)

    def __getattr__(self, n):
        return getattr(self._te, n)


def _bf(x):
    return np.ascontiguousarray(x, dtype=BF)


def _f32(x):
    return np.ascontiguousarray(x, dtype=np.float32)



def prep_mamba_weights(inp):
    w = {}
    w["w_inT"] = _f32(np.asarray(inp["L0_attn_in_proj_w"]).T)
    cw = np.asarray(inp["L0_attn_conv_w"])[:, 0, :]
    cd = np.zeros((128, 72, 32), dtype=BF)
    ii = np.arange(32)
    for t in range(18):
        for tau in range(4):
            for sub in range(4):
                cd[sub * 32 + ii, t * 4 + tau, ii] = cw[t * 128 + sub * 32 + ii, tau].astype(BF)
    w["cd"] = cd
    w["conv_b"] = _f32(np.asarray(inp["L0_attn_conv_b"]).reshape(18, 128).T)
    w["dtb"] = _f32(np.asarray(inp["L0_attn_dt_bias"]).reshape(32, 1))
    w["alog"] = _f32(np.asarray(inp["L0_attn_A_log"]).reshape(32, 1))
    w["drep"] = _f32(np.repeat(np.asarray(inp["L0_attn_D_param"]), 64).reshape(16, 128).T)
    w["nw"] = _f32(np.asarray(inp["L0_attn_norm_w"]).reshape(16, 128).T)
    w["w_outT"] = _f32(np.asarray(inp["L0_attn_out_proj_w"]).T)
    w["ones2"] = np.ones((2, 128), dtype=BF)
    w["onesf"] = np.ones((1, 128), dtype=np.float32)
    esel = np.zeros((65, 4096), dtype=np.float32)
    for q in range(8):
        for hl in range(4):
            k = 4 * q + hl
            esel[k, q * 512 + hl * 128: q * 512 + (hl + 1) * 128] = -1.0
            esel[k + 32, q * 512 + hl * 128: q * 512 + (hl + 1) * 128] = -1.0
    esel[64, :] = -1e4
    w["esel"] = esel.astype(BF)
    tri = np.zeros((128, 512), dtype=np.float32)
    for k in range(128):
        for hl in range(4):
            tri[k, hl * 128 + k:(hl + 1) * 128] = 1e4
    w["tri"] = tri.astype(BF)
    w["idtb"] = np.eye(128, dtype=BF)
    return w


_MAMBA_DRAM = (("x", [1024, -1], F32), ("w_inT", [1024, 4384], F32), ("cd", [128, 72, 32], BF16),
               ("conv_b", [128, 18], F32), ("dtb", [32, 1], F32), ("alog", [32, 1], F32),
               ("drep", [128, 16], F32), ("nw", [128, 16], F32), ("w_outT", [2048, 1024], F32),
               ("ones2", [2, 128], BF16), ("onesf", [1, 128], F32), ("esel", [65, 4096], BF16),
               ("tri", [128, 512], BF16), ("idtb", [128, 128], BF16))


def declare_dram(nc, specs, T, pre=""):
    dr = {}
    for nm, shape, dt in specs:
        shape = [T if s == -1 else s for s in shape]
        dr[nm] = nc.dram_tensor(pre + nm, shape, dt, kind="ExternalInput").ap()
    return dr


def make_sems(nc, ses):
    sems = {e: ses.enter_context(nc.semaphore("sem_" + e)) for e in ENGINES}
    dma_pools = {"sync": [ses.enter_context(nc.semaphore(f"dp_sy{i}")) for i in range(24)],
                 "scalar": [ses.enter_context(nc.semaphore(f"dp_sc{i}")) for i in range(20)],
                 "gpsimd": [ses.enter_context(nc.semaphore(f"dp_gp{i}")) for i in range(12)],
                 "vector": [ses.enter_context(nc.semaphore(f"dp_ve{i}")) for i in range(8)],
                 "tensor": [ses.enter_context(nc.semaphore("dp_te0"))],
                 "cc": [ses.enter_context(nc.semaphore(f"dp_cc{i}")) for i in range(4)]}
    return sems, dma_pools


def emit_mamba(nc, p, dr, x1_d, T=4096, gate=None, pre="m"):
    TT_ = T // 512
    start_ix = len(p.ops)
    x_d = dr["x"]
    winT_d = dr["w_inT"]
    cd_d = dr["cd"]
    convb_d = dr["conv_b"]
    dtb_d = dr["dtb"]
    alog_d = dr["alog"]
    drep_d = dr["drep"]
    nw_d = dr["nw"]
    woutT_d = dr["w_outT"]
    ones2_d = dr["ones2"]
    onesf_d = dr["onesf"]
    esel_d = dr["esel"]
    tri_d = dr["tri"]
    idtb_d = dr["idtb"]
    zdr = nc.dram_tensor(pre + "_zdr", [128, 16, T], BF16, kind="Internal").ap()
    csdr = nc.dram_tensor(pre + "_csdr", [2, 32, T], BF16, kind="Internal").ap()

    from contextlib import ExitStack
    es = ExitStack()
    sb = lambda name, shape, dt=F32: es.enter_context(nc.sbuf_tensor(pre + name, shape, dt))
    ps = lambda name, shape, dt=F32: es.enter_context(nc.psum_tensor(pre + name, shape, dt))

    w_in = sb("w_in", [128, 8 * 4384], BF16)
    cd = sb("cd_sb", [128, 72 * 32], BF16)
    convb = sb("convb_sb", [128, 18])
    dtb = sb("dtb_sb", [32, 1])
    alog = sb("alog_sb", [32, 1])
    negA = sb("negA", [32, 1])
    drep = sb("drep_sb", [128, 16])
    nw = sb("nw_sb", [128, 16])
    ones2 = sb("ones2_sb", [2, 128], BF16)
    onesf = sb("onesf_sb", [1, 128])
    ones128 = sb("ones128", [128, 1], BF16)
    esel = sb("esel_sb", [65, 4096], BF16)
    tri = sb("tri_sb", [128, 512], BF16)
    idtb = sb("idtb_sb", [128, 128], BF16)
    zeros32 = sb("zeros32", [32, 128])

    xb = sb("xb", [128, 8, 512], BF16)
    xBC = sb("xBC", [128, 18, 528], BF16)
    halo = sb("halo", [128, 18, 3], BF16)
    zst = sb("zst", [128, 2, 512], BF16)
    dtraw = sb("dtraw", [32, 512])
    dts1 = sb("dts1", [32, 512])
    dts2 = sb("dts2", [32, 512])
    csf = sb("csf", [32, 512])
    u2 = sb("u2", [65, 512], BF16)
    cshl = sb("cshl", [32, 2, 512], BF16)
    csfl = sb("csfl", [2, 4096], BF16)
    ecst = sb("ecst", [128, 512], BF16)
    dtot = sb("dtot", [128, 32], BF16)
    ltd = sb("ltd", [128, 4096], BF16)
    wsl = sb("wsl", [128, 32], BF16)
    ctil = sb("ctil", [128, 4096], BF16)
    xtm = sb("xtm", [128, 2048], BF16)
    x2t = sb("x2t", [128, 2048], BF16)
    btm = sb("btm", [128, 128], BF16)
    gt = sb("gt", [128, 128], BF16)
    S = sb("S", [128, 2048])
    Sbf = sb("Sbf", [128, 2048], BF16)
    zbk = sb("zbk", [128, 16, 128], BF16)
    tmpZ = sb("tmpZ", [128, 2048], BF16)
    yf = sb("yf", [128, 2048], BF16)
    ytil = sb("ytil", [128, 16, 256], BF16)
    wout = sb("wout", [128, 2, 1024], BF16)
    uln = sb("uln", [128, 8, 512], BF16)
    lrow = sb("lrow", [1, 512])
    lrow2 = sb("lrow2", [1, 512])
    stg = sb("stg", [128, 512])

    pip = ps("pip", [128, 2, 512])
    pcv = ps("pcv", [128, 2, 512])
    pdd = ps("pdd", [128, 2, 512])
    psm = ps("psm", [128, 512])
    pec = ps("pec", [128, 512])
    pbanks = [pip[:, 0, :], pip[:, 1, :], pcv[:, 0, :], pcv[:, 1, :],
              pdd[:, 0, :], pdd[:, 1, :], psm[:], pec[:]]

    with es:
        ld = []
        for kc in range(8):
            ld.append(p.dma("gpsimd", (lambda kc: lambda g: g.dma_start(
                out=w_in[:, kc * 4384:(kc + 1) * 4384],
                in_=winT_d[kc * 128:(kc + 1) * 128, :]))(kc), name=f"ld_win{kc}"))
        ld.append(p.dma("sync", lambda e: e.dma_start(
            out=cd[:].rearrange("p (t f) -> p t f", f=32), in_=cd_d), name="ld_cd"))
        for nm, s_, d_ in (("convb", convb, convb_d), ("dtb", dtb, dtb_d), ("alog", alog, alog_d),
                           ("drep", drep, drep_d), ("nw", nw, nw_d), ("ones2", ones2, ones2_d),
                           ("onesf", onesf, onesf_d), ("esel", esel, esel_d), ("tri", tri, tri_d),
                           ("idtb", idtb, idtb_d)):
            ld.append(p.dma("sync", (lambda s_, d_: lambda e: e.dma_start(out=s_[:], in_=d_))(s_, d_),
                            name="ld_" + nm))
        LD = list(ld)
        init0 = p.op("vector", lambda v: v.memset(zeros32[:], 0.0), name="z32")
        p.op("vector", lambda v: v.memset(ones128[:], 1.0), name="o128")
        initH = p.op("vector", lambda v: v.memset(xBC[:, :, 13:16], 0.0), name="halo0")
        initU = p.op("vector", lambda v: v.memset(u2[64:65, :], 1.0), name="u2o")
        initS = p.op("vector", lambda v: v.memset(S[:], 0.0), name="S0")
        initSb = p.op("vector", lambda v: v.memset(Sbf[:], 0.0), name="Sbf0")
        expA = p.op("scalar", lambda s: s.activation(negA[:], alog[:], AF.Exp), deps=LD, name="expA")
        negAo = p.op("vector", lambda v: v.tensor_scalar_mul(negA[:], negA[:], -1.0), deps=[expA], name="negA")

        ip_evac = {}
        cv_evac = {}
        xb_users = {}
        zdma = {}
        halo_restore = None
        wout_users = []
        wl_mms = [None, None]
        Sbf_ready = initSb
        S_ready = initS
        chunk_mark = initH
        tile_mark = None
        prev_x1s = []
        zsl_hist = {}
        csfl_users = []
        uln_free = []
        op_free = []
        uln_half0 = None

        def mk(*a):
            return "_".join(str(x) for x in a)

        for t in range(TT_):
            xldt = p.dma("gpsimd", (lambda t: lambda g: g.dma_start(
                out=xb[:],
                in_=x_d[:, t * 512:(t + 1) * 512].rearrange("(kc p) t -> p kc t", p=128)))(t),
                deps=xb_users.get(t - 1, []), name=mk("xld", t))
            xb_users[t] = []

            for m in range(35):
                bank = m % 2
                rows = 128 if m < 34 else 32
                mdeps = [xldt] + (LD[:8] if t == 0 else [])
                key = (t, m - 2) if m >= 2 else (t - 1, 32 + m)
                if key in ip_evac:
                    mdeps.append(ip_evac[key])
                if m < 2 and tile_mark is not None:
                    mdeps.append(tile_mark)
                lastmm = None
                for k in range(8):
                    lastmm = p.op("tensor", (lambda m, k, bank, rows: lambda te: te.matmul(
                        pip[0:rows, bank, :],
                        w_in[:, k * 4384 + m * 128: k * 4384 + m * 128 + rows],
                        xb[:, k, :], start=(k == 0), stop=(k == 7)))(m, k, bank, rows),
                        deps=mdeps if k == 0 else [], name=mk("ip", t, m, k))
                xb_users[t].append(lastmm)
                if m < 16:
                    g4 = m // 2
                    zkey = (t, g4 - 1) if g4 > 0 else (t - 1, 7)
                    zwar = [zdma[zkey]] if zkey in zdma else []
                    ev = p.op("scalar", (lambda m, bank: lambda s: s.copy(
                        zst[:, m % 2, :], pip[:, bank, :]))(m, bank),
                        deps=[lastmm] + zwar, name=mk("ipe", t, m))
                    ip_evac[(t, m)] = ev
                    if m % 2 == 1:
                        zdma[(t, g4)] = p.dma("sync", (lambda t, g4: lambda e: e.dma_start(
                            out=zdr[:, g4 * 2:(g4 + 1) * 2, t * 512:(t + 1) * 512], in_=zst[:]))(t, g4),
                            deps=[ip_evac[(t, mm)] for mm in range(g4 * 2, g4 * 2 + 2)],
                            name=mk("zd", t, g4))
                elif m < 34:
                    ft = m - 16
                    ev = p.op("vector", (lambda ft, bank: lambda v: v.tensor_copy(
                        xBC[:, ft, 16:528], pip[:, bank, :]))(ft, bank),
                        deps=[lastmm] + ([halo_restore] if halo_restore is not None else []),
                        name=mk("ipe", t, m))
                    ip_evac[(t, m)] = ev
                else:
                    ev = p.op("vector", (lambda bank: lambda v: v.tensor_copy(
                        dtraw[:], pip[0:32, bank, :]))(bank), deps=[lastmm], name=mk("ipe", t, m))
                    ip_evac[(t, m)] = ev

            hs = p.op("vector", lambda v: v.tensor_copy(halo[:], xBC[:, :, 525:528]),
                      deps=[ip_evac[(t, m)] for m in range(16, 34)], name=mk("hsave", t))
            silu_ev = {}
            last_tap = None
            for ft in range(18):
                bank = ft % 2
                cdeps = [ip_evac[(t, 16 + ft)], LD[8]]
                if ft < 2:
                    if halo_restore is not None:
                        cdeps.append(halo_restore)
                    if tile_mark is not None:
                        cdeps.append(tile_mark)
                key = (t, ft - 2)
                if key in cv_evac:
                    cdeps.append(cv_evac[key])
                for tau in range(4):
                    for sub2 in range(4):
                        mmv = p.op("tensor", (lambda ft, tau, sub2, bank: lambda te: te.matmul(
                            pcv[sub2 * 32:(sub2 + 1) * 32, bank, :],
                            cd[sub2 * 32:(sub2 + 1) * 32, (ft * 4 + tau) * 32:(ft * 4 + tau + 1) * 32],
                            xBC[sub2 * 32:(sub2 + 1) * 32, ft, 13 + tau:13 + tau + 512],
                            start=(tau == 0), stop=(tau == 3),
                            tile_position=(sub2 * 32, sub2 * 32)))(ft, tau, sub2, bank),
                            deps=cdeps if (tau == 0 and sub2 == 0) else [], name=mk("cv", t, ft, tau, sub2))
                last_tap = mmv
                ev = p.op("scalar", (lambda ft, bank: lambda s: s.activation(
                    xBC[:, ft, 16:528], pcv[:, bank, :], AF.Silu, bias=convb[:, ft:ft + 1]))(ft, bank),
                    deps=[mmv, hs], name=mk("cse", t, ft))
                cv_evac[(t, ft)] = ev
                silu_ev[ft] = ev
            halo_restore = p.op("vector", lambda v: v.tensor_copy(xBC[:, :, 13:16], halo[:]),
                                deps=[last_tap, hs], name=mk("hrest", t))

            dt_ev = ip_evac[(t, 34)]
            o1 = p.op("scalar", lambda s: s.activation(dts1[:], dtraw[:], AF.Abs, bias=dtb[:]),
                      deps=[dt_ev] + (LD if t == 0 else []), name=mk("dtabs", t))
            o2 = p.op("scalar", lambda s: s.activation(dts1[:], dts1[:], AF.Exp, scale=-1.0),
                      deps=[o1], name=mk("dtexp", t))
            o3 = p.op("scalar", lambda s: s.activation(dts1[:], dts1[:], AF.Ln, bias=1.0),
                      deps=[o2], name=mk("dtln1", t))
            o4 = p.op("vector", lambda v: v.tensor_scalar_add(dts2[:], dtraw[:], dtb[:]),
                      deps=[dt_ev], name=mk("dtadd", t))
            o5 = p.op("vector", lambda v: v.tensor_single_scalar(dts2[:], dts2[:], 0.0, ALU.max),
                      deps=[o4], name=mk("dtmax", t))
            o6 = p.op("vector", lambda v: v.tensor_add(dtraw[:], dts1[:], dts2[:]),
                      deps=[o3, o5], name=mk("dt", t))
            o7 = p.op("vector", lambda v: v.tensor_scalar_mul(dts2[:], dtraw[:], negA[:]),
                      deps=[o6, negAo], name=mk("a", t))
            cs_ops = []
            for c4 in range(4):
                cs_ops.append(p.op("vector", (lambda c4: lambda v: v.tensor_tensor_scan(
                    csf[:, c4 * 128:(c4 + 1) * 128], dts2[:, c4 * 128:(c4 + 1) * 128], zeros32[:],
                    0.0, ALU.add, ALU.add))(c4), deps=[o7, init0], name=mk("cs", t, c4)))
            lnd = p.op("scalar", lambda s: s.activation(dts1[:], dtraw[:], AF.Ln),
                       deps=[o6, o3], name=mk("lndt", t))
            uo = p.op("vector", lambda v: v.tensor_sub(dts1[:], csf[:], dts1[:]),
                      deps=[lnd] + cs_ops, name=mk("u", t))
            uhi = p.op("vector", lambda v: v.tensor_copy(u2[0:32, :], dts1[:]),
                       deps=[uo, initU, chunk_mark], name=mk("uhi", t))
            ulo = p.op("vector", lambda v: v.tensor_sub(u2[32:64, :], dts1[:], u2[0:32, :]),
                       deps=[uhi], name=mk("ulo", t))
            chi = p.op("vector", lambda v: v.tensor_copy(cshl[:, 0, :], csf[:]),
                       deps=cs_ops, name=mk("chi", t))
            clo = p.op("vector", lambda v: v.tensor_sub(cshl[:, 1, :], csf[:], cshl[:, 0, :]),
                       deps=[chi], name=mk("clo", t))
            bo1 = p.dma("sync", (lambda t: lambda e: e.dma_start(
                out=csdr[:, :, t * 512:(t + 1) * 512].rearrange("r h t -> h r t"), in_=cshl[:]))(t),
                deps=[chi, clo], name=mk("bo1", t))

            for c4 in range(4):
                c = t * 4 + c4
                col0 = 16 + c4 * 128
                cm = [chunk_mark]
                bif = p.dma("sync", (lambda t, c4: lambda e: e.dma_start(
                    out=csfl[:].rearrange("r (h i) -> r h i", h=32),
                    in_=csdr[:, :, t * 512 + c4 * 128: t * 512 + (c4 + 1) * 128]))(t, c4),
                    deps=[bo1] + csfl_users, name=mk("bif", c))
                csfl_users = []
                zrb = p.dma("sync", (lambda t, c4: lambda e: e.dma_start(
                    out=zbk[:],
                    in_=zdr[:, :, t * 512 + c4 * 128: t * 512 + (c4 + 1) * 128]))(t, c4),
                    deps=[zdma[(t, g)] for g in range(8)] +
                         ([zsl_hist[c - 1]] if (c - 1) in zsl_hist else []),
                    name=mk("zrb", c))

                exps = []
                for q in range(8):
                    bank = q % 2
                    ddeps = [bif, ulo]
                    if q >= 2:
                        ddeps.append(exps[q - 2])
                    else:
                        ddeps += cm
                    m1 = p.op("tensor", (lambda q, bank: lambda te: te.matmul(
                        pdd[:, bank, :], ones2[:], csfl[:, q * 512:(q + 1) * 512],
                        start=True, stop=False))(q, bank), deps=ddeps, name=mk("D1", c, q))
                    csfl_users.append(m1)
                    p.op("tensor", (lambda c4, q, bank: lambda te: te.matmul(
                        pdd[:, bank, :], u2[:, c4 * 128:(c4 + 1) * 128],
                        esel[:, q * 512:(q + 1) * 512], start=False, stop=False))(c4, q, bank),
                        name=mk("D2", c, q))
                    m3 = p.op("tensor", (lambda bank: lambda te: te.matmul(
                        pdd[:, bank, :], idtb[:], tri[:], start=False, stop=True))(bank),
                        name=mk("D3", c, q))
                    ex = p.op("scalar", (lambda q, bank: lambda s: s.activation(
                        ltd[:, q * 512:(q + 1) * 512], pdd[:, bank, :], AF.Exp))(q, bank),
                        deps=[m3] + cm, name=mk("Dx", c, q))
                    exps.append(ex)
                wso = p.op("vector", lambda v: v.tensor_copy(
                    wsl[:].unsqueeze(2), ltd[:].rearrange("p (h i) -> p h i", h=32)[:, :, 127:128]),
                    deps=exps + cm, name=mk("w", c))
                gmm = p.op("tensor", (lambda col0: lambda te: te.matmul(
                    psm[:, 0:128], xBC[:, 16, col0:col0 + 128], xBC[:, 17, col0:col0 + 128],
                    start=True, stop=True))(col0),
                    deps=[silu_ev[16], silu_ev[17]] + cm, name=mk("gt", c))
                gev = p.op("scalar", lambda s: s.copy(gt[:], psm[:, 0:128]), deps=[gmm], name=mk("gte", c))
                mt = p.op("vector", lambda v: v.tensor_tensor(
                    ltd[:].rearrange("p (h i) -> p h i", h=32),
                    ltd[:].rearrange("p (h i) -> p h i", h=32),
                    gt[:].unsqueeze(1).broadcast_to([128, 32, 128]), ALU.mult),
                    deps=[gev, wso] + exps, name=mk("mt", c))
                ctil_ops = []
                prev_ct = None
                for q in range(8):
                    emm = p.op("tensor", (lambda q: lambda te: te.matmul(
                        pec[:], ones2[:], csfl[:, q * 512:(q + 1) * 512],
                        start=True, stop=True))(q),
                        deps=([bif] + cm if q == 0 else [prev_ct]), name=mk("ecm", c, q))
                    csfl_users.append(emm)
                    exo = p.op("scalar", (lambda q: lambda s: s.activation(
                        ecst[:], pec[:], AF.Exp))(q), deps=[emm], name=mk("ecx", c, q))
                    dcp = p.op("vector", (lambda q: lambda v: v.tensor_copy(
                        dtot[:, 4 * q:4 * q + 4].unsqueeze(2),
                        ecst[:].rearrange("p (h i) -> p h i", h=4)[:, :, 127:128]))(q),
                        deps=[exo], name=mk("dcp", c, q))
                    ct = p.op("vector", (lambda q, col0: lambda v: v.tensor_tensor(
                        ctil[:, q * 512:(q + 1) * 512].rearrange("p (h i) -> p h i", h=4),
                        ecst[:].rearrange("p (h i) -> p h i", h=4),
                        xBC[:, 17, col0:col0 + 128].unsqueeze(1).broadcast_to([128, 4, 128]),
                        ALU.mult))(q, col0), deps=[exo, silu_ev[17]], name=mk("ct", c, q))
                    ctil_ops += [ct, dcp]
                    prev_ct = p.marker("scalar", [ct, dcp], name=mk("ctm", c, q))
                btmd = p.dma("scalar", (lambda col0: lambda s: s.dma_start_transpose(
                    btm[:], xBC[:, 16, col0:col0 + 128]))(col0),
                    deps=[silu_ev[16]] + cm, name=mk("btm", c))
                xtm_dmas = []
                for ft in range(16):
                    xtm_dmas.append(p.dma("scalar", (lambda ft, col0: lambda s: s.dma_start_transpose(
                        xtm[:, ft * 128:(ft + 1) * 128], xBC[:, ft, col0:col0 + 128]))(ft, col0),
                        deps=[silu_ev[ft]] + cm, name=mk("xtm", c, ft)))
                x2o = p.op("vector", lambda v: v.tensor_tensor(
                    x2t[:].rearrange("p (h q) -> p h q", h=32),
                    xtm[:].rearrange("p (h q) -> p h q", h=32),
                    wsl[:].unsqueeze(2).broadcast_to([128, 32, 64]), ALU.mult),
                    deps=[wso] + xtm_dmas, name=mk("x2", c))
                yevs = []
                for hp in range(16):
                    slot = hp % 2
                    c0_ = 128
                    ydeps = [mt, ctil_ops[2 * (hp // 2)], Sbf_ready, xtm_dmas[hp]] + exps
                    if hp >= 2:
                        ydeps.append(yevs[hp - 2])
                    else:
                        ydeps += cm
                    p.op("tensor", (lambda hp, slot, c0_: lambda te: te.matmul(
                        pdd[0:64, slot, c0_:c0_ + 128], xtm[:, hp * 128:hp * 128 + 64],
                        ltd[:, (2 * hp) * 128:(2 * hp + 1) * 128],
                        start=True, stop=False, tile_position=(0, 0)))(hp, slot, c0_),
                        deps=ydeps, name=mk("ya", c, hp))
                    p.op("tensor", (lambda hp, slot, c0_: lambda te: te.matmul(
                        pdd[64:128, slot, c0_:c0_ + 128], xtm[:, hp * 128 + 64:hp * 128 + 128],
                        ltd[:, (2 * hp + 1) * 128:(2 * hp + 2) * 128],
                        start=True, stop=False, tile_position=(0, 64)))(hp, slot, c0_),
                        name=mk("yb", c, hp))
                    p.op("tensor", (lambda hp, slot, c0_: lambda te: te.matmul(
                        pdd[0:64, slot, c0_:c0_ + 128], Sbf[:, hp * 128:hp * 128 + 64],
                        ctil[:, (2 * hp) * 128:(2 * hp + 1) * 128],
                        start=False, stop=True, tile_position=(0, 0)))(hp, slot, c0_),
                        name=mk("yc", c, hp))
                    ymd = p.op("tensor", (lambda hp, slot, c0_: lambda te: te.matmul(
                        pdd[64:128, slot, c0_:c0_ + 128], Sbf[:, hp * 128 + 64:hp * 128 + 128],
                        ctil[:, (2 * hp + 1) * 128:(2 * hp + 2) * 128],
                        start=False, stop=True, tile_position=(0, 64)))(hp, slot, c0_),
                        name=mk("yd", c, hp))
                    ev = p.op("vector", (lambda hp, slot, col0: lambda v: v.scalar_tensor_tensor(
                        yf[:, hp * 128:(hp + 1) * 128], xBC[:, hp, col0:col0 + 128], drep[:, hp:hp + 1],
                        pdd[:, slot, c0_:c0_ + 128], ALU.mult, ALU.add))(hp, slot, col0),
                        deps=[ymd], name=mk("ye", c, hp))
                    yevs.append(ev)
                sup = []
                for grp in range(4):
                    last = None
                    for hh in range(8):
                        h = grp * 8 + hh
                        gdeps = []
                        if hh == 0:
                            gdeps = [btmd, x2o, prev_ct]
                            if grp > 0:
                                gdeps.append(sup[grp - 1])
                            else:
                                gdeps += cm
                        last = p.op("tensor", (lambda h, hh: lambda te: te.matmul(
                            pec[:, hh * 64:(hh + 1) * 64], btm[:],
                            x2t[:, h * 64:(h + 1) * 64], start=(hh == 0), stop=(hh == 7)))(h, hh),
                            deps=gdeps, name=mk("sl", c, h))
                    s1_ = p.op("vector", (lambda grp: lambda v: v.tensor_tensor(
                        S[:, grp * 512:(grp + 1) * 512].rearrange("p (h q) -> p h q", h=8),
                        S[:, grp * 512:(grp + 1) * 512].rearrange("p (h q) -> p h q", h=8),
                        dtot[:, grp * 8:(grp + 1) * 8].unsqueeze(2).broadcast_to([128, 8, 64]),
                        ALU.mult))(grp), deps=[S_ready, last] + ctil_ops, name=mk("su1", c, grp))
                    s2_ = p.op("vector", (lambda grp: lambda v: v.tensor_add(
                        S[:, grp * 512:(grp + 1) * 512], S[:, grp * 512:(grp + 1) * 512],
                        pec[:]))(grp), deps=[s1_], name=mk("su2", c, grp))
                    sup.append(s2_)
                scast = p.op("vector", lambda v: v.tensor_copy(Sbf[:], S[:]),
                             deps=sup + [yevs[-1]], name=mk("scast", c))
                Sbf_ready = scast
                S_ready = sup[-1]

                zsl = p.op("scalar", lambda s: s.activation(
                    tmpZ[:], zbk[:].rearrange("p a b -> p (a b)"), AF.Tanh, scale=0.5),
                    deps=[zrb] + cm, name=mk("zth", c))
                zs2 = p.op("vector", lambda v: v.scalar_tensor_tensor(
                    tmpZ[:], tmpZ[:], 1.0, zbk[:].rearrange("p a b -> p (a b)"),
                    ALU.add, ALU.mult), deps=[zsl], name=mk("zs2", c))
                zsl_hist[c] = zs2
                gmul = p.op("vector", lambda v: v.tensor_mul(yf[:], yf[:], tmpZ[:]),
                            deps=[zs2] + yevs, name=mk("gmul", c))
                gsqo = p.op("scalar", lambda s: s.activation(tmpZ[:], yf[:], AF.Square),
                            deps=[gmul], name=mk("gsq", c))
                rs_last = None
                for m16 in range(16):
                    rs_last = p.op("tensor", (lambda m16: lambda te: te.matmul(
                        pcv[0:1, 0, 0:128], ones128[:], tmpZ[:, m16 * 128:(m16 + 1) * 128],
                        start=(m16 == 0), stop=(m16 == 15)))(m16),
                        deps=([gsqo, silu_ev[16], silu_ev[17]] + cm if m16 == 0 else []),
                        name=mk("rs", c, m16))
                r1 = p.op("vector", lambda v: v.tensor_scalar(
                    stg[0:1, 0:128], pcv[0:1, 0, 0:128], 1.0 / 2048.0, 4e-5, ALU.mult, ALU.add),
                    deps=[rs_last], name=mk("r1", c))
                r2 = p.op("scalar", lambda s: s.activation(stg[0:1, 128:256], stg[0:1, 0:128], AF.Ln),
                          deps=[r1], name=mk("r2", c))
                r3 = p.op("scalar", lambda s: s.activation(stg[0:1, 0:128], stg[0:1, 128:256],
                                                           AF.Exp, scale=-0.5),
                          deps=[r2], name=mk("r3", c))
                rb = p.op("tensor", lambda te: te.matmul(
                    pcv[:, 1, 0:128], onesf[:], stg[0:1, 0:128], start=True, stop=True),
                    deps=[r3, silu_ev[17]] + cm, name=mk("rb", c))
                yt = p.op("vector", (lambda c4: lambda v: v.tensor_tensor(
                    ytil[:, :, (c4 % 2) * 128:(c4 % 2) * 128 + 128],
                    yf[:].rearrange("p (a b) -> p a b", a=16),
                    pcv[:, 1, 0:128].unsqueeze(1).broadcast_to([128, 16, 128]),
                    ALU.mult))(c4), deps=[rb, gmul] + wout_users, name=mk("yt", c))

                chunk_mark = p.marker("vector",
                                      [yt, scast, r1, wso, mt, x2o, zs2, gsqo] + yevs + ctil_ops,
                                      name=mk("cmark", c))

                if c4 % 2 == 1:
                    hf = c4 // 2
                    op_evs = []
                    prev_users = list(wout_users)
                    wout_users = []
                    for kt in range(16):
                        wbuf = kt % 2
                        wdeps = [LD[0]]
                        if kt < 2:
                            wdeps += prev_users
                        else:
                            wdeps.append(wl_mms[wbuf])
                        wl = p.dma("gpsimd", (lambda kt, wbuf: lambda g: g.dma_start(
                            out=wout[:, wbuf],
                            in_=woutT_d[kt * 128:(kt + 1) * 128, :]))(kt, wbuf),
                            deps=wdeps, name=mk("wl", t, hf, kt))
                        nsc = p.op("vector", (lambda kt, wbuf: lambda v: v.tensor_scalar_mul(
                            wout[:, wbuf], wout[:, wbuf],
                            nw[:, kt:kt + 1]))(kt, wbuf), deps=[wl], name=mk("nsc", t, hf, kt))
                        for m in range(8):
                            st_ = (kt == 0)
                            sp_ = (kt == 15)
                            kdeps = []
                            if m == 0:
                                kdeps = [nsc, yt, chunk_mark]
                                if kt == 0:
                                    kdeps += op_free
                            lastk = p.op("tensor", (lambda m, kt, wbuf, st_, sp_: lambda te: te.matmul(
                                pbanks[m][:, 0:256],
                                wout[:, wbuf, m * 128:(m + 1) * 128],
                                ytil[:, kt, :],
                                start=st_, stop=sp_))(m, kt, wbuf, st_, sp_),
                                deps=kdeps, name=mk("op", t, hf, kt, m))
                            if kt == 15:
                                wout_users.append(lastk)
                                ev = p.op("vector", (lambda m, hf: lambda v: v.scalar_tensor_tensor(
                                    uln[:, m, hf * 256:(hf + 1) * 256], pbanks[m][:, 0:256], 1.0,
                                    xb[:, m, hf * 256:(hf + 1) * 256],
                                    ALU.mult, ALU.add))(m, hf), deps=[lastk] + uln_free,
                                    name=mk("ope", t, hf, m))
                                op_evs.append(ev)
                        wl_mms[wbuf] = lastk
                    xb_users[t].extend(op_evs)
                    op_free = op_evs
                    chunk_mark = p.marker("vector", [chunk_mark] + op_evs, name=mk("omark", t, hf))
                    if hf == 0:
                        uln_half0 = op_evs
                    else:
                        uln_free = []
                        lns_last = None
                        lnq_last = None
                        for m in range(8):
                            sq_ = p.op("scalar", (lambda m: lambda s: s.activation(
                                tmpZ[:, 0:512], uln[:, m, :], AF.Square))(m),
                                deps=[op_evs[m], uln_half0[m]] + ([lnq_last] if lnq_last else []),
                                name=mk("usq", t, m))
                            lns_last = p.op("tensor", (lambda m: lambda te: te.matmul(
                                pcv[0:1, 0, 0:512], ones128[:], uln[:, m, :],
                                start=(m == 0), stop=(m == 7)))(m),
                                deps=[op_evs[m], uln_half0[m]] + (op_evs if m == 0 else []),
                                name=mk("lns", t, m))
                            lnq_last = p.op("tensor", (lambda m: lambda te: te.matmul(
                                pcv[0:1, 1, 0:512], ones128[:], tmpZ[:, 0:512],
                                start=(m == 0), stop=(m == 7)))(m), deps=[sq_], name=mk("lnq", t, m))
                        l1 = p.op("vector", lambda v: v.tensor_scalar_mul(lrow[:], pcv[0:1, 0, 0:512], 1.0 / 1024.0),
                                  deps=[lns_last], name=mk("l1", t))
                        l2 = p.op("vector", lambda v: v.tensor_scalar_mul(lrow2[:], pcv[0:1, 1, 0:512], 1.0 / 1024.0),
                                  deps=[lnq_last], name=mk("l2", t))
                        l3 = p.op("vector", lambda v: v.tensor_mul(stg[0:1, :], lrow[:], lrow[:]),
                                  deps=[l1], name=mk("l3", t))
                        l4 = p.op("vector", lambda v: v.tensor_scalar_add(
                            lrow2[:], lrow2[:], 1e-5),
                            deps=[l2], name=mk("l4e", t))
                        l4 = p.op("vector", lambda v: v.tensor_sub(lrow2[:], lrow2[:], stg[0:1, :]),
                                  deps=[l4, l3], name=mk("l4", t))
                        l5 = p.op("scalar", lambda s: s.activation(stg[0:1, :], lrow2[:], AF.Ln),
                                  deps=[l4], name=mk("l5", t))
                        l6 = p.op("scalar", lambda s: s.activation(lrow2[:], stg[0:1, :], AF.Exp, scale=-0.5),
                                  deps=[l5], name=mk("l6", t))
                        l7 = p.op("vector", lambda v: v.tensor_mul(lrow[:], lrow[:], lrow2[:]),
                                  deps=[l6], name=mk("l7", t))
                        b1 = p.op("tensor", lambda te: te.matmul(
                            pcv[:, 0, :], onesf[:], lrow2[:], start=True, stop=True),
                            deps=[l7, l1, lns_last], name=mk("b1", t))
                        b2 = p.op("tensor", lambda te: te.matmul(
                            pcv[:, 1, :], onesf[:], lrow[:], start=True, stop=True),
                            deps=[l7, l2, lnq_last], name=mk("b2", t))
                        sts = []
                        prev_st = None
                        for m in range(8):
                            a1 = p.op("vector", (lambda m: lambda v: v.tensor_mul(
                                stg[:], uln[:, m, :], pcv[:, 0, :]))(m),
                                deps=[b1] + ([prev_st] if prev_st else []), name=mk("a1", t, m))
                            a2 = p.op("vector", (lambda m: lambda v: v.tensor_sub(
                                stg[:], stg[:], pcv[:, 1, :]))(m),
                                deps=[a1, b2], name=mk("a2", t, m))
                            x1s = p.dma("sync", (lambda t, m: lambda e: e.dma_start(
                                out=x1_d[m * 128:(m + 1) * 128, t * 512:(t + 1) * 512], in_=stg[:]))(t, m),
                                deps=[a2], name=mk("x1s", t, m))
                            prev_st = x1s
                            sts.append(x1s)
                            uln_free.append(x1s)
                        prev_x1s = sts
                        tile_mark = p.marker("vector", sts + op_evs + [b1, b2], name=mk("tmark", t))
                        chunk_mark = tile_mark
                        ip_evac[(t, 32)] = op_evs[0]
                        ip_evac[(t, 33)] = op_evs[1]
                        cv_evac[(t, 16)] = tile_mark
                        cv_evac[(t, 17)] = tile_mark

        p.op("sync", lambda e: e.nop(), deps=prev_x1s, name="fin")
    return p.phase_wrap(start_ix, gate, name=pre + "_done")


def build_mamba(T=4096):
    nc = bass.Bass("TRN2", target_bir_lowering=False)
    p = Prog(nc)
    dr = declare_dram(nc, _MAMBA_DRAM, T)
    x1_d = nc.dram_tensor("x1", [1024, T], F32, kind="ExternalOutput").ap()
    from contextlib import ExitStack
    ses = ExitStack()
    with ses:
        sems, dma_pools = make_sems(nc, ses)
        emit_mamba(nc, p, dr, x1_d, T)
        with nc.Block() as block:
            p.emit(block, sems, dma_pools)
    return nc


_cache = {}


def launch_mamba(inputs):
    x = np.asarray(inputs["L0_attn_x"])
    w = prep_mamba_weights(inputs)
    if "mamba" not in _cache:
        _cache["mamba"] = build_mamba(4096)
    nc = _cache["mamba"]
    in_maps = []
    for core in range(8):
        b = core % 4
        m = {"x": _f32(x[b].T)}
        m.update(w)
        in_maps.append(m)
    res = run_bass_kernel_spmd(nc, in_maps, core_ids=list(range(8)))
    x1 = np.stack([np.asarray(res.results[b]["x1"]).T for b in range(4)], axis=0)
    return x1


def kernel_unfused(**inputs):
    x1 = launch_mamba(inputs)
    x2 = launch_ffn0(inputs, x1)
    x3 = launch_attn1(inputs, x2)
    x4 = launch_ffn1(inputs, x3)
    return np.ascontiguousarray(x4, dtype=np.float32)


def kernel(**inputs):
    x4 = launch_fused(inputs)
    return np.ascontiguousarray(x4, dtype=np.float32)



def prep_ffn0_weights(inp):
    w = {}
    w["wq"] = _f32(np.asarray(inp["L0_ffn_Wq"]))
    w["bq2"] = _f32(np.asarray(inp["L0_ffn_bq"]).reshape(16, 128).T)
    w["gam"] = _f32(np.asarray(inp["L0_ffn_bn_gamma"]).reshape(2, 128).T)
    w["bet"] = _f32(np.asarray(inp["L0_ffn_bn_beta"]).reshape(2, 128).T)
    w["k0T"] = _f32(np.asarray(inp["L0_ffn_sub_key0"]).T)
    w["k1T"] = _f32(np.asarray(inp["L0_ffn_sub_key1"]).T)
    w["dnT"] = _f32(np.asarray(inp["L0_ffn_expert_down"]).T)
    w["upW"] = _f32(np.asarray(inp["L0_ffn_expert_up"]))
    w["onesf"] = np.ones((1, 128), dtype=np.float32)
    return w


_FFN0_DRAM = (("wq", [1024, 2048], F32), ("bq2", [128, 16], F32), ("gam", [128, 2], F32),
              ("bet", [128, 2], F32), ("k0T", [128, 32], F32), ("k1T", [128, 32], F32),
              ("dnT", [1024, 1024], F32), ("upW", [1024, 1024], F32), ("onesf", [1, 128], F32))


def emit_ffn0(nc, p, dr, x2_d, T=2048, total_samples=131072.0, gate=None, pre="f0",
              xfull=None, iso_d=None):
    NTT = T // 512
    start_ix = len(p.ops)
    wq_d = dr["wq"]
    bq2_d = dr["bq2"]
    gam_d = dr["gam"]
    bet_d = dr["bet"]
    k0_d = dr["k0T"]
    k1_d = dr["k1T"]
    dnT_d = dr["dnT"]
    up_d = dr["upW"]
    onesf_d = dr["onesf"]
    xf_d = dr.get("xf")
    qdr = nc.dram_tensor(pre + "_qdr", [128, 16, T], BF16, kind="Internal").ap()
    gehd = nc.dram_tensor(pre + "_gehd", [128, 8, T], BF16, kind="Internal").ap()
    nrmd = nc.dram_tensor(pre + "_nrmd", [16, T], BF16, kind="Internal").ap()
    ccin = nc.dram_tensor(pre + "_ccin", [128, 4], F32, kind="Internal").ap()
    ccout = nc.dram_tensor(pre + "_ccout", [128, 4], F32, kind="Internal").ap()

    from contextlib import ExitStack
    es = ExitStack()
    es1 = ExitStack()
    sb = lambda name, shape, dt=F32: es.enter_context(nc.sbuf_tensor(pre + name, shape, dt))
    sb1 = lambda name, shape, dt=F32: es1.enter_context(nc.sbuf_tensor(pre + name, shape, dt))
    ps = lambda name, shape, dt=F32: es.enter_context(nc.psum_tensor(pre + name, shape, dt))

    x1b = sb("x1b", [128, 8, T], BF16)
    upb = sb("upb", [128, 8, 1024], BF16)
    gst = sb("gst", [128, 8, 512], BF16)
    ghb = sb("ghb", [128, 8, 512], BF16)
    bq2 = sb("bq2_sb", [128, 16])
    gam = sb("gam_sb", [128, 2])
    bet = sb("bet_sb", [128, 2])
    k0 = sb("k0_sb", [128, 32])
    k1 = sb("k1_sb", [128, 32])
    k0b = sb("k0b", [128, 32], BF16)
    k1b = sb("k1b", [128, 32], BF16)
    onesf = sb("onesf_sb", [1, 128])
    ones128 = sb("ones128", [128, 1], BF16)
    onescf = sb("onescf", [128, 1])
    qst = sb("qst", [128, 16, 512], BF16)
    scr = sb("scr", [128, 512], BF16)
    qs_acc = sb("qs_acc", [128, 16, NTT])
    qq_acc = sb("qq_acc", [128, 16, NTT])
    stats = sb("stats", [128, 4])
    bnsc = sb("bnsc", [128, 2])
    bnbi = sb("bnbi", [128, 2])
    stsc = sb("stsc", [128, 16])
    stg = sb("stg", [128, 512])
    wqb = sb1("wqb", [128, 8, 2048], BF16)
    dnb = sb1("dnb", [128, 8, 1024], BF16)
    pA = ps("pA", [128, 2, 512])
    pB = ps("pB", [128, 2, 512])
    pS = ps("pS", [128, 2, 512])
    pN = ps("pN", [128, 2, 512])
    pbanks_f0 = [pA[:, 0, :], pA[:, 1, :], pS[:, 0, :], pS[:, 1, :],
                 pN[:, 0, :], pN[:, 1, :], pB[:, 0, :], pB[:, 1, :]]

    def mk(*a):
        return "_".join(str(x) for x in a)

    with es:
        LD = []
        if xfull is None:
            LD.append(p.dma("gpsimd", lambda g: g.dma_start(
                out=x1b[:], in_=xf_d.rearrange("(kc p) t -> p kc t", p=128)), name="ld_x"))
        else:
            iso_f0 = sb("iso", [128, 1])
            ldi = p.dma("sync", lambda e: e.dma_start(out=iso_f0[:], in_=iso_d), name="ld_iso0")
            xr = xfull.rearrange("(kc p) t -> p kc t", p=128)
            sels = []
            prev_sel = {}
            for kc in range(8):
                qv = qn[:, (kc % 2) * 8:(kc % 2) * 8 + 8, :].rearrange("p k t -> p (k t)")[:, 0:T]
                la = p.dma("gpsimd", (lambda kc, qv: lambda g: g.dma_start(
                    out=qv, in_=xr[:, kc, 0:T]))(kc, qv),
                    deps=prev_sel.get(kc % 2, []), name=mk("selA", kc))
                lb = p.dma("gpsimd", (lambda kc: lambda g: g.dma_start(
                    out=x1b[:, kc, :], in_=xr[:, kc, T:2 * T]))(kc), name=mk("selB", kc))
                s1 = p.op("vector", (lambda kc, qv: lambda v: v.tensor_sub(
                    x1b[:, kc, :], x1b[:, kc, :], qv))(kc, qv), deps=[la, lb], name=mk("sel1", kc))
                s2 = p.op("vector", (lambda kc, qv: lambda v: v.scalar_tensor_tensor(
                    x1b[:, kc, :], x1b[:, kc, :], iso_f0[:], qv, ALU.mult, ALU.add))(kc, qv),
                    deps=[s1, ldi], name=mk("sel2", kc))
                prev_sel[kc % 2] = [s2]
                sels.append(s2)
            LD.append(p.marker("vector", sels, name="ld_x"))
        for k in range(8):
            LD.append(p.dma("gpsimd", (lambda k: lambda g: g.dma_start(
                out=wqb[:, k, :], in_=wq_d[k * 128:(k + 1) * 128, :]))(k), name=f"ld_wq{k}"))
        LD.append(p.dma("gpsimd", lambda g: g.dma_start(
            out=dnb[:], in_=dnT_d.rearrange("(k p) e -> p k e", p=128)), name="ld_dn"))
        LD.append(p.dma("gpsimd", lambda g: g.dma_start(
            out=upb[:], in_=up_d.rearrange("(k p) e -> p k e", p=128)), name="ld_up"))
        for nm, s_, d_ in (("bq2", bq2, bq2_d), ("gam", gam, gam_d), ("bet", bet, bet_d),
                           ("k0", k0, k0_d), ("k1", k1, k1_d), ("onesf", onesf, onesf_d)):
            LD.append(p.dma("sync", (lambda s_, d_: lambda e: e.dma_start(out=s_[:], in_=d_))(s_, d_),
                            name="ld_" + nm))
        o128 = p.op("vector", lambda v: v.memset(ones128[:], 1.0), name="o128")
        ocf = p.op("vector", lambda v: v.memset(onescf[:], 1.0), name="ocf")
        pn0 = p.op("vector", lambda v: v.memset(pN[:], 0.0), name="pn0")

        kh_ops = []
        for kk, (ksb, kbb) in enumerate(((k0, k0b), (k1, k1b))):
            sq = p.op("scalar", (lambda ksb: lambda s: s.activation(stg[:, 0:32], ksb[:], AF.Square))(ksb),
                      deps=LD + kh_ops, name=mk("ksq", kk))
            mm = p.op("tensor", lambda te: te.matmul(pN[0:1, 0, 0:32], onescf[:], stg[:, 0:32],
                      start=True, stop=True), deps=[sq, ocf, pn0], name=mk("knm", kk))
            e1 = p.op("vector", lambda v: v.tensor_single_scalar(stg[0:1, 32:64], pN[0:1, 0, 0:32], 1e-30, ALU.max),
                      deps=[mm], name=mk("kmx", kk))
            e2 = p.op("scalar", lambda s: s.activation(stg[0:1, 64:96], stg[0:1, 32:64], AF.Ln),
                      deps=[e1], name=mk("kln", kk))
            e3 = p.op("scalar", lambda s: s.activation(stg[0:1, 32:64], stg[0:1, 64:96], AF.Exp, scale=-0.5),
                      deps=[e2], name=mk("kex", kk))
            bmm = p.op("tensor", lambda te: te.matmul(pN[:, 1, 0:32], onesf[:], stg[0:1, 32:64],
                       start=True, stop=True), deps=[e3], name=mk("kbc", kk))
            sc = p.op("vector", (lambda ksb, kbb: lambda v: v.tensor_mul(kbb[:], ksb[:], pN[:, 1, 0:32]))(ksb, kbb),
                      deps=[bmm], name=mk("ksc", kk))
            kh_ops.append(sc)

        q_dma = {}
        qst_free = []
        prev_ipe = {}
        prev_sqa = None
        geh_done = {}
        for tt in range(NTT):
            for m in range(16):
                bank = m % 2
                mdeps = LD[:9] if tt == 0 and m < 2 else []
                key = m - 2 if m >= 2 else None
                if key is not None and (tt, key) in prev_ipe:
                    mdeps = [prev_ipe[(tt, key)]]
                elif m < 2 and (tt - 1, 14 + m) in prev_ipe:
                    mdeps = [prev_ipe[(tt - 1, 14 + m)]]
                lastmm = None
                for k in range(8):
                    lastmm = p.op("tensor", (lambda tt, m, k, bank: lambda te: te.matmul(
                        pA[:, bank, :], wqb[:, k, m * 128:(m + 1) * 128],
                        x1b[:, k, tt * 512:(tt + 1) * 512], start=(k == 0), stop=(k == 7)))(tt, m, k, bank),
                        deps=(mdeps + LD[0:1]) if k == 0 else [], name=mk("qm", tt, m, k))
                ev = p.op("scalar", (lambda tt, m, bank: lambda s: s.activation(
                    qst[:, m, :], pA[:, bank, :], AF.Identity, bias=bq2[:, m:m + 1],
                    accum_out=qs_acc[:, m, tt:tt + 1]))(tt, m, bank),
                    deps=[lastmm] + ([q_dma[tt - 1]] if (tt - 1) in q_dma else []), name=mk("qe", tt, m))
                prev_ipe[(tt, m)] = ev
                sqa = p.op("vector", (lambda tt, m: lambda v: v.scalar_tensor_tensor(
                    scr[:], qst[:, m, :], 1.0, qst[:, m, :], ALU.mult, ALU.mult,
                    accum_out=qq_acc[:, m, tt:tt + 1]))(tt, m),
                    deps=[ev] + ([prev_sqa] if prev_sqa is not None else []), name=mk("qq", tt, m))
                prev_sqa = sqa
                prev_ipe[(tt, m)] = sqa
            q_dma[tt] = p.dma("sync", (lambda tt: lambda e: e.dma_start(
                out=qdr[:, :, tt * 512:(tt + 1) * 512], in_=qst[:]))(tt),
                deps=[prev_ipe[(tt, m)] for m in range(16)], name=mk("qpark", tt))

        gelu_evs = []
        prev_de = {}
        g_dma = {}
        for tt in range(NTT):
            for m8 in range(8):
                bank = m8 % 2
                ddeps = []
                key = (tt, m8 - 2) if m8 >= 2 else (tt - 1, 6 + m8)
                if key in prev_de:
                    ddeps.append(prev_de[key])
                if tt == 0 and m8 < 2:
                    ddeps += [LD[0], LD[9]]
                lastmm = None
                for k in range(8):
                    lastmm = p.op("tensor", (lambda tt, m8, k, bank: lambda te: te.matmul(
                        pB[:, bank, :], dnb[:, k, m8 * 128:(m8 + 1) * 128],
                        x1b[:, k, tt * 512:(tt + 1) * 512], start=(k == 0), stop=(k == 7)))(tt, m8, k, bank),
                        deps=ddeps if k == 0 else [], name=mk("dm", tt, m8, k))
                ev = p.op("scalar", (lambda tt, m8, bank: lambda s: s.activation(
                    gst[:, m8, :], pB[:, bank, :], AF.Gelu))(tt, m8, bank),
                    deps=[lastmm] + ([g_dma[tt - 1]] if (tt - 1) in g_dma else []), name=mk("ge", tt, m8))
                prev_de[(tt, m8)] = ev
                gelu_evs.append(ev)
            g_dma[tt] = p.dma("sync", (lambda tt: lambda e: e.dma_start(
                out=gehd[:, :, tt * 512:(tt + 1) * 512], in_=gst[:]))(tt),
                deps=[prev_de[(tt, m8b_)] for m8b_ in range(8)], name=mk("gpark", tt))

        all_acc = [prev_ipe[(tt2, m2)] for tt2 in range(NTT) for m2 in range(16)]
        rops = []
        for i, (acc, par) in enumerate(((qs_acc, 0), (qs_acc, 1), (qq_acc, 0), (qq_acc, 1))):
            ra = p.op("vector", (lambda acc, par, i: lambda v: v.tensor_reduce(
                stsc[:, i * NTT:(i + 1) * NTT].unsqueeze(2), acc[:, par::2, :].transpose([0, 2, 1]),
                AX.X, ALU.add))(acc, par, i), deps=all_acc, name=f"sta{i}")
            rb = p.op("vector", (lambda i: lambda v: v.tensor_reduce(
                stats[:, i:i + 1].unsqueeze(2), stsc[:, i * NTT:(i + 1) * NTT].unsqueeze(1),
                AX.X, ALU.add))(i), deps=[ra], name=f"stb{i}")
            rops.append(rb)
        r1, r2, r3, r4 = rops
        cpy = p.dma("gpsimd", lambda g: g.dma_start(out=ccin, in_=stats[:]),
                    deps=[r1, r2, r3, r4], name="cc_out")
        cc = p.cc(lambda g: g.collective_compute(
            "AllReduce", ALU.add, replica_groups=[[0, 1, 2, 3, 4, 5, 6, 7]],
            ins=[ccin.opt()], outs=[ccout.opt()]), deps=[cpy], name="allreduce")
        cbk = p.dma("gpsimd", lambda g: g.dma_start(out=stats[:], in_=ccout), deps=[cc], name="cc_in")
        b1_ = p.op("vector", lambda v: v.tensor_scalar_mul(stats[:, 0:2], stats[:, 0:2], 1.0 / total_samples),
                   deps=[cbk], name="bn1")
        b2_ = p.op("vector", lambda v: v.tensor_scalar_mul(stats[:, 2:4], stats[:, 2:4], 1.0 / total_samples),
                   deps=[cbk], name="bn2")
        b3_ = p.op("vector", lambda v: v.tensor_mul(bnsc[:], stats[:, 0:2], stats[:, 0:2]),
                   deps=[b1_], name="bn3")
        b4_ = p.op("vector", lambda v: v.tensor_sub(bnsc[:], stats[:, 2:4], bnsc[:]),
                   deps=[b2_, b3_], name="bn4")
        b5_ = p.op("vector", lambda v: v.tensor_scalar_add(bnsc[:], bnsc[:], 1e-5), deps=[b4_], name="bn5")
        b6_ = p.op("scalar", lambda s: s.activation(bnbi[:], bnsc[:], AF.Ln), deps=[b5_], name="bn6")
        b7_ = p.op("scalar", lambda s: s.activation(bnsc[:], bnbi[:], AF.Exp, scale=-0.5), deps=[b6_], name="bn7")
        b8_ = p.op("vector", lambda v: v.tensor_mul(bnsc[:], bnsc[:], gam[:]), deps=[b7_] + LD, name="bn8")
        b9_ = p.op("vector", lambda v: v.tensor_mul(bnbi[:], stats[:, 0:2], bnsc[:]), deps=[b8_, b1_], name="bn9")
        b10 = p.op("vector", lambda v: v.tensor_sub(bnbi[:], bet[:], bnbi[:]), deps=[b9_], name="bn10")

        es1.close()
        qn = sb("qn", [128, 16, 512], BF16)
        nrm_bf = sb("nrm_bf", [128, 2, 2, 512], BF16)
        nrmT = sb("nrmT", [128, 2, 16], BF16)
        invn = sb("invn", [128, 2, 16])
        un = sb("un", [128, 2, 1024])
        wtmb = sb("wtmb", [128, 2, 1024])
        m8b = sb("m8b", [128, 2, 2, 8, 8])
        kept = sb("kept", [128, 2, 2, 256], BF16)
        ssum = sb("ssum", [128, 2, 16])
        pbf = sb("pbf", [128, 2, 2, 256], BF16)
        Fb = sb("Fb", [128, 8192], BF16)
        Wtb = sb("Wtb", [128, 1024], BF16)
        Wfm = sb("Wfm", [128, 8, 512], BF16)
        rhsb = sb("rhsb", [128, 8, 512], BF16)
        uln = sb("uln", [128, 8, 512], BF16)
        usq = sb("usq", [128, 512], BF16)
        lrow = sb("lrow", [1, 512])
        lrow2 = sb("lrow2", [1, 512])
        s_sb_ = [un[:, par, 0:512].rearrange("p (s a) -> p s a", s=2) for par in range(2)]
        sexp_ = [un[:, par, 512:1024].rearrange("p (s a) -> p s a", s=2) for par in range(2)]
        Wtm_ = [wtmb[:, par, :] for par in range(2)]

        prev_q_users = []
        prev_sl_stage = {}
        wfm_users = []
        uln_free = []
        up_users = []
        prev_x2s = []
        ln_free = []
        for tt in range(NTT):
            qbk = p.dma("sync", (lambda tt: lambda e: e.dma_start(
                out=qst[:], in_=qdr[:, :, tt * 512:(tt + 1) * 512]))(tt),
                deps=[q_dma[tt]] + prev_q_users + ([q_dma[NTT - 1]] if tt == 0 else []), name=mk("qbk", tt))
            prev_q_users = []
            bn_ops = []
            for m in range(16):
                o = p.op("vector", (lambda m: lambda v: v.tensor_scalar(
                    qn[:, m, :], qst[:, m, :], bnsc[:, m % 2:m % 2 + 1], bnbi[:, m % 2:m % 2 + 1],
                    ALU.mult, ALU.add))(m), deps=[qbk, b10, b8_], name=mk("bna", tt, m))
                bn_ops.append(o)
                prev_q_users.append(o)
            nrm_copies = []
            for w in range(2):
                wave_mms = []
                for m0 in range(8):
                    m = w * 8 + m0
                    bnk = m0 // 4
                    base = 32 * (m0 % 4)
                    sq = p.op("vector", (lambda m: lambda v: v.scalar_tensor_tensor(
                        scr[:], qn[:, m, :], 1.0, qn[:, m, :], ALU.mult, ALU.mult))(m),
                        deps=[bn_ops[m], prev_sqa] + nrm_copies + wave_mms[-1:], name=mk("nsq", tt, m))
                    prev_sqa = sq
                    mm = p.op("tensor", (lambda bnk, base: lambda te: te.matmul(
                        pN[base:base + 1, bnk, :], ones128[:], scr[:],
                        start=True, stop=True, tile_position=(0, base)))(bnk, base),
                        deps=[sq, pn0], name=mk("nmm", tt, m))
                    wave_mms.append(mm)
                for bnk in range(2):
                    nc_ = p.op("vector", (lambda w, bnk: lambda v: v.tensor_copy(
                        nrm_bf[:, w, bnk, :], pN[:, bnk, :]))(w, bnk),
                        deps=wave_mms, name=mk("nev", tt, w, bnk))
                    nrm_copies.append(nc_)
            ndma = p.dma("sync", (lambda tt: lambda e: e.dma_start(
                out=nrmd[:, tt * 512:(tt + 1) * 512].rearrange("(w b i) t -> i w b t", w=2, b=2),
                in_=nrm_bf[0:128:32, :, :, :]))(tt), deps=nrm_copies, name=mk("ndma", tt))

            for sl in range(4):
                par = (tt * 4 + sl) % 2
                s_sb = s_sb_[par]
                sexp = sexp_[par]
                Wtm = Wtm_[par]
                sdeps = [kh_ops[0], kh_ops[1]]
                smms = []
                for st in range(2):
                    for h in range(8):
                        m = h * 2 + st
                        mm = p.op("tensor", (lambda st, h, m, sl: lambda te: te.matmul(
                            pS[:, st, h * 32:(h + 1) * 32], qn[:, m, sl * 128:(sl + 1) * 128],
                            k0b[:] if st == 0 else k1b[:],
                            start=(h == 0), stop=(h == 7)))(st, h, m, sl),
                            deps=(bn_ops + sdeps + (prev_sl_stage.get(("s", st), []))) if h == 0 else [],
                            name=mk("sc", tt, sl, st, h))
                        smms.append(mm)
                ev0 = p.op("vector", (lambda s_sb: lambda v: v.tensor_copy(
                    s_sb[:, 0, :], pS[:, 0, 0:256]))(s_sb),
                    deps=smms + prev_sl_stage.get(("su", par), []), name=mk("sev0", tt, sl))
                ev1 = p.op("vector", (lambda s_sb: lambda v: v.tensor_copy(
                    s_sb[:, 1, :], pS[:, 1, 0:256]))(s_sb),
                    deps=smms + prev_sl_stage.get(("su", par), []), name=mk("sev1", tt, sl))
                prev_sl_stage[("s", 0)] = [ev0]
                prev_sl_stage[("s", 1)] = [ev1]
                ntr = p.dma("sync", (lambda tt, sl, par: lambda s: s.dma_start_transpose(
                    nrmT[:, par, :], nrmd[:, tt * 512 + sl * 128: tt * 512 + (sl + 1) * 128]))(tt, sl, par),
                    deps=[ndma] + prev_sl_stage.get(("ninv", par), []), name=mk("ntr", tt, sl))
                iv1 = p.op("vector", (lambda par: lambda v: v.tensor_single_scalar(
                    invn[:, par, :], nrmT[:, par, :], 1e-30, ALU.max))(par),
                    deps=[ntr], name=mk("iv1", tt, sl))
                iv2 = p.op("scalar", (lambda par: lambda s: s.activation(
                    invn[:, par, :], invn[:, par, :], AF.Ln))(par),
                    deps=[iv1], name=mk("iv2", tt, sl))
                iv3 = p.op("scalar", (lambda par: lambda s: s.activation(
                    invn[:, par, :], invn[:, par, :], AF.Exp, scale=-0.5))(par),
                    deps=[iv2], name=mk("iv3", tt, sl))
                prev_sl_stage[("ninv", par)] = [iv3]
                ssc = p.op("vector", (lambda s_sb, par: lambda v: v.tensor_tensor(
                    s_sb.rearrange("p s (h a) -> p s h a", h=8),
                    s_sb.rearrange("p s (h a) -> p s h a", h=8),
                    invn[:, par, :].rearrange("p (h s) -> p s h", s=2).unsqueeze(3).broadcast_to([128, 2, 8, 32]),
                    ALU.mult))(s_sb, par), deps=[iv3, ev0, ev1], name=mk("ssc", tt, sl))
                m8ops = []
                for st in range(2):
                    for h in range(8):
                        m8ops.append(p.op("vector", (lambda st, h, s_sb, par: lambda v: v.max(
                            out=m8b[:, par, st, h, :], in_=s_sb[:, st, h * 32:(h + 1) * 32]))(st, h, s_sb, par),
                            deps=[ssc], name=mk("m8", tt, sl, st, h)))
                kp = p.op("vector", (lambda s_sb, par: lambda v: v.tensor_tensor(
                    kept[:, par].rearrange("p s (h a) -> p s h a", h=8),
                    s_sb.rearrange("p s (h a) -> p s h a", h=8),
                    m8b[:, par, :, :, 3:4].broadcast_to([128, 2, 8, 32]), ALU.is_ge))(s_sb, par),
                    deps=m8ops, name=mk("kp", tt, sl))
                ex = p.op("scalar", (lambda s_sb, sexp: lambda s: s.activation(
                    sexp.rearrange("p s a -> p (s a)"), s_sb.rearrange("p s a -> p (s a)"),
                    AF.Exp))(s_sb, sexp), deps=[ssc] + prev_sl_stage.get(("se", par), []),
                    name=mk("ex", tt, sl))
                pe = p.op("vector", (lambda sexp, par: lambda v: v.tensor_mul(
                    sexp.rearrange("p s a -> p (s a)"), sexp.rearrange("p s a -> p (s a)"),
                    kept[:, par].rearrange("p s a -> p (s a)")))(sexp, par),
                    deps=[ex, kp], name=mk("pe", tt, sl))
                sm = p.op("vector", (lambda sexp, par: lambda v: v.tensor_reduce(
                    ssum[:, par, :].rearrange("p (s h) -> p s h", s=2).unsqueeze(3),
                    sexp.rearrange("p s (h a) -> p s h a", h=8), AX.X, ALU.add))(sexp, par),
                    deps=[pe], name=mk("sm", tt, sl))
                rc = p.op("vector", (lambda par: lambda v: v.reciprocal(
                    ssum[:, par, :], ssum[:, par, :]))(par), deps=[sm], name=mk("rc", tt, sl))
                pn = p.op("vector", (lambda sexp, par: lambda v: v.tensor_tensor(
                    pbf[:, par].rearrange("p s (h a) -> p s h a", h=8),
                    sexp.rearrange("p s (h a) -> p s h a", h=8),
                    ssum[:, par, :].rearrange("p (s h) -> p s h", s=2).unsqueeze(3).broadcast_to([128, 2, 8, 32]),
                    ALU.mult))(sexp, par), deps=[rc], name=mk("pn", tt, sl))
                prev_sl_stage[("su", par)] = [kp, ex]
                prev_sl_stage[("se", par)] = [pn]
                fb = p.op("vector", (lambda par: lambda v: v.tensor_tensor(
                    Fb[:].rearrange("p (h a b) -> p h a b", h=8, a=32),
                    pbf[:, par, 0, :].rearrange("p (h a) -> p h a", h=8).unsqueeze(3).broadcast_to([128, 8, 32, 32]),
                    pbf[:, par, 1, :].rearrange("p (h b) -> p h b", h=8).unsqueeze(2).broadcast_to([128, 8, 32, 32]),
                    ALU.mult))(par), deps=[pn] + prev_sl_stage.get("fb", []), name=mk("fb", tt, sl))
                hr = p.op("vector", (lambda Wtm: lambda v: v.tensor_reduce(
                    Wtm.unsqueeze(2),
                    Fb[:].rearrange("p (h e) -> p h e", h=8).transpose([0, 2, 1]), AX.X, ALU.add))(Wtm),
                    deps=[fb] + prev_sl_stage.get(("wtm", par), []), name=mk("hr", tt, sl))
                prev_sl_stage["fb"] = [hr]
                wc = p.op("vector", (lambda Wtm: lambda v: v.tensor_copy(Wtb[:], Wtm))(Wtm),
                          deps=[hr] + prev_sl_stage.get("wc", []), name=mk("wc", tt, sl))
                prev_sl_stage[("wtm", par)] = [wc]
                trs = []
                for eb in range(8):
                    tq = "scalar" if eb % 2 == 0 else "sync"
                    trs.append(p.dma(tq, (lambda eb, sl: lambda s: s.dma_start_transpose(
                        Wfm[:, eb, sl * 128:(sl + 1) * 128], Wtb[:, eb * 128:(eb + 1) * 128]))(eb, sl),
                        deps=[wc] + prev_sl_stage.get("wc", []) + wfm_users, name=mk("wtr", tt, sl, eb)))
                wfm_users = []
                prev_sl_stage["wc"] = trs
                wfm_tt = trs if sl == 0 else wfm_tt + trs

            gbk = p.dma("sync", (lambda tt: lambda e: e.dma_start(
                out=ghb[:], in_=gehd[:, :, tt * 512:(tt + 1) * 512]))(tt),
                deps=[g_dma[tt]] + up_users, name=mk("gbk", tt))
            rhs_ops = []
            for eb in range(8):
                o = p.op("vector", (lambda eb: lambda v: v.tensor_mul(
                    rhsb[:, eb, :], Wfm[:, eb, :], ghb[:, eb, :]))(eb),
                    deps=wfm_tt + [gbk] + up_users, name=mk("rhs", tt, eb))
                rhs_ops.append(o)
            up_users = []
            wfm_users = rhs_ops
            op_evs = []
            for m in range(8):
                lastk = None
                for k in range(8):
                    kdeps = []
                    if k == 0:
                        kdeps = [rhs_ops[m2] for m2 in range(8)] + LD[10:11] + ln_free
                    lastk = p.op("tensor", (lambda m, k: lambda te: te.matmul(
                        pbanks_f0[m],
                        upb[:, k, m * 128:(m + 1) * 128], rhsb[:, k, :],
                        start=(k == 0), stop=(k == 7)))(m, k),
                        deps=kdeps, name=mk("up", tt, m, k))
                up_users.append(lastk)
                ev = p.op("vector", (lambda m, tt: lambda v: v.scalar_tensor_tensor(
                    uln[:, m, :], pbanks_f0[m], 1.0, x1b[:, m, tt * 512:(tt + 1) * 512],
                    ALU.mult, ALU.add))(m, tt), deps=[lastk] + uln_free, name=mk("upe", tt, m))
                op_evs.append(ev)
            uln_free = []
            lns_last = None
            lnq_last = None
            for m in range(8):
                sq_ = p.op("scalar", (lambda m: lambda s: s.activation(
                    usq[:], uln[:, m, :], AF.Square))(m),
                    deps=[op_evs[m]] + ([lnq_last] if lnq_last else []), name=mk("usq", tt, m))
                lns_last = p.op("tensor", (lambda m: lambda te: te.matmul(
                    pB[0:1, 0, 0:512], ones128[:], uln[:, m, :],
                    start=(m == 0), stop=(m == 7)))(m),
                    deps=[op_evs[m]] + (op_evs + [gelu_evs[-1]] if m == 0 else []), name=mk("lns", tt, m))
                lnq_last = p.op("tensor", (lambda m: lambda te: te.matmul(
                    pB[0:1, 1, 0:512], ones128[:], usq[:],
                    start=(m == 0), stop=(m == 7)))(m), deps=[sq_], name=mk("lnq", tt, m))
            l1 = p.op("vector", lambda v: v.tensor_scalar_mul(lrow[:], pB[0:1, 0, 0:512], 1.0 / 1024.0),
                      deps=[lns_last], name=mk("l1", tt))
            l2 = p.op("vector", lambda v: v.tensor_scalar_mul(lrow2[:], pB[0:1, 1, 0:512], 1.0 / 1024.0),
                      deps=[lnq_last], name=mk("l2", tt))
            l3 = p.op("vector", lambda v: v.tensor_mul(stg[0:1, :], lrow[:], lrow[:]),
                      deps=[l1], name=mk("l3", tt))
            l4e = p.op("vector", lambda v: v.tensor_scalar_add(lrow2[:], lrow2[:], 1e-5),
                       deps=[l2], name=mk("l4e", tt))
            l4 = p.op("vector", lambda v: v.tensor_sub(lrow2[:], lrow2[:], stg[0:1, :]),
                      deps=[l4e, l3], name=mk("l4", tt))
            l5 = p.op("scalar", lambda s: s.activation(stg[0:1, :], lrow2[:], AF.Ln),
                      deps=[l4], name=mk("l5", tt))
            l6 = p.op("scalar", lambda s: s.activation(lrow2[:], stg[0:1, :], AF.Exp, scale=-0.5),
                      deps=[l5], name=mk("l6", tt))
            l7 = p.op("vector", lambda v: v.tensor_mul(lrow[:], lrow[:], lrow2[:]),
                      deps=[l6], name=mk("l7", tt))
            bb1 = p.op("tensor", lambda te: te.matmul(
                pB[:, 0, :], onesf[:], lrow2[:], start=True, stop=True),
                deps=[l7, l1, lns_last], name=mk("bb1", tt))
            bb2 = p.op("tensor", lambda te: te.matmul(
                pB[:, 1, :], onesf[:], lrow[:], start=True, stop=True),
                deps=[l7, l2, lnq_last], name=mk("bb2", tt))
            sts = []
            prev_st = None
            for m in range(8):
                a1 = p.op("vector", (lambda m: lambda v: v.tensor_mul(
                    stg[:], uln[:, m, :], pB[:, 0, :]))(m),
                    deps=[bb1] + ([prev_st] if prev_st else []), name=mk("a1", tt, m))
                a2 = p.op("vector", (lambda m: lambda v: v.tensor_sub(
                    stg[:], stg[:], pB[:, 1, :]))(m),
                    deps=[a1, bb2], name=mk("a2", tt, m))
                x2s = p.dma("sync", (lambda tt, m: lambda e: e.dma_start(
                    out=x2_d[m * 128:(m + 1) * 128, tt * 512:(tt + 1) * 512], in_=stg[:]))(tt, m),
                    deps=[a2], name=mk("x2s", tt, m))
                prev_st = x2s
                sts.append(x2s)
                uln_free.append(x2s)
            prev_x2s = sts
            ln_free = [bb1, bb2] + sts

        p.op("sync", lambda e: e.nop(), deps=prev_x2s, name="fin")
    return p.phase_wrap(start_ix, gate, name=pre + "_done")


def build_ffn0(T=2048, total_samples=131072.0):
    nc = bass.Bass("TRN2", target_bir_lowering=False)
    p = Prog(nc)
    dr = declare_dram(nc, _FFN0_DRAM, T)
    dr["xf"] = nc.dram_tensor("xf", [1024, T], F32, kind="ExternalInput").ap()
    x2_d = nc.dram_tensor("x2", [1024, T], F32, kind="ExternalOutput").ap()
    from contextlib import ExitStack
    ses = ExitStack()
    with ses:
        sems, dma_pools = make_sems(nc, ses)
        emit_ffn0(nc, p, dr, x2_d, T, total_samples)
        with nc.Block() as block:
            p.emit(block, sems, dma_pools)
    return nc


def launch_ffn0(inputs, x1):
    w = prep_ffn0_weights(inputs)
    if "ffn0" not in _cache:
        _cache["ffn0"] = build_ffn0(2048, 131072.0)
    nc = _cache["ffn0"]
    in_maps = []
    for core in range(8):
        b, half = core // 2, core % 2
        m = {"xf": _f32(x1[b, half * 2048:(half + 1) * 2048, :].T)}
        m.update(w)
        in_maps.append(m)
    res = run_bass_kernel_spmd(nc, in_maps, core_ids=list(range(8)))
    x2 = np.stack([np.concatenate([np.asarray(res.results[b * 2 + h]["x2"]).T for h in range(2)], axis=0)
                   for b in range(4)], axis=0)
    return x2



def prep_attn1_weights(inp):
    w = {}
    for nm, key in (("wqT", "L1_attn_W_Q"), ("wkT", "L1_attn_W_K"), ("wvT", "L1_attn_W_V"),
                    ("wpT", "L1_attn_W_Kp"), ("woT", "L1_attn_W_O")):
        w[nm] = _f32(np.asarray(inp[key]))
    w["lga"] = _f32(np.asarray(inp["L1_attn_ln_gamma"]).reshape(8, 128).T)
    w["lgb"] = _f32(np.asarray(inp["L1_attn_ln_beta"]).reshape(8, 128).T)
    mask = np.full((32, 1024), -1e5, dtype=np.float32)
    for t in range(32):
        for j in range(t + 1):
            mask[t, j * 32: j * 32 + t + 1] = 0.0
    w["amask"] = mask
    w["onesf"] = np.ones((1, 128), dtype=np.float32)
    return w


_ATTN1_DRAM = (("wqT", [1024, 1024], F32), ("wkT", [1024, 1024], F32), ("wvT", [1024, 1024], F32),
               ("wpT", [1024, 1024], F32), ("woT", [1024, 1024], F32), ("lga", [128, 8], F32),
               ("lgb", [128, 8], F32), ("amask", [32, 1024], F32), ("onesf", [1, 128], F32))


def emit_attn1(nc, p, dr, xf_d, iso_d, x3_d, T=2048, gate=None, pre="a1"):
    NTT = T // 512
    start_ix = len(p.ops)
    wd = {nm: dr[nm] for nm in ("wqT", "wkT", "wvT", "wpT", "woT")}
    lga_d = dr["lga"]
    lgb_d = dr["lgb"]
    amask_d = dr["amask"]
    onesf_d = dr["onesf"]

    from contextlib import ExitStack
    es = ExitStack()
    sb = lambda name, shape, dt=F32: es.enter_context(nc.sbuf_tensor(pre + name, shape, dt))
    ps = lambda name, shape, dt=F32: es.enter_context(nc.psum_tensor(pre + name, shape, dt))

    x2b = sb("x2b", [128, 8, T], BF16)
    wsb = {nm: sb(nm + "_sb", [128, 8, 1024], BF16) for nm in ("wqT", "wkT", "wvT", "wpT", "woT")}
    lga = sb("lga_sb", [128, 8])
    lgb = sb("lgb_sb", [128, 8])
    iso = sb("iso_sb", [128, 1])
    amask = sb("amask_sb", [32, 1024])
    onesf = sb("onesf_sb", [1, 128])
    ones128 = sb("ones128", [128, 1], BF16)
    onescf = sb("onescf", [128, 1])
    ones64f = sb("ones64f", [128, 64])
    q4 = {nm: sb("q4" + nm, [128, 8, 128]) for nm in "qkpv"}
    qsq = sb("qsq", [128, 128])
    qh = {nm: sb("qh" + nm, [128, 8, 128], BF16) for nm in "qkpv"}
    nrow = sb("nrow", [128, 2, 128])
    tk = sb("tk", [128, 1024], BF16)
    tv = sb("tv", [128, 1024], BF16)
    asb = sb("asb", [32, 1024])
    aex = sb("aex", [32, 1024])
    ars = sb("ars", [32, 1])
    pbt = sb("pbt", [32, 1024], BF16)
    ptr = sb("ptr", [128, 8, 32], BF16)
    utr = sb("utr", [128, 8, 64], BF16)
    zf = sb("zf", [128, 8, 32], BF16)
    ztail = sb("ztail", [128, 8, 32])
    ubf = sb("ubf", [128, 8, 512], BF16)
    w1b = sb("w1b", [128, 8, 512], BF16)
    vbf = sb("vbf", [128, 8, 512], BF16)
    usq = sb("usq", [128, 512], BF16)
    lrow = sb("lrow", [1, 512])
    lrow2 = sb("lrow2", [1, 512])
    stg = sb("stg", [128, 512])

    pA = ps("pA", [128, 2, 512])
    pB = ps("pB", [128, 2, 512])

    def mk(*a):
        return "_".join(str(x) for x in a)

    with es:
        LD = [p.dma("gpsimd", lambda g: g.dma_start(
            out=x2b[:], in_=xf_d.rearrange("(kc p) t -> p kc t", p=128)), name="ld_x")]
        for nm in ("wqT", "wkT", "wvT", "wpT", "woT"):
            LD.append(p.dma("gpsimd", (lambda nm: lambda g: g.dma_start(
                out=wsb[nm][:], in_=wd[nm].rearrange("(k p) e -> p k e", p=128)))(nm), name="ld_" + nm))
        for nm, s_, d_ in (("lga", lga, lga_d), ("lgb", lgb, lgb_d), ("iso", iso, iso_d),
                           ("amask", amask, amask_d), ("onesf", onesf, onesf_d)):
            LD.append(p.dma("sync", (lambda s_, d_: lambda e: e.dma_start(out=s_[:], in_=d_))(s_, d_),
                            name="ld_" + nm))
        o128 = p.op("vector", lambda v: v.memset(ones128[:], 1.0), name="o128")
        ocf = p.op("vector", lambda v: v.memset(onescf[:], 1.0), name="ocf")
        o64 = p.op("vector", lambda v: v.memset(ones64f[:], 1.0), name="o64")
        pb0 = p.op("vector", lambda v: v.memset(pB[:], 0.0), name="pb0")

        mats = (("q", "wqT"), ("k", "wkT"), ("p", "wpT"), ("v", "wvT"))
        ev4 = {}
        prev_ev = None
        for mi, (qn_, wn) in enumerate(mats):
            for m in range(8):
                bank = m % 2
                lastmm = None
                for k in range(8):
                    lastmm = p.op("tensor", (lambda qn_, wn, m, k, bank: lambda te: te.matmul(
                        pA[:, bank, :128], wsb[wn][:, k, m * 128:(m + 1) * 128],
                        x2b[:, k, T - 128:T], start=(k == 0), stop=(k == 7)))(qn_, wn, m, k, bank),
                        deps=(LD + ([prev_ev] if prev_ev else [])) if k == 0 else [], name=mk("pm", qn_, m, k))
                ev = p.op("vector", (lambda qn_, m, bank: lambda v: v.tensor_copy(
                    q4[qn_][:, m, :], pA[:, bank, :128]))(qn_, m, bank),
                    deps=[lastmm], name=mk("pe", qn_, m))
                ev4[(qn_, m)] = ev
                prev_ev = ev

        rows = [(qn_, h) for qn_ in "qkp" for h in range(16)]
        qh_ops = {}
        prev_nmm = None
        prev_nsc = [None, None]
        vcp = p.op("vector", lambda v: v.tensor_copy(
            qh["v"][:].rearrange("p a b -> p (a b)"), q4["v"][:].rearrange("p a b -> p (a b)")),
            deps=[ev4[("v", m)] for m in range(8)], name="vcp")
        prev_sq = None
        for wv in range(6):
            wave_rows = rows[wv * 8: wv * 8 + 8]
            wave_mms = []
            for ri, (qn_, h) in enumerate(wave_rows):
                bnk = ri // 4
                base = 32 * (ri % 4)
                sq = p.op("scalar", (lambda qn_, h: lambda s: s.activation(
                    qsq[:], q4[qn_][:, h // 2, :], AF.Square))(qn_, h),
                    deps=[ev4[(qn_, h // 2)]] + ([prev_sq] if prev_sq else []) +
                         ([prev_nmm] if prev_nmm else []), name=mk("nsq", wv, ri))
                prev_sq = sq
                mm = p.op("tensor", (lambda h, bnk, base: lambda te: te.matmul(
                    pB[base:base + 1, bnk, 0:128], onescf[(h % 2) * 64:(h % 2) * 64 + 64, :],
                    qsq[(h % 2) * 64:(h % 2) * 64 + 64, :],
                    start=True, stop=True, tile_position=((h % 2) * 64, base)))(h, bnk, base),
                    deps=[sq, pb0], name=mk("nmm", wv, ri))
                prev_nmm = mm
                wave_mms.append(mm)
            for bnk in range(2):
                c1 = p.op("vector", (lambda wv, bnk: lambda v: v.tensor_single_scalar(
                    nrow[:, bnk, :], pB[:, bnk, 0:128], 1e-38, ALU.max))(wv, bnk),
                    deps=wave_mms, name=mk("nc1", wv, bnk))
                c2 = p.op("scalar", (lambda bnk: lambda s: s.activation(
                    nrow[:, bnk, :], nrow[:, bnk, :], AF.Ln))(bnk), deps=[c1], name=mk("nc2", wv, bnk))
                c3 = p.op("scalar", (lambda bnk: lambda s: s.activation(
                    nrow[:, bnk, :], nrow[:, bnk, :], AF.Exp, scale=0.5))(bnk),
                    deps=[c2], name=mk("nc3", wv, bnk))
                c4 = p.op("vector", (lambda bnk: lambda v: v.tensor_scalar_add(
                    nrow[:, bnk, :], nrow[:, bnk, :], 1e-7))(bnk), deps=[c3], name=mk("nc4", wv, bnk))
                c5 = p.op("vector", (lambda bnk: lambda v: v.reciprocal(
                    nrow[:, bnk, :], nrow[:, bnk, :]))(bnk), deps=[c4], name=mk("nc5", wv, bnk))
                for ri, (qn_, h) in enumerate(wave_rows):
                    if ri // 4 != bnk:
                        continue
                    base = 32 * (ri % 4)
                    bc = p.op("tensor", (lambda bnk, base: lambda te: te.matmul(
                        pB[64:128, bnk, 128:256], ones64f[base:base + 1, :],
                        nrow[base:base + 1, bnk, :], start=True, stop=True,
                        tile_position=(base, 64)))(bnk, base),
                        deps=[c5, o64] + ([prev_nsc[bnk]] if prev_nsc[bnk] else []), name=mk("nbc", wv, ri))
                    sc = p.op("vector", (lambda qn_, h, bnk: lambda v: v.tensor_mul(
                        qh[qn_][(h % 2) * 64:(h % 2) * 64 + 64, h // 2, :],
                        q4[qn_][(h % 2) * 64:(h % 2) * 64 + 64, h // 2, :],
                        pB[64:128, bnk, 128:256][0:64, :]))(qn_, h, bnk),
                        deps=[bc], name=mk("nsc", wv, ri))
                    prev_nsc[bnk] = sc
                    qh_ops[(qn_, h)] = sc

        z_evs = []
        prev_h = {}
        for h in range(16):
            b64 = (h % 2) * 64
            tl = h // 2
            tkop = p.op("vector", (lambda b64, tl: lambda v: v.tensor_tensor(
                tk[b64:b64 + 64, :].rearrange("p (j k) -> p j k", j=32),
                qh["k"][b64:b64 + 64, tl, 96:128].unsqueeze(2).broadcast_to([64, 32, 32]),
                qh["p"][b64:b64 + 64, tl, 96:128].unsqueeze(1).broadcast_to([64, 32, 32]),
                ALU.mult))(b64, tl),
                deps=[qh_ops[("k", h)], qh_ops[("p", h)]] + prev_h.get("tk", []), name=mk("tk", h))
            tvop = p.op("vector", (lambda b64, tl: lambda v: v.tensor_tensor(
                tv[b64:b64 + 64, :].rearrange("p (j k) -> p j k", j=32),
                qh["v"][b64:b64 + 64, tl, 96:128].unsqueeze(2).broadcast_to([64, 32, 32]),
                qh["v"][b64:b64 + 64, tl, 96:128].unsqueeze(1).broadcast_to([64, 32, 32]),
                ALU.mult))(b64, tl), deps=[vcp] + prev_h.get("tk", []), name=mk("tv", h))
            amms = []
            for half in range(2):
                amms.append(p.op("tensor", (lambda b64, tl, half: lambda te: te.matmul(
                    pA[0:32, half, :], qh["q"][b64:b64 + 64, tl, 96:128],
                    tk[b64:b64 + 64, half * 512:(half + 1) * 512],
                    start=True, stop=True, tile_position=(b64, 0)))(b64, tl, half),
                    deps=[qh_ops[("q", h)], tkop] + prev_h.get("a", []), name=mk("amm", h, half)))
            am = p.op("vector", lambda v: v.tensor_add(
                asb[:, 0:512], pA[0:32, 0, :], amask[:, 0:512]), deps=amms + LD, name=mk("am0", h))
            am2 = p.op("vector", lambda v: v.tensor_add(
                asb[:, 512:1024], pA[0:32, 1, :], amask[:, 512:1024]), deps=amms, name=mk("am1", h))
            exo = p.op("scalar", lambda s: s.activation(
                aex[:], asb[:], AF.Exp, scale=0.125, accum_out=ars[:]),
                deps=[am, am2], name=mk("aex", h))
            rc = p.op("vector", lambda v: v.reciprocal(ars[:], ars[:]), deps=[exo], name=mk("arc", h))
            pno = p.op("vector", lambda v: v.tensor_scalar_mul(pbt[:], aex[:], ars[:]),
                       deps=[rc], name=mk("apn", h))
            prev_h["a"] = [pno]
            trs = []
            for ch in range(8):
                trs.append(p.dma("scalar", (lambda ch: lambda s: s.dma_start_transpose(
                    ptr[:, ch, :], pbt[:, ch * 128:(ch + 1) * 128]))(ch),
                    deps=[pno] + prev_h.get("ztr", []), name=mk("ptr", h, ch)))
                trs.append(p.dma("sync", (lambda ch, b64: lambda s: s.dma_start_transpose(
                    utr[:, ch, :], tv[b64:b64 + 64, ch * 128:(ch + 1) * 128]))(ch, b64),
                    deps=[tvop] + prev_h.get("ztr", []), name=mk("utr", h, ch)))
            zmm = None
            for ch in range(8):
                zmm = p.op("tensor", (lambda ch: lambda te: te.matmul(
                    pB[0:64, 0, 256:288], utr[:, ch, :], ptr[:, ch, :],
                    start=(ch == 0), stop=(ch == 7)))(ch),
                    deps=(trs + prev_h.get("zf", [])) if ch == 0 else [], name=mk("zmm", h, ch))
            zev = p.op("vector", (lambda b64, tl: lambda v: v.tensor_copy(
                zf[b64:b64 + 64, tl, :], pB[0:64, 0, 256:288]))(b64, tl),
                deps=[zmm], name=mk("zev", h))
            prev_h["zf"] = [zev]
            prev_h["tk"] = [am, am2, tvop, zev] + trs
            prev_h["ztr"] = [zmm]
            z_evs.append(zev)

        zt_ops = []
        for m in range(8):
            bank = m % 2
            lastmm = None
            for k in range(8):
                lastmm = p.op("tensor", (lambda m, k, bank: lambda te: te.matmul(
                    pA[:, bank, 0:32], wsb["woT"][:, k, m * 128:(m + 1) * 128],
                    zf[:, k, :], start=(k == 0), stop=(k == 7)))(m, k, bank),
                    deps=(z_evs + zt_ops) if k == 0 else [], name=mk("wo", m, k))
            ev = p.op("vector", (lambda m, bank: lambda v: v.tensor_scalar_mul(
                ztail[:, m, :], pA[:, bank, 0:32], iso[:]))(m, bank),
                deps=[lastmm, LD[8]], name=mk("zt", m))
            zt_ops.append(ev)

        prev_st = None
        prev_x3 = []
        ln_free = []
        for tt in range(NTT):
            uop = p.op("vector", (lambda tt: lambda v: v.tensor_copy(
                ubf[:], x2b[:, :, tt * 512:(tt + 1) * 512]))(tt),
                deps=[LD[0]] + ln_free, name=mk("u", tt))
            uops = [uop]
            if tt == NTT - 1:
                uz = p.op("vector", lambda v: v.tensor_add(
                    ubf[:, :, 480:512], ubf[:, :, 480:512], ztail[:]),
                    deps=[uop] + zt_ops, name=mk("uz", tt))
                uops.append(uz)
            for phase in range(2):
                src_buf = ubf if phase == 0 else vbf
                lns_last = None
                lnq_last = None
                for m in range(8):
                    sq_ = p.op("scalar", (lambda m, src_buf: lambda s: s.activation(
                        usq[:], src_buf[:, m, :], AF.Square))(m, src_buf),
                        deps=uops + ([lnq_last] if lnq_last else []), name=mk("usq", tt, phase, m))
                    lns_last = p.op("tensor", (lambda m, src_buf: lambda te: te.matmul(
                        pB[0:1, 0, 0:512], ones128[:], src_buf[:, m, :],
                        start=(m == 0), stop=(m == 7)))(m, src_buf),
                        deps=uops + (ln_free if m == 0 else []), name=mk("lns", tt, phase, m))
                    lnq_last = p.op("tensor", (lambda m: lambda te: te.matmul(
                        pB[0:1, 1, 0:512], ones128[:], usq[:],
                        start=(m == 0), stop=(m == 7)))(m), deps=[sq_], name=mk("lnq", tt, phase, m))
                l1 = p.op("vector", (lambda: lambda v: v.tensor_scalar_mul(
                    lrow[:], pB[0:1, 0, 0:512], 1.0 / 1024.0))(), deps=[lns_last], name=mk("l1", tt, phase))
                l2 = p.op("vector", lambda v: v.tensor_scalar_mul(lrow2[:], pB[0:1, 1, 0:512], 1.0 / 1024.0),
                          deps=[lnq_last], name=mk("l2", tt, phase))
                l3 = p.op("vector", lambda v: v.tensor_mul(stg[0:1, :], lrow[:], lrow[:]),
                          deps=[l1], name=mk("l3", tt, phase))
                l4e = p.op("vector", lambda v: v.tensor_scalar_add(lrow2[:], lrow2[:], 1e-5),
                           deps=[l2], name=mk("l4e", tt, phase))
                l4 = p.op("vector", lambda v: v.tensor_sub(lrow2[:], lrow2[:], stg[0:1, :]),
                          deps=[l4e, l3], name=mk("l4", tt, phase))
                l5 = p.op("scalar", lambda s: s.activation(stg[0:1, :], lrow2[:], AF.Ln),
                          deps=[l4], name=mk("l5", tt, phase))
                l6 = p.op("scalar", lambda s: s.activation(lrow2[:], stg[0:1, :], AF.Exp, scale=-0.5),
                          deps=[l5], name=mk("l6", tt, phase))
                l7 = p.op("vector", lambda v: v.tensor_mul(lrow[:], lrow[:], lrow2[:]),
                          deps=[l6], name=mk("l7", tt, phase))
                bb1 = p.op("tensor", lambda te: te.matmul(
                    pB[:, 0, :], onesf[:], lrow2[:], start=True, stop=True),
                    deps=[l7, l1, lns_last], name=mk("bb1", tt, phase))
                bb2 = p.op("tensor", lambda te: te.matmul(
                    pB[:, 1, :], onesf[:], lrow[:], start=True, stop=True),
                    deps=[l7, l2, lnq_last], name=mk("bb2", tt, phase))
                sts = []
                for m in range(8):
                    a1 = p.op("vector", (lambda m, src_buf: lambda v: v.tensor_mul(
                        stg[:], src_buf[:, m, :], pB[:, 0, :]))(m, src_buf),
                        deps=[bb1] + ([prev_st] if prev_st else []), name=mk("a1", tt, phase, m))
                    a2 = p.op("vector", (lambda m: lambda v: v.tensor_sub(
                        stg[:], stg[:], pB[:, 1, :]))(m),
                        deps=[a1, bb2], name=mk("a2", tt, phase, m))
                    if phase == 0:
                        a3 = p.op("vector", (lambda m, tt: lambda v: v.scalar_tensor_tensor(
                            vbf[:, m, :], stg[:], lga[:, m:m + 1],
                            x2b[:, m, tt * 512:(tt + 1) * 512], ALU.mult, ALU.add))(m, tt),
                            deps=[a2], name=mk("a3", tt, m))
                        a4 = p.op("vector", (lambda m: lambda v: v.tensor_scalar_add(
                            vbf[:, m, :], vbf[:, m, :], lgb[:, m:m + 1]))(m),
                            deps=[a3], name=mk("a4", tt, m))
                        prev_st = a4
                        sts.append(a4)
                    else:
                        x3s = p.dma("sync", (lambda tt, m: lambda e: e.dma_start(
                            out=x3_d[m * 128:(m + 1) * 128, tt * 512:(tt + 1) * 512], in_=stg[:]))(tt, m),
                            deps=[a2], name=mk("x3s", tt, m))
                        prev_st = x3s
                        sts.append(x3s)
                if phase == 0:
                    uops = sts
                else:
                    prev_x3 = sts
                    ln_free = sts + [bb1, bb2]

        p.op("sync", lambda e: e.nop(), deps=prev_x3, name="fin")
    return p.phase_wrap(start_ix, gate, name=pre + "_done")


def build_attn1(T=2048):
    nc = bass.Bass("TRN2", target_bir_lowering=False)
    p = Prog(nc)
    dr = declare_dram(nc, _ATTN1_DRAM, T)
    xf_d = nc.dram_tensor("xf", [1024, T], F32, kind="ExternalInput").ap()
    iso_d = nc.dram_tensor("iso", [128, 1], F32, kind="ExternalInput").ap()
    x3_d = nc.dram_tensor("x3", [1024, T], F32, kind="ExternalOutput").ap()
    from contextlib import ExitStack
    ses = ExitStack()
    with ses:
        sems, dma_pools = make_sems(nc, ses)
        emit_attn1(nc, p, dr, xf_d, iso_d, x3_d, T)
        with nc.Block() as block:
            p.emit(block, sems, dma_pools)
    return nc


def launch_attn1(inputs, x2):
    w = prep_attn1_weights(inputs)
    if "attn1" not in _cache:
        _cache["attn1"] = build_attn1(2048)
    nc = _cache["attn1"]
    in_maps = []
    for core in range(8):
        b, half = core // 2, core % 2
        m = {"xf": _f32(x2[b, half * 2048:(half + 1) * 2048, :].T),
             "iso": np.full((128, 1), float(half), dtype=np.float32)}
        m.update(w)
        in_maps.append(m)
    res = run_bass_kernel_spmd(nc, in_maps, core_ids=list(range(8)))
    x3 = np.stack([np.concatenate([np.asarray(res.results[b * 2 + h]["x3"]).T for h in range(2)], axis=0)
                   for b in range(4)], axis=0)
    return x3



def prep_ffn1_weights(inp):
    w = {}
    wc = np.asarray(inp["L1_ffn_Wc"]).reshape(1024, 128)
    w["wc"] = _bf(wc.reshape(8, 128, 128).transpose(1, 0, 2))
    w["bcv"] = _f32(np.asarray(inp["L1_ffn_bc"]).reshape(128, 1))
    w1 = np.asarray(inp["L1_ffn_W1"]).reshape(128, 1024, 32)
    w["w1q"] = _bf(w1.reshape(32, 4, 8, 128, 32).transpose(0, 3, 2, 1, 4).reshape(32, 128, 8, 128))
    w2 = np.asarray(inp["L1_ffn_W2"]).reshape(128, 1024, 32)
    w["w2q"] = _bf(w2.reshape(32, 4, 1024, 32).transpose(0, 1, 3, 2).reshape(32, 128, 1024))
    b1 = np.asarray(inp["L1_ffn_b1"]).reshape(128, 32)
    w["b1t"] = _f32(b1.reshape(32, 4, 32).transpose(1, 2, 0).reshape(128, 32))
    w["b2f"] = _bf(np.asarray(inp["L1_ffn_b2"]).reshape(128, 1024))
    band = np.zeros((128, 4), dtype=np.float32)
    for t in range(128):
        band[t, t // 32] = 1.0
    w["band"] = _bf(band)
    return w


_FFN1_DRAM = (("wc", [128, 8, 128], BF16), ("bcv", [128, 1], F32), ("w1q", [32, 128, 8, 128], BF16),
              ("w2q", [32, 128, 1024], BF16), ("b1t", [128, 32], F32), ("b2f", [128, 1024], BF16),
              ("band", [128, 4], BF16))


def emit_ffn1(nc, p, dr, xf_d, x4_d, T=2048, gate=None, pre="f1"):
    NG = T // 32
    NCK = T // 128
    NTL = T // 512
    start_ix = len(p.ops)
    wc_d = dr["wc"]
    bc_d = dr["bcv"]
    w1q_d = dr["w1q"]
    w2q_d = dr["w2q"]
    b1t_d = dr["b1t"]
    b2f_d = dr["b2f"]
    band_d = dr["band"]
    rdr = nc.dram_tensor(pre + "_rbounce", [NG, 128, 1024], BF16, kind="Internal").ap()

    from contextlib import ExitStack
    es = ExitStack()
    es1 = ExitStack()
    sb = lambda name, shape, dt=F32: es.enter_context(nc.sbuf_tensor(pre + name, shape, dt))
    sb1 = lambda name, shape, dt=F32: es1.enter_context(nc.sbuf_tensor(pre + name, shape, dt))
    ps = lambda name, shape, dt=F32: es.enter_context(nc.psum_tensor(pre + name, shape, dt))

    wc_sb = sb("wc_sb", [128, 8, 128], BF16)
    bc_sb = sb("bc_sb", [128, 1])
    b1t_sb = sb("b1t_sb", [128, 32])
    b2f_sb = sb("b2f_sb", [128, 1024], BF16)
    band_sb = sb("band_sb", [128, 4], BF16)
    perm = sb("perm_sb", [128, T], BF16)
    gmax = sb("gmax", [128, 64])
    inner = sb("inner_sb", [128, 32, 64], BF16)
    rsb = sb("rsb", [128, 2, 1024], BF16)
    ones128 = sb("ones128", [128, 1], BF16)
    onesf = sb("onesf_sb", [1, 128])
    x3b = sb1("x3b", [128, 8, T], BF16)
    ptm = sb1("ptm", [128, 2, 128], BF16)
    bdt = sb1("bdt", [128, 2, 512], BF16)
    xtm = sb1("xtm", [128, 2, 8, 128], BF16)
    ph = sb1("ph", [128, 8, NG * 128], BF16)
    w1s = sb1("w1s", [128, 4, 8, 128], BF16)
    w2s = sb1("w2s", [128, 4, 1024], BF16)

    pS = ps("pS", [128, 8, 512])

    def mk(*a):
        return "_".join(str(x) for x in a)

    with es:
        LDC = [p.dma("gpsimd", lambda g: g.dma_start(
                   out=x3b[:], in_=xf_d.rearrange("(kc p) t -> p kc t", p=128)), name="ldxb"),
               p.dma("gpsimd", lambda g: g.dma_start(out=wc_sb[:], in_=wc_d), name="ldwc"),
               p.dma("gpsimd", lambda g: g.dma_start(out=bc_sb[:], in_=bc_d), name="ldbc"),
               p.dma("gpsimd", lambda g: g.dma_start(out=b1t_sb[:], in_=b1t_d), name="ldb1"),
               p.dma("gpsimd", lambda g: g.dma_start(out=b2f_sb[:], in_=b2f_d), name="ldb2"),
               p.dma("gpsimd", lambda g: g.dma_start(out=band_sb[:], in_=band_d), name="ldband")]
        o128 = p.op("vector", lambda v: v.memset(ones128[:], 1.0), name="o128")
        onf = p.op("vector", lambda v: v.memset(onesf[:], 1.0), name="onf")

        pge = {}
        for tl in range(NTL):
            mm = None
            for kc in range(8):
                mm = p.op("tensor", (lambda tl, kc: lambda te: te.matmul(
                    pS[:, tl % 2, :], wc_sb[:, kc, :], x3b[:, kc, tl * 512:(tl + 1) * 512],
                    start=(kc == 0), stop=(kc == 7)))(tl, kc),
                    deps=(LDC + ([pge[tl - 2]] if tl >= 2 else [])) if kc == 0 else [],
                    name=mk("lg", tl, kc))
            bca = p.op("vector", (lambda tl: lambda v: v.tensor_scalar_add(
                pS[:, tl % 2, :], pS[:, tl % 2, :], bc_sb[:]))(tl), deps=[mm], name=mk("bca", tl))
            gmx = p.op("vector", (lambda tl: lambda v: v.tensor_reduce(
                gmax[:, tl * 16:(tl + 1) * 16].unsqueeze(2),
                pS[:, tl % 2, :].rearrange("p (g t) -> p g t", t=32), AX.X, ALU.max))(tl),
                deps=[bca], name=mk("gmx", tl))
            pge[tl] = p.op("vector", (lambda tl: lambda v: v.tensor_tensor(
                perm[:, tl * 512:(tl + 1) * 512].rearrange("p (g t) -> p g t", t=32),
                pS[:, tl % 2, :].rearrange("p (g t) -> p g t", t=32),
                gmax[:, tl * 16:(tl + 1) * 16].unsqueeze(2).broadcast_to([128, 16, 32]),
                ALU.is_ge))(tl), deps=[gmx], name=mk("pge", tl))
        pev = {}
        prev_ck = {}
        for ck in range(NCK):
            cb = ck % 2
            tq = "scalar" if ck % 2 == 0 else "sync"
            ptr = p.dma(tq, (lambda ck, cb: lambda s: s.dma_start_transpose(
                ptm[:, cb, :], perm[:, ck * 128:(ck + 1) * 128]))(ck, cb),
                deps=[pge[ck // 4]] + prev_ck.get(("bd", cb), []), name=mk("ptr", ck))
            bdo = p.op("vector", (lambda cb: lambda v: v.tensor_tensor(
                bdt[:, cb, :].rearrange("p (g e) -> p g e", e=128),
                ptm[:, cb, :].unsqueeze(1).broadcast_to([128, 4, 128]),
                band_sb[:].unsqueeze(2).broadcast_to([128, 4, 128]),
                ALU.mult))(cb), deps=[ptr, LDC[5]] + prev_ck.get(("pm", cb), []),
                name=mk("bd", ck))
            prev_ck[("bd", cb)] = [bdo]
            pmms = []
            for kc in range(8):
                pb = 2 + ((ck * 8 + kc) % 4)
                xtq = "scalar" if kc % 2 == 0 else "sync"
                xtr = p.dma(xtq, (lambda ck, cb, kc: lambda s: s.dma_start_transpose(
                    xtm[:, cb, kc, :], x3b[:, kc, ck * 128:(ck + 1) * 128]))(ck, cb, kc),
                    deps=[LDC[0]] + prev_ck.get(("pm", cb, kc), []), name=mk("xtr", ck, kc))
                ps_war = pev.get((ck, kc - 4), []) if kc >= 4 else pev.get((ck - 1, kc + 4), [])
                pmm = p.op("tensor", (lambda cb, kc, pb: lambda te: te.matmul(
                    pS[:, pb, :], xtm[:, cb, kc, :], bdt[:, cb, :],
                    start=True, stop=True))(cb, kc, pb),
                    deps=[bdo, xtr] + ps_war, name=mk("pmm", ck, kc))
                pmms.append(pmm)
                prev_ck[("pm", cb, kc)] = [pmm]
                if kc % 2 == 0:
                    ev = p.op("vector", (lambda ck, kc, pb: lambda v: v.tensor_copy(
                        ph[:, kc, ck * 512:(ck + 1) * 512], pS[:, pb, :]))(ck, kc, pb),
                        deps=[pmm], name=mk("pev", ck, kc))
                else:
                    ev = p.op("scalar", (lambda ck, kc, pb: lambda s: s.activation(
                        ph[:, kc, ck * 512:(ck + 1) * 512], pS[:, pb, :], AF.Identity))(ck, kc, pb),
                        deps=[pmm], name=mk("pev", ck, kc))
                pev[(ck, kc)] = [ev]
            prev_ck[("pm", cb)] = pmms
        p_done = p.marker("vector", [e for v in pev.values() for e in v], name="p_done")
        rdmas = []
        prev_q = {}
        for t in range(32):
            tb = t % 4
            lw1 = p.dma("sync", (lambda t, tb: lambda e: e.dma_start(
                out=w1s[:, tb, :, :], in_=w1q_d[t]))(t, tb),
                deps=prev_q.get(("w1mm", tb), []), name=mk("lw1", t))
            lw2 = p.dma("gpsimd", (lambda t, tb: lambda e: e.dma_start(
                out=w2s[:, tb, :], in_=w2q_d[t]))(t, tb),
                deps=prev_q.get(("w2mm", tb), []), name=mk("lw2", t))
            w1mms = []
            for j in range(4):
                mm = None
                for kc in range(8):
                    mm = p.op("tensor", (lambda t, tb, j, kc: lambda te: te.matmul(
                        pS[32 * j:32 * j + 32, 2 + (t % 2), 0:64],
                        w1s[:, tb, kc, 32 * j:32 * j + 32],
                        ph[:, kc, :].rearrange("p (c e) -> p c e", e=128)[:, :, 4 * t + j],
                        start=(kc == 0), stop=(kc == 7), tile_position=(0, 32 * j)))(t, tb, j, kc),
                        deps=([lw1, p_done] + prev_q.get(("rel", t % 2), [])) if kc == 0 else [],
                        name=mk("w1", t, j, kc))
                w1mms.append(mm)
            prev_q[("w1mm", tb)] = w1mms
            rel = p.op("scalar", (lambda t: lambda s: s.activation(
                inner[:, t, :], pS[:, 2 + (t % 2), 0:64], AF.Relu, bias=b1t_sb[:, t:t + 1]))(t),
                deps=w1mms + [LDC[3]], name=mk("rel", t))
            prev_q[("rel", t % 2)] = [rel]
            w2mms = []
            for j in range(4):
                qp = 64 * (j % 2)
                bk = 4 + 2 * (t % 2) if j < 2 else 6 - 2 * (t % 2)
                for n in range(2):
                    w2mms.append(p.op("tensor", (lambda t, tb, j, n, qp, bk: lambda te: te.matmul(
                        pS[qp:qp + 64, bk + n, :],
                        inner[32 * j:32 * j + 32, t, :],
                        w2s[32 * j:32 * j + 32, tb, n * 512:(n + 1) * 512],
                        start=True, stop=True, tile_position=(32 * j, qp)))(t, tb, j, n, qp, bk),
                        deps=[rel, lw2] + prev_q.get(("eva", t % 2, j // 2), []),
                        name=mk("w2", t, j, n)))
            prev_q[("w2mm", tb)] = w2mms
            for pp in range(2):
                bk = 4 + 2 * (t % 2) if pp == 0 else 6 - 2 * (t % 2)
                eng = "scalar" if (t + pp) % 2 == 0 else "vector"
                if eng == "scalar":
                    eva = p.op(eng, (lambda pp, bk: lambda s: s.activation(
                        rsb[:, pp, :], pS[:, bk:bk + 2, :].rearrange("p b f -> p (b f)"),
                        AF.Identity))(pp, bk), deps=w2mms[4 * pp:4 * pp + 4] +
                        prev_q.get(("rdma", pp), []), name=mk("eva", t, pp))
                else:
                    eva = p.op(eng, (lambda pp, bk: lambda v: v.tensor_copy(
                        rsb[:, pp, :], pS[:, bk:bk + 2, :].rearrange("p b f -> p (b f)")))(pp, bk),
                        deps=w2mms[4 * pp:4 * pp + 4] + prev_q.get(("rdma", pp), []),
                        name=mk("eva", t, pp))
                prev_q[("eva", t % 2, pp)] = [eva]
                rd = p.dma("scalar", (lambda t, pp: lambda s: s.dma_start(
                    out=rdr.rearrange("g e d -> e g d")[4 * t + 2 * pp:4 * t + 2 * pp + 2, :, :],
                    in_=rsb[:, pp, :]))(t, pp), deps=[eva], name=mk("rdma", t, pp))
                prev_q[("rdma", pp)] = [rd]
                rdmas.append(rd)
        r_done = p.marker("sync", rdmas, name="r_done")
        es1.close()
        rg = sb("rg", [128, 8, 1024], BF16)
        x3t = sb("x3t", [128, 2, 8, 512])
        ubf = sb("ubf1", [128, 8, 512], BF16)
        usq = sb("usq1", [128, 512], BF16)
        lrow = sb("lrow_f", [1, 512])
        lrow2 = sb("lrow2_f", [1, 512])
        stg = sb("stg_f", [128, 512])
        tile_done = None
        rgls = {}
        sc_by_g = {}
        x3lds = {}
        for tl in range(NTL):
            x3lds[tl] = p.dma("gpsimd", (lambda tl: lambda g: g.dma_start(
                out=x3t[:, tl % 2, :, :],
                in_=xf_d.rearrange("(kc p) t -> p kc t", p=128)[:, :, tl * 512:(tl + 1) * 512]))(tl),
                deps=[r_done] + ([tile_done] if tile_done else []), name=mk("x3ld", tl))
            td_dep = [tile_done] if tile_done else []
            b2m = {}
            for ds in range(8):
                b2m[ds] = p.op("tensor", (lambda tl, ds: lambda te: te.matmul(
                    pS[:, ds, :], b2f_sb[:, ds * 128:(ds + 1) * 128],
                    perm[:, tl * 512:(tl + 1) * 512], start=True, stop=False))(tl, ds),
                    deps=[r_done, LDC[4]] + td_dep, name=mk("b2m", tl, ds))
            last_sc = {}
            for g16 in range(16):
                g = tl * 16 + g16
                gb = g % 8
                gq = "sync" if g % 2 == 0 else "gpsimd"
                rgls[g] = p.dma(gq, (lambda g, gb: lambda e: e.dma_start(
                    out=rg[:, gb, :], in_=rdr[g]))(g, gb),
                    deps=[r_done] + sc_by_g.get(g - 8, []) + td_dep, name=mk("rgl", g))
                scs = []
                for ds in range(8):
                    scs.append(p.op("tensor", (lambda tl, g16, g, gb, ds: lambda te: te.matmul(
                        pS[:, ds, g16 * 32:(g16 + 1) * 32], rg[:, gb, ds * 128:(ds + 1) * 128],
                        perm[:, tl * 512 + g16 * 32: tl * 512 + (g16 + 1) * 32],
                        start=False, stop=(g16 == 15)))(tl, g16, g, gb, ds),
                        deps=[rgls[g], b2m[ds]] + last_sc.get(ds, []), name=mk("sc", g, ds)))
                    last_sc[ds] = [scs[-1]]
                sc_by_g[g] = scs
            ubevs = {}
            for ds in range(8):
                ubevs[ds] = p.op("vector", (lambda tl, ds: lambda v: v.tensor_add(
                    ubf[:, ds, :], pS[:, ds, :], x3t[:, tl % 2, ds, :]))(tl, ds),
                    deps=last_sc[ds] + [x3lds[tl]], name=mk("ubev", tl, ds))
            lns_last = None
            lnq_last = None
            for m in range(8):
                sq_ = p.op("scalar", (lambda m: lambda s: s.activation(
                    usq[:], ubf[:, m, :], AF.Square))(m),
                    deps=[ubevs[m]] + ([lnq_last] if lnq_last else []), name=mk("usq", tl, m))
                lns_last = p.op("tensor", (lambda m: lambda te: te.matmul(
                    pS[0:1, 0, 0:512], ones128[:], ubf[:, m, :],
                    start=(m == 0), stop=(m == 7)))(m),
                    deps=[ubevs[m], ubevs[0], o128], name=mk("lns", tl, m))
                lnq_last = p.op("tensor", (lambda m: lambda te: te.matmul(
                    pS[0:1, 1, 0:512], ones128[:], usq[:],
                    start=(m == 0), stop=(m == 7)))(m), deps=[sq_, ubevs[1]], name=mk("lnq", tl, m))
            l1 = p.op("vector", lambda v: v.tensor_scalar_mul(
                lrow[:], pS[0:1, 0, 0:512], 1.0 / 1024.0), deps=[lns_last], name=mk("l1", tl))
            l2 = p.op("vector", lambda v: v.tensor_scalar_mul(lrow2[:], pS[0:1, 1, 0:512], 1.0 / 1024.0),
                      deps=[lnq_last], name=mk("l2", tl))
            l3 = p.op("vector", lambda v: v.tensor_mul(stg[0:1, :], lrow[:], lrow[:]),
                      deps=[l1], name=mk("l3", tl))
            l4e = p.op("vector", lambda v: v.tensor_scalar_add(lrow2[:], lrow2[:], 1e-5),
                       deps=[l2], name=mk("l4e", tl))
            l4 = p.op("vector", lambda v: v.tensor_sub(lrow2[:], lrow2[:], stg[0:1, :]),
                      deps=[l4e, l3], name=mk("l4", tl))
            l5 = p.op("scalar", lambda s: s.activation(stg[0:1, :], lrow2[:], AF.Ln),
                      deps=[l4], name=mk("l5", tl))
            l6 = p.op("scalar", lambda s: s.activation(lrow2[:], stg[0:1, :], AF.Exp, scale=-0.5),
                      deps=[l5], name=mk("l6", tl))
            l7 = p.op("vector", lambda v: v.tensor_mul(lrow[:], lrow[:], lrow2[:]),
                      deps=[l6], name=mk("l7", tl))
            bb1 = p.op("tensor", lambda te: te.matmul(
                pS[:, 0, :], onesf[:], lrow2[:], start=True, stop=True),
                deps=[l6, l7, l1, onf], name=mk("bb1", tl))
            bb2 = p.op("tensor", lambda te: te.matmul(
                pS[:, 1, :], onesf[:], lrow[:], start=True, stop=True),
                deps=[l7, l2], name=mk("bb2", tl))
            prev_st = None
            outs = []
            for m in range(8):
                a1 = p.op("vector", (lambda m: lambda v: v.tensor_mul(
                    stg[:], ubf[:, m, :], pS[:, 0, :]))(m),
                    deps=[bb1] + ([prev_st] if prev_st else []), name=mk("a1", tl, m))
                a2 = p.op("vector", (lambda m: lambda v: v.tensor_sub(
                    stg[:], stg[:], pS[:, 1, :]))(m), deps=[a1, bb2], name=mk("a2", tl, m))
                x4s = p.dma("sync", (lambda m, tl: lambda e: e.dma_start(
                    out=x4_d[m * 128:(m + 1) * 128, tl * 512:(tl + 1) * 512],
                    in_=stg[:]))(m, tl), deps=[a2], name=mk("x4s", tl, m))
                prev_st = x4s
                outs.append(x4s)
            tile_done = p.marker("vector", outs + [bb1, bb2], name=mk("tdone", tl))
        p.op("sync", lambda e: e.nop(), deps=[tile_done], name="fin")
    return p.phase_wrap(start_ix, gate, name=pre + "_done")


def build_ffn1(T=2048):
    nc = bass.Bass("TRN2", target_bir_lowering=False)
    p = Prog(nc)
    dr = declare_dram(nc, _FFN1_DRAM, T)
    xf_d = nc.dram_tensor("xf", [1024, T], F32, kind="ExternalInput").ap()
    x4_d = nc.dram_tensor("x4", [1024, T], F32, kind="ExternalOutput").ap()
    from contextlib import ExitStack
    ses = ExitStack()
    with ses:
        sems, dma_pools = make_sems(nc, ses)
        emit_ffn1(nc, p, dr, xf_d, x4_d, T)
        with nc.Block() as block:
            p.emit(block, sems, dma_pools)
    return nc


def launch_ffn1(inputs, x3):
    w = prep_ffn1_weights(inputs)
    if "ffn1" not in _cache:
        _cache["ffn1"] = build_ffn1(2048)
    nc = _cache["ffn1"]
    in_maps = []
    for core in range(8):
        b, half = core // 2, core % 2
        m = {"xf": _f32(x3[b, half * 2048:(half + 1) * 2048, :].T)}
        m.update(w)
        in_maps.append(m)
    res = run_bass_kernel_spmd(nc, in_maps, core_ids=list(range(8)))
    x4 = np.stack([np.concatenate([np.asarray(res.results[b * 2 + h]["x4"]).T for h in range(2)], axis=0)
                   for b in range(4)], axis=0)
    return x4



def build_fused():
    nc = bass.Bass("TRN2", target_bir_lowering=False)
    p = Prog(nc)
    drm = declare_dram(nc, tuple(s for s in _MAMBA_DRAM if s[0] != "x"), 2048, pre="m_")
    drm["x"] = nc.dram_tensor("m_x", [1024, 2051], F32, kind="ExternalInput").ap()
    dr0 = declare_dram(nc, _FFN0_DRAM, 2048, pre="f0_")
    dra = declare_dram(nc, _ATTN1_DRAM, 2048, pre="a1_")
    dr1 = declare_dram(nc, _FFN1_DRAM, 2048, pre="f1_")
    iso_d = nc.dram_tensor("iso", [128, 1], F32, kind="ExternalInput").ap()
    x1_d = nc.dram_tensor("x1i", [1024, 2048], F32, kind="Internal").ap()
    x2_d = nc.dram_tensor("x2i", [1024, 2048], F32, kind="Internal").ap()
    x3_d = nc.dram_tensor("x3i", [1024, 2048], F32, kind="Internal").ap()
    x4_d = nc.dram_tensor("x4", [1024, 2048], F32, kind="ExternalOutput").ap()
    dr0["xf"] = x1_d
    from contextlib import ExitStack
    ses = ExitStack()
    with ses:
        sems, dma_pools = make_sems(nc, ses)
        g1 = emit_mamba2(nc, p, drm, x1_d, iso_d, 2048, None, pre="m")
        g2 = emit_ffn0(nc, p, dr0, x2_d, 2048, 131072.0, g1, pre="f0")
        g3 = emit_attn1(nc, p, dra, x2_d, iso_d, x3_d, 2048, g2, pre="a1")
        g4 = emit_ffn1(nc, p, dr1, x3_d, x4_d, 2048, g3, pre="f1")
        p.op("sync", lambda e: e.nop(), deps=[g4], name="endfin")
        with nc.Block() as block:
            p.emit(block, sems, dma_pools)
    return nc


def launch_fused(inputs):
    if "fused" not in _cache:
        _cache["fused"] = build_fused()
    nc = _cache["fused"]
    x = np.asarray(inputs["L0_attn_x"])
    wsets = (("m_", prep_mamba_weights(inputs)), ("f0_", prep_ffn0_weights(inputs)),
             ("a1_", prep_attn1_weights(inputs)), ("f1_", prep_ffn1_weights(inputs)))
    in_maps = []
    for core in range(8):
        b, half = core // 2, core % 2
        m = {}
        for pre, w in wsets:
            for k, v in w.items():
                m[pre + k] = v
        xe = np.zeros((1024, 2051), dtype=np.float32)
        xe[:, 3:] = x[b, half * 2048:(half + 1) * 2048, :].T
        if half == 1:
            xe[:, 0:3] = x[b, 2045:2048, :].T
        m["m_x"] = xe
        m["iso"] = np.full((128, 1), float(half), dtype=np.float32)
        in_maps.append(m)
    res = run_bass_kernel_spmd(nc, in_maps, core_ids=list(range(8)))
    x4 = np.stack([np.concatenate([np.asarray(res.results[b * 2 + h]["x4"]).T for h in range(2)], axis=0)
                   for b in range(4)], axis=0)
    return x4



def emit_mamba2(nc, p, dr, x1_d, iso_d, T=2048, gate=None, pre="m"):
    TT_ = T // 512
    TC = T // 128
    start_ix = len(p.ops)
    x_d = dr["x"]
    winT_d = dr["w_inT"]
    cd_d = dr["cd"]
    convb_d = dr["conv_b"]
    dtb_d = dr["dtb"]
    alog_d = dr["alog"]
    drep_d = dr["drep"]
    nw_d = dr["nw"]
    woutT_d = dr["w_outT"]
    ones2_d = dr["ones2"]
    onesf_d = dr["onesf"]
    esel_d = dr["esel"]
    tri_d = dr["tri"]
    idtb_d = dr["idtb"]
    zdr = nc.dram_tensor(pre + "_zdr", [128, 16, T], BF16, kind="Internal").ap()
    csdr = nc.dram_tensor(pre + "_csdr", [2, 32, T], BF16, kind="Internal").ap()
    yfdr = nc.dram_tensor(pre + "_yfdr", [128, TC, 2048], BF16, kind="Internal").ap()
    ldr = nc.dram_tensor(pre + "_ldr", [128, TC, 2048], BF16, kind="Internal").ap()
    ccin = nc.dram_tensor(pre + "_ccin", [128, 2048], F32, kind="Internal").ap()
    ccout = nc.dram_tensor(pre + "_ccout", [256, 2048], F32, kind="Internal").ap()

    from contextlib import ExitStack
    es = ExitStack()
    es1 = ExitStack()
    sb = lambda name, shape, dt=F32: es.enter_context(nc.sbuf_tensor(pre + name, shape, dt))
    sb1 = lambda name, shape, dt=F32: es1.enter_context(nc.sbuf_tensor(pre + name, shape, dt))
    ps = lambda name, shape, dt=F32: es.enter_context(nc.psum_tensor(pre + name, shape, dt))

    drep = sb("drep_sb", [128, 16])
    nw = sb("nw_sb", [128, 16])
    ones2 = sb("ones2_sb", [2, 128], BF16)
    onesf = sb("onesf_sb", [1, 128])
    ones128 = sb("ones128", [128, 1], BF16)
    iso = sb("iso_sb", [128, 1])
    xb = sb("xb", [128, 8, 512], BF16)
    xBC = sb("xBC", [128, 18, 528], BF16)
    zst = sb("zst", [128, 2, 512], BF16)
    csfl = sb("csfl", [2, 4, 4096], BF16)
    ecst = sb("ecst", [128, 512], BF16)
    dtot_all = sb("dtot", [128, TC, 32], BF16)
    pfxall = sb("pfx", [128, TC, 32])
    ltd = sb("ltd", [128, 4096], BF16)
    wsl = sb("wsl", [128, 32], BF16)
    cpark = sb("cpark", [128, T], BF16)
    S = sb("S", [128, 2048])
    Sbf = sb("Sbf", [128, 2048], BF16)
    yf = sb("yf", [128, 2048], BF16)

    w_in = sb1("w_in", [128, 8 * 4384], BF16)
    cd = sb1("cd_sb", [128, 72 * 32], BF16)
    convb = sb1("convb_sb", [128, 18])
    dtb = sb1("dtb_sb", [32, 1])
    alog = sb1("alog_sb", [32, 1])
    negA = sb1("negA", [32, 1])
    esel = sb1("esel_sb", [65, 4096], BF16)
    tri = sb1("tri_sb", [128, 512], BF16)
    idtb = sb1("idtb_sb", [128, 128], BF16)
    zeros32 = sb1("zeros32", [32, 128])
    xh = sb1("xh", [128, 8, 3], BF16)
    halo = sb1("halo", [128, 18, 3], BF16)
    dtraw = sb1("dtraw", [32, 512])
    dts1 = sb1("dts1", [32, 512])
    dts2 = sb1("dts2", [32, 512])
    csf = sb1("csf", [32, 512])
    u2 = sb1("u2", [65, 512], BF16)
    cshl = sb1("cshl", [32, 2, 512], BF16)
    xtm = sb1("xtm", [128, 2048], BF16)
    x2t = sb1("x2t", [128, 2048], BF16)
    btm = sb1("btm", [128, 128], BF16)
    gt = sb1("gt", [128, 128], BF16)

    pip = ps("pip", [128, 2, 512])
    pcv = ps("pcv", [128, 2, 512])
    pdd = ps("pdd", [128, 2, 512])
    psm = ps("psm", [128, 512])
    pec = ps("pec", [128, 512])
    pbanks = [pip[:, 0, :], pip[:, 1, :], pcv[:, 0, :], pcv[:, 1, :],
              pdd[:, 0, :], pdd[:, 1, :], psm[:], pec[:]]

    with es:
        ld = []
        for kc in range(8):
            ld.append(p.dma("gpsimd", (lambda kc: lambda g: g.dma_start(
                out=w_in[:, kc * 4384:(kc + 1) * 4384],
                in_=winT_d[kc * 128:(kc + 1) * 128, :]))(kc), name=f"ld_win{kc}"))
        ld.append(p.dma("sync", lambda e: e.dma_start(
            out=cd[:].rearrange("p (t f) -> p t f", f=32), in_=cd_d), name="ld_cd"))
        for nm, s_, d_ in (("convb", convb, convb_d), ("dtb", dtb, dtb_d), ("alog", alog, alog_d),
                           ("drep", drep, drep_d), ("nw", nw, nw_d), ("ones2", ones2, ones2_d),
                           ("onesf", onesf, onesf_d), ("esel", esel, esel_d), ("tri", tri, tri_d),
                           ("idtb", idtb, idtb_d), ("iso", iso, iso_d)):
            ld.append(p.dma("sync", (lambda s_, d_: lambda e: e.dma_start(out=s_[:], in_=d_))(s_, d_),
                            name="ld_" + nm))
        LD = list(ld)
        init0 = p.op("vector", lambda v: v.memset(zeros32[:], 0.0), name="z32")
        p.op("vector", lambda v: v.memset(ones128[:], 1.0), name="o128")
        initU = p.op("vector", lambda v: v.memset(u2[64:65, :], 1.0), name="u2o")
        initS = p.op("vector", lambda v: v.memset(S[:], 0.0), name="S0")
        initSb = p.op("vector", lambda v: v.memset(Sbf[:], 0.0), name="Sbf0")
        expA = p.op("scalar", lambda s: s.activation(negA[:], alog[:], AF.Exp), deps=LD, name="expA")
        negAo = p.op("vector", lambda v: v.tensor_scalar_mul(negA[:], negA[:], -1.0), deps=[expA], name="negA")

        def mk(*a):
            return "_".join(str(x) for x in a)

        xhld = p.dma("gpsimd", lambda g: g.dma_start(
            out=xh[:], in_=x_d[:, 0:3].rearrange("(kc p) t -> p kc t", p=128)), name="xhld")
        hevs = []
        hev_by_bank = {}
        for ft in range(18):
            m = 16 + ft
            bank = ft % 2
            hdeps = [xhld] + (LD[:8] if ft < 2 else []) + hev_by_bank.get(bank, [])
            lastmm = None
            for k in range(8):
                lastmm = p.op("tensor", (lambda m, k, bank: lambda te: te.matmul(
                    pip[:, bank, 0:3],
                    w_in[:, k * 4384 + m * 128: k * 4384 + m * 128 + 128],
                    xh[:, k, :], start=(k == 0), stop=(k == 7)))(m, k, bank),
                    deps=hdeps if k == 0 else [], name=mk("hip", ft, k))
            hev = p.op("vector", (lambda ft, bank: lambda v: v.tensor_copy(
                xBC[:, ft, 13:16], pip[:, bank, 0:3]))(ft, bank), deps=[lastmm], name=mk("hev", ft))
            hev_by_bank[bank] = [hev]
            hevs.append(hev)
        halo_restore = p.marker("vector", hevs, name="halo_init")

        ip_evac = {(-1, 32): hev_by_bank[0][0], (-1, 33): hev_by_bank[1][0]}
        cv_evac = {}
        xb_users = {}
        zdma = {}
        Sbf_ready = initSb
        S_ready = initS
        chunk_mark = halo_restore
        tile_mark = None
        zsl_hist = {}
        csfl_users = {}
        silu_ev_by_t = {}
        dcp_by_c = {}
        yfps = []
        lparks = []
        bo1s = []
        cpks = {}

        for t in range(TT_):
            xldt = p.dma("gpsimd", (lambda t: lambda g: g.dma_start(
                out=xb[:],
                in_=x_d[:, 3 + t * 512:3 + (t + 1) * 512].rearrange("(kc p) t -> p kc t", p=128)))(t),
                deps=xb_users.get(t - 1, []) + ([xhld] if t == 0 else []), name=mk("xld", t))
            xb_users[t] = []

            for m in range(35):
                bank = m % 2
                rows = 128 if m < 34 else 32
                mdeps = [xldt] + (LD[:8] if t == 0 else [])
                key = (t, m - 2) if m >= 2 else (t - 1, 32 + m)
                if key in ip_evac:
                    mdeps.append(ip_evac[key])
                if m < 2 and tile_mark is not None:
                    mdeps.append(tile_mark)
                lastmm = None
                for k in range(8):
                    lastmm = p.op("tensor", (lambda m, k, bank, rows: lambda te: te.matmul(
                        pip[0:rows, bank, :],
                        w_in[:, k * 4384 + m * 128: k * 4384 + m * 128 + rows],
                        xb[:, k, :], start=(k == 0), stop=(k == 7)))(m, k, bank, rows),
                        deps=mdeps if k == 0 else [], name=mk("ip", t, m, k))
                xb_users[t].append(lastmm)
                if m < 16:
                    g4 = m // 2
                    zkey = (t, g4 - 1) if g4 > 0 else (t - 1, 7)
                    zwar = [zdma[zkey]] if zkey in zdma else []
                    ev = p.op("scalar", (lambda m, bank: lambda s: s.copy(
                        zst[:, m % 2, :], pip[:, bank, :]))(m, bank),
                        deps=[lastmm] + zwar, name=mk("ipe", t, m))
                    ip_evac[(t, m)] = ev
                    if m % 2 == 1:
                        zdma[(t, g4)] = p.dma("sync", (lambda t, g4: lambda e: e.dma_start(
                            out=zdr[:, g4 * 2:(g4 + 1) * 2, t * 512:(t + 1) * 512], in_=zst[:]))(t, g4),
                            deps=[ip_evac[(t, mm)] for mm in range(g4 * 2, g4 * 2 + 2)],
                            name=mk("zd", t, g4))
                elif m < 34:
                    ft = m - 16
                    ev = p.op("vector", (lambda ft, bank: lambda v: v.tensor_copy(
                        xBC[:, ft, 16:528], pip[:, bank, :]))(ft, bank),
                        deps=[lastmm] + ([halo_restore] if halo_restore is not None else []),
                        name=mk("ipe", t, m))
                    ip_evac[(t, m)] = ev
                else:
                    ev = p.op("vector", (lambda bank: lambda v: v.tensor_copy(
                        dtraw[:], pip[0:32, bank, :]))(bank), deps=[lastmm], name=mk("ipe", t, m))
                    ip_evac[(t, m)] = ev

            hs = p.op("vector", lambda v: v.tensor_copy(halo[:], xBC[:, :, 525:528]),
                      deps=[ip_evac[(t, m)] for m in range(16, 34)], name=mk("hsave", t))
            silu_ev = {}
            last_tap = None
            for ft in range(18):
                bank = ft % 2
                cdeps = [ip_evac[(t, 16 + ft)], LD[8]]
                if ft < 2:
                    if halo_restore is not None:
                        cdeps.append(halo_restore)
                    if tile_mark is not None:
                        cdeps.append(tile_mark)
                key = (t, ft - 2)
                if key in cv_evac:
                    cdeps.append(cv_evac[key])
                for tau in range(4):
                    for sub2 in range(4):
                        mmv = p.op("tensor", (lambda ft, tau, sub2, bank: lambda te: te.matmul(
                            pcv[sub2 * 32:(sub2 + 1) * 32, bank, :],
                            cd[sub2 * 32:(sub2 + 1) * 32, (ft * 4 + tau) * 32:(ft * 4 + tau + 1) * 32],
                            xBC[sub2 * 32:(sub2 + 1) * 32, ft, 13 + tau:13 + tau + 512],
                            start=(tau == 0), stop=(tau == 3),
                            tile_position=(sub2 * 32, sub2 * 32)))(ft, tau, sub2, bank),
                            deps=cdeps if (tau == 0 and sub2 == 0) else [], name=mk("cv", t, ft, tau, sub2))
                last_tap = mmv
                ev = p.op("scalar", (lambda ft, bank: lambda s: s.activation(
                    xBC[:, ft, 16:528], pcv[:, bank, :], AF.Silu, bias=convb[:, ft:ft + 1]))(ft, bank),
                    deps=[mmv, hs], name=mk("cse", t, ft))
                cv_evac[(t, ft)] = ev
                silu_ev[ft] = ev
            silu_ev_by_t[t] = silu_ev
            halo_restore = p.op("vector", lambda v: v.tensor_copy(xBC[:, :, 13:16], halo[:]),
                                deps=[last_tap, hs], name=mk("hrest", t))
            cpk = p.op("vector", (lambda t: lambda v: v.tensor_copy(
                cpark[:, t * 512:(t + 1) * 512], xBC[:, 17, 16:528]))(t),
                deps=[silu_ev[17]], name=mk("cpk", t))
            cpks[t] = cpk

            dt_ev = ip_evac[(t, 34)]
            o1 = p.op("scalar", lambda s: s.activation(dts1[:], dtraw[:], AF.Abs, bias=dtb[:]),
                      deps=[dt_ev] + (LD if t == 0 else []), name=mk("dtabs", t))
            o2 = p.op("scalar", lambda s: s.activation(dts1[:], dts1[:], AF.Exp, scale=-1.0),
                      deps=[o1], name=mk("dtexp", t))
            o3 = p.op("scalar", lambda s: s.activation(dts1[:], dts1[:], AF.Ln, bias=1.0),
                      deps=[o2], name=mk("dtln1", t))
            o4 = p.op("vector", lambda v: v.tensor_scalar_add(dts2[:], dtraw[:], dtb[:]),
                      deps=[dt_ev], name=mk("dtadd", t))
            o5 = p.op("vector", lambda v: v.tensor_single_scalar(dts2[:], dts2[:], 0.0, ALU.max),
                      deps=[o4], name=mk("dtmax", t))
            o6 = p.op("vector", lambda v: v.tensor_add(dtraw[:], dts1[:], dts2[:]),
                      deps=[o3, o5], name=mk("dt", t))
            o7 = p.op("vector", lambda v: v.tensor_scalar_mul(dts2[:], dtraw[:], negA[:]),
                      deps=[o6, negAo], name=mk("a", t))
            cs_ops = []
            for c4 in range(4):
                cs_ops.append(p.op("vector", (lambda c4: lambda v: v.tensor_tensor_scan(
                    csf[:, c4 * 128:(c4 + 1) * 128], dts2[:, c4 * 128:(c4 + 1) * 128], zeros32[:],
                    0.0, ALU.add, ALU.add))(c4), deps=[o7, init0], name=mk("cs", t, c4)))
            lnd = p.op("scalar", lambda s: s.activation(dts1[:], dtraw[:], AF.Ln),
                       deps=[o6, o3], name=mk("lndt", t))
            uo = p.op("vector", lambda v: v.tensor_sub(dts1[:], csf[:], dts1[:]),
                      deps=[lnd] + cs_ops, name=mk("u", t))
            uhi = p.op("vector", lambda v: v.tensor_copy(u2[0:32, :], dts1[:]),
                       deps=[uo, initU, chunk_mark], name=mk("uhi", t))
            ulo = p.op("vector", lambda v: v.tensor_sub(u2[32:64, :], dts1[:], u2[0:32, :]),
                       deps=[uhi], name=mk("ulo", t))
            chi = p.op("vector", lambda v: v.tensor_copy(cshl[:, 0, :], csf[:]),
                       deps=cs_ops, name=mk("chi", t))
            clo = p.op("vector", lambda v: v.tensor_sub(cshl[:, 1, :], csf[:], cshl[:, 0, :]),
                       deps=[chi], name=mk("clo", t))
            bo1 = p.dma("sync", (lambda t: lambda e: e.dma_start(
                out=csdr[:, :, t * 512:(t + 1) * 512].rearrange("r h t -> h r t"), in_=cshl[:]))(t),
                deps=[chi, clo], name=mk("bo1", t))
            bo1s.append(bo1)

            bifs = {}
            for c4 in range(4):
                bifs[c4] = p.dma("sync", (lambda t, c4: lambda e: e.dma_start(
                    out=csfl[:, c4, :].rearrange("r (h i) -> r h i", h=32),
                    in_=csdr[:, :, t * 512 + c4 * 128: t * 512 + (c4 + 1) * 128]))(t, c4),
                    deps=[bo1] + csfl_users.get(c4, []), name=mk("bif", t, c4))
            csfl_users = {}
            for c4 in range(4):
                c = t * 4 + c4
                col0 = 16 + c4 * 128
                cm = [chunk_mark]
                bif = bifs[c4]
                exps = []
                for q in range(8):
                    bank = q % 2
                    ddeps = [bif, ulo]
                    if q >= 2:
                        ddeps.append(exps[q - 2])
                    else:
                        ddeps += cm
                    m1 = p.op("tensor", (lambda q, bank, c4: lambda te: te.matmul(
                        pdd[:, bank, :], ones2[:], csfl[:, c4, q * 512:(q + 1) * 512],
                        start=True, stop=False))(q, bank, c4), deps=ddeps, name=mk("D1", c, q))
                    csfl_users.setdefault(c4, []).append(m1)
                    p.op("tensor", (lambda c4, q, bank: lambda te: te.matmul(
                        pdd[:, bank, :], u2[:, c4 * 128:(c4 + 1) * 128],
                        esel[:, q * 512:(q + 1) * 512], start=False, stop=False))(c4, q, bank),
                        name=mk("D2", c, q))
                    m3 = p.op("tensor", (lambda bank: lambda te: te.matmul(
                        pdd[:, bank, :], idtb[:], tri[:], start=False, stop=True))(bank),
                        name=mk("D3", c, q))
                    ex = p.op("scalar", (lambda q, bank: lambda s: s.activation(
                        ltd[:, q * 512:(q + 1) * 512], pdd[:, bank, :], AF.Exp))(q, bank),
                        deps=[m3] + cm, name=mk("Dx", c, q))
                    exps.append(ex)
                wso = p.op("vector", lambda v: v.tensor_copy(
                    wsl[:].unsqueeze(2), ltd[:].rearrange("p (h i) -> p h i", h=32)[:, :, 127:128]),
                    deps=exps + cm, name=mk("w", c))
                gmm = p.op("tensor", (lambda col0: lambda te: te.matmul(
                    psm[:, 0:128], xBC[:, 16, col0:col0 + 128], xBC[:, 17, col0:col0 + 128],
                    start=True, stop=True))(col0),
                    deps=[silu_ev[16], silu_ev[17]] + cm, name=mk("gt", c))
                gev = p.op("scalar", lambda s: s.copy(gt[:], psm[:, 0:128]), deps=[gmm], name=mk("gte", c))
                mt = p.op("vector", lambda v: v.tensor_tensor(
                    ltd[:].rearrange("p (h i) -> p h i", h=32),
                    ltd[:].rearrange("p (h i) -> p h i", h=32),
                    gt[:].unsqueeze(1).broadcast_to([128, 32, 128]), ALU.mult),
                    deps=[gev, wso] + exps, name=mk("mt", c))
                dcp_ops = []
                prev_ct = None
                for q in range(8):
                    emm = p.op("tensor", (lambda q, c4: lambda te: te.matmul(
                        pec[:], ones2[:], csfl[:, c4, q * 512:(q + 1) * 512],
                        start=True, stop=True))(q, c4),
                        deps=([bif] + cm if q == 0 else [prev_ct]), name=mk("ecm", c, q))
                    csfl_users.setdefault(c4, []).append(emm)
                    exo = p.op("scalar", (lambda q: lambda s: s.activation(
                        ecst[:], pec[:], AF.Exp))(q), deps=[emm], name=mk("ecx", c, q))
                    dcp = p.op("vector", (lambda q, c: lambda v: v.tensor_copy(
                        dtot_all[:, c, 4 * q:4 * q + 4].unsqueeze(2),
                        ecst[:].rearrange("p (h i) -> p h i", h=4)[:, :, 127:128]))(q, c),
                        deps=[exo], name=mk("dcp", c, q))
                    dcp_ops.append(dcp)
                    prev_ct = p.marker("scalar", [dcp], name=mk("ctm", c, q))
                dcp_by_c[c] = dcp_ops
                btmd = p.dma("sync", (lambda col0: lambda s: s.dma_start_transpose(
                    btm[:], xBC[:, 16, col0:col0 + 128]))(col0),
                    deps=[silu_ev[16]] + cm, name=mk("btm", c))
                xtm_dmas = []
                for ft in range(16):
                    xq = "scalar" if ft % 2 == 0 else "sync"
                    xtm_dmas.append(p.dma(xq, (lambda ft, col0: lambda s: s.dma_start_transpose(
                        xtm[:, ft * 128:(ft + 1) * 128], xBC[:, ft, col0:col0 + 128]))(ft, col0),
                        deps=[silu_ev[ft]] + cm, name=mk("xtm", c, ft)))
                x2o = p.op("vector", lambda v: v.tensor_tensor(
                    x2t[:].rearrange("p (h q) -> p h q", h=32),
                    xtm[:].rearrange("p (h q) -> p h q", h=32),
                    wsl[:].unsqueeze(2).broadcast_to([128, 32, 64]), ALU.mult),
                    deps=[wso] + xtm_dmas, name=mk("x2", c))
                yevs = []
                for hp in range(16):
                    slot = hp % 2
                    c0_ = 128
                    ydeps = [mt, xtm_dmas[hp]]
                    if hp >= 2:
                        ydeps.append(yevs[hp - 2])
                    else:
                        ydeps += cm
                    p.op("tensor", (lambda hp, slot, c0_: lambda te: te.matmul(
                        pdd[0:64, slot, c0_:c0_ + 128], xtm[:, hp * 128:hp * 128 + 64],
                        ltd[:, (2 * hp) * 128:(2 * hp + 1) * 128],
                        start=True, stop=True, tile_position=(0, 0)))(hp, slot, c0_),
                        deps=ydeps, name=mk("ya", c, hp))
                    ymd = p.op("tensor", (lambda hp, slot, c0_: lambda te: te.matmul(
                        pdd[64:128, slot, c0_:c0_ + 128], xtm[:, hp * 128 + 64:hp * 128 + 128],
                        ltd[:, (2 * hp + 1) * 128:(2 * hp + 2) * 128],
                        start=True, stop=True, tile_position=(0, 64)))(hp, slot, c0_),
                        name=mk("yb", c, hp))
                    ev = p.op("vector", (lambda hp, slot, col0, c0_: lambda v: v.scalar_tensor_tensor(
                        yf[:, hp * 128:(hp + 1) * 128], xBC[:, hp, col0:col0 + 128], drep[:, hp:hp + 1],
                        pdd[:, slot, c0_:c0_ + 128], ALU.mult, ALU.add))(hp, slot, col0, c0_),
                        deps=[ymd], name=mk("ye", c, hp))
                    yevs.append(ev)
                yfp = p.dma("sync", (lambda c: lambda e: e.dma_start(
                    out=yfdr[:, c, :], in_=yf[:]))(c), deps=yevs, name=mk("yfp", c))
                yfps.append(yfp)
                lpark = p.dma("sync", (lambda c: lambda e: e.dma_start(
                    out=ldr[:, c, :], in_=Sbf[:]))(c), deps=[Sbf_ready], name=mk("lpark", c))
                lparks.append(lpark)
                sup = []
                for grp in range(4):
                    last = None
                    for hh in range(8):
                        h = grp * 8 + hh
                        gdeps = []
                        if hh == 0:
                            gdeps = [btmd, x2o, prev_ct]
                            if grp > 0:
                                gdeps.append(sup[grp - 1])
                            else:
                                gdeps += cm
                        last = p.op("tensor", (lambda h, hh: lambda te: te.matmul(
                            pec[:, hh * 64:(hh + 1) * 64], btm[:],
                            x2t[:, h * 64:(h + 1) * 64], start=(hh == 0), stop=(hh == 7)))(h, hh),
                            deps=gdeps, name=mk("sl", c, h))
                    s1_ = p.op("vector", (lambda grp, c: lambda v: v.tensor_tensor(
                        S[:, grp * 512:(grp + 1) * 512].rearrange("p (h q) -> p h q", h=8),
                        S[:, grp * 512:(grp + 1) * 512].rearrange("p (h q) -> p h q", h=8),
                        dtot_all[:, c, grp * 8:(grp + 1) * 8].unsqueeze(2).broadcast_to([128, 8, 64]),
                        ALU.mult))(grp, c), deps=[S_ready, last] + dcp_ops, name=mk("su1", c, grp))
                    s2_ = p.op("vector", (lambda grp: lambda v: v.tensor_add(
                        S[:, grp * 512:(grp + 1) * 512], S[:, grp * 512:(grp + 1) * 512],
                        pec[:]))(grp), deps=[s1_], name=mk("su2", c, grp))
                    sup.append(s2_)
                scast = p.op("vector", lambda v: v.tensor_copy(Sbf[:], S[:]),
                             deps=sup + [lpark], name=mk("scast", c))
                Sbf_ready = scast
                S_ready = sup[-1]
                chunk_mark = p.marker("vector", [scast, wso, mt, x2o, yfp] + yevs + dcp_ops,
                                      name=mk("cmark", c))
            tile_mark = p.marker("vector", [chunk_mark, cpk], name=mk("tmark1", t))
            chunk_mark = tile_mark
            ip_evac[(t, 32)] = ip_evac[(t, 34)]
            cv_evac[(t, 16)] = tile_mark
            cv_evac[(t, 17)] = tile_mark

        l1done = p.marker("sync", [tile_mark, S_ready, Sbf_ready] + yfps + lparks +
                          list(zdma.values()) + bo1s, name="l1done")

        ccd = p.dma("sync", lambda e: e.dma_start(out=ccin, in_=S[:]), deps=[S_ready], name="ccd")
        ccop = p.cc(lambda g: g.collective_compute(
            "AllGather", ALU.bypass, replica_groups=[[0, 1], [2, 3], [4, 5], [6, 7]],
            ins=[ccin.opt()], outs=[ccout.opt()]), deps=[ccd], name="ccag")
        crb = p.dma("gpsimd", lambda g: g.dma_start(out=S[:], in_=ccout[0:128, :]),
                    deps=[ccop, ccd], name="crb")
        sio = p.op("vector", lambda v: v.tensor_scalar_mul(S[:], S[:], iso[:]),
                   deps=[crb, LD[-1]], name="sio")
        pf0 = p.op("vector", lambda v: v.memset(pfxall[:, 0, :], 1.0), name="pf0")
        pfo = pf0
        for c in range(1, TC):
            pfo = p.op("vector", (lambda c: lambda v: v.tensor_mul(
                pfxall[:, c, :], pfxall[:, c - 1, :], dtot_all[:, c - 1, :]))(c),
                deps=[pfo] + dcp_by_c[c - 1], name=mk("pfx", c))
        pfx_done = pfo

        es1.close()
        ctil = sb("ctil", [128, 4096], BF16)
        zbk = sb("zbk", [128, 2, 16, 128], BF16)
        tmpZ = sb("tmpZ", [128, 2, 2048], BF16)
        ytil = sb("ytil", [128, 16, 256], BF16)
        woutf = sb("woutf", [128, 16, 1024], BF16)
        yfb2 = sb("yfb2", [128, 2048], BF16)
        uln = sb("uln", [128, 8, 512], BF16)
        lrow = sb("lrow", [1, 512])
        lrow2 = sb("lrow2", [1, 512])
        stg = sb("stg", [128, 512])
        wnscs = []
        for kt in range(16):
            wlf = p.dma("gpsimd", (lambda kt: lambda g: g.dma_start(
                out=woutf[:, kt, :], in_=woutT_d[kt * 128:(kt + 1) * 128, :]))(kt),
                deps=[l1done], name=mk("wlf", kt))
            wnscs.append(p.op("vector", (lambda kt: lambda v: v.tensor_scalar_mul(
                woutf[:, kt, :], woutf[:, kt, :], nw[:, kt:kt + 1]))(kt),
                deps=[wlf, LD[13]], name=mk("nscf", kt)))

        xb2_users = {}
        prev_r1 = l1done
        prev_yt = l1done
        csfl_users = {0: [l1done], 1: [l1done], 2: [l1done], 3: [l1done]}
        prev_sadd = None
        y2_by_c = {}
        yt_by_c = {}
        gmul_by_c = {}
        zsl_by_c = {}
        rs_by_c = {}
        wout_users = []
        op_free = []
        uln_free = []
        uln_half0 = None
        prev_x1s = []
        chunk_mark = l1done
        tile_mark2 = None
        for t2 in range(TT_):
            xldt = p.dma("gpsimd", (lambda t2: lambda g: g.dma_start(
                out=xb[:],
                in_=x_d[:, 3 + t2 * 512:3 + (t2 + 1) * 512].rearrange("(kc p) t -> p kc t", p=128)))(t2),
                deps=xb2_users.get(t2 - 1, []) + (xb_users[TT_ - 1] if t2 == 0 else []),
                name=mk("xld2", t2))
            xb2_users[t2] = []
            bifs = {}
            for c4 in range(4):
                bifs[c4] = p.dma("sync", (lambda t2, c4: lambda e: e.dma_start(
                    out=csfl[:, c4, :].rearrange("r (h i) -> r h i", h=32),
                    in_=csdr[:, :, t2 * 512 + c4 * 128: t2 * 512 + (c4 + 1) * 128]))(t2, c4),
                    deps=csfl_users.get(c4, []), name=mk("bif2", t2, c4))
            csfl_users = {}
            for c4 in range(4):
                c = t2 * 4 + c4
                cb2 = c % 2
                cm = [chunk_mark]
                bif = bifs[c4]
                yfc = yf if cb2 == 0 else yfb2
                zrb = p.dma("sync", (lambda t2, c4, cb2: lambda e: e.dma_start(
                    out=zbk[:, cb2, :, :],
                    in_=zdr[:, :, t2 * 512 + c4 * 128: t2 * 512 + (c4 + 1) * 128]))(t2, c4, cb2),
                    deps=[l1done] + ([zsl_hist[c - 2]] if (c - 2) in zsl_hist else []),
                    name=mk("zrb2", c))
                ct_ops = []
                prev_ct = None
                for q in range(8):
                    emm = p.op("tensor", (lambda q, c4: lambda te: te.matmul(
                        pec[:], ones2[:], csfl[:, c4, q * 512:(q + 1) * 512],
                        start=True, stop=True))(q, c4),
                        deps=([bif] + cm if q == 0 else [prev_ct]), name=mk("ecm2", c, q))
                    csfl_users.setdefault(c4, []).append(emm)
                    exo = p.op("scalar", (lambda q: lambda s: s.activation(
                        ecst[:], pec[:], AF.Exp))(q), deps=[emm], name=mk("ecx2", c, q))
                    ct = p.op("vector", (lambda q, c, t2: lambda v: v.tensor_tensor(
                        ctil[:, q * 512:(q + 1) * 512].rearrange("p (h i) -> p h i", h=4),
                        ecst[:].rearrange("p (h i) -> p h i", h=4),
                        cpark[:, c * 128:(c + 1) * 128].unsqueeze(1).broadcast_to([128, 4, 128]),
                        ALU.mult))(q, c, t2), deps=[exo, cpks[t2]] + y2_by_c.get((c - 1, q), cm),
                        name=mk("ct2", c, q))
                    ct_ops.append(ct)
                    prev_ct = p.marker("scalar", [ct], name=mk("ctm2", c, q))
                ssc1 = p.op("vector", (lambda c: lambda v: v.tensor_tensor(
                    ltd[:, 0:2048].rearrange("p (h q) -> p h q", h=32),
                    S[:].rearrange("p (h q) -> p h q", h=32),
                    pfxall[:, c, :].unsqueeze(2).broadcast_to([128, 32, 64]), ALU.mult))(c),
                    deps=[sio, pfx_done] + ([prev_sadd] if prev_sadd else cm), name=mk("ssc", c))
                lrb = p.dma("sync", (lambda c: lambda e: e.dma_start(
                    out=Sbf[:], in_=ldr[:, c, :]))(c),
                    deps=[l1done] + cm, name=mk("lrb", c))
                sadd = p.op("vector", lambda v: v.tensor_add(Sbf[:], Sbf[:], ltd[:, 0:2048]),
                            deps=[lrb, ssc1], name=mk("sadd", c))
                prev_sadd = sadd
                yrb = p.dma("gpsimd", (lambda c, yfc: lambda e: e.dma_start(
                    out=yfc[:], in_=yfdr[:, c, :]))(c, yfc),
                    deps=[l1done] + yt_by_c.get(c - 2, []) + gmul_by_c.get(c - 2, []),
                    name=mk("yrb", c))
                ye2s = []
                for hp in range(16):
                    slot = hp % 2
                    c0_ = 128
                    ydeps = [sadd, ct_ops[(2 * hp) // 4]]
                    if hp >= 2:
                        ydeps.append(ye2s[hp - 2])
                    else:
                        ydeps += cm
                    ymc = p.op("tensor", (lambda hp, slot, c0_: lambda te: te.matmul(
                        pdd[0:64, slot, c0_:c0_ + 128], Sbf[:, hp * 128:hp * 128 + 64],
                        ctil[:, (2 * hp) * 128:(2 * hp + 1) * 128],
                        start=True, stop=True, tile_position=(0, 0)))(hp, slot, c0_),
                        deps=ydeps, name=mk("y2a", c, hp))
                    ymd = p.op("tensor", (lambda hp, slot, c0_: lambda te: te.matmul(
                        pdd[64:128, slot, c0_:c0_ + 128], Sbf[:, hp * 128 + 64:hp * 128 + 128],
                        ctil[:, (2 * hp + 1) * 128:(2 * hp + 2) * 128],
                        start=True, stop=True, tile_position=(0, 64)))(hp, slot, c0_),
                        deps=[ct_ops[(2 * hp + 1) // 4]], name=mk("y2b", c, hp))
                    ev = p.op("vector", (lambda hp, slot, c0_, yfc: lambda v: v.tensor_add(
                        yfc[:, hp * 128:(hp + 1) * 128], yfc[:, hp * 128:(hp + 1) * 128],
                        pdd[:, slot, c0_:c0_ + 128]))(hp, slot, c0_, yfc),
                        deps=[ymc, ymd, yrb], name=mk("ye2", c, hp))
                    ye2s.append(ev)
                    y2_by_c.setdefault((c, (2 * hp) // 4), []).append(ymc)
                    y2_by_c.setdefault((c, (2 * hp + 1) // 4), []).append(ymd)
                zsl = p.op("scalar", (lambda cb2: lambda s: s.activation(
                    tmpZ[:, cb2, :], zbk[:, cb2, :, :].rearrange("p a b -> p (a b)"),
                    AF.Tanh, scale=0.5))(cb2),
                    deps=[zrb] + rs_by_c.get(c - 2, []) + (cm if c4 == 0 else []),
                    name=mk("zth", c))
                zs2 = p.op("vector", (lambda cb2: lambda v: v.scalar_tensor_tensor(
                    tmpZ[:, cb2, :], tmpZ[:, cb2, :], 1.0,
                    zbk[:, cb2, :, :].rearrange("p a b -> p (a b)"),
                    ALU.add, ALU.mult))(cb2), deps=[zsl], name=mk("zs2", c))
                zsl_hist[c] = zs2
                gmul = p.op("vector", (lambda cb2, yfc: lambda v: v.tensor_mul(
                    yfc[:], yfc[:], tmpZ[:, cb2, :]))(cb2, yfc),
                    deps=[zs2] + ye2s, name=mk("gmul", c))
                gmul_by_c[c] = [gmul]
                gsqo = p.op("scalar", (lambda cb2, yfc: lambda s: s.activation(
                    tmpZ[:, cb2, :], yfc[:], AF.Square))(cb2, yfc),
                    deps=[gmul], name=mk("gsq", c))
                rs_last = None
                for m16 in range(16):
                    rs_last = p.op("tensor", (lambda m16, cb2: lambda te: te.matmul(
                        pcv[0:1, 0, 0:128], ones128[:], tmpZ[:, cb2, m16 * 128:(m16 + 1) * 128],
                        start=(m16 == 0), stop=(m16 == 15)))(m16, cb2),
                        deps=([gsqo, prev_r1] + cm if m16 == 0 else []),
                        name=mk("rs", c, m16))
                rs_by_c[c] = [rs_last]
                r1 = p.op("vector", (lambda cb2: lambda v: v.tensor_scalar(
                    stg[0:1, cb2 * 256:cb2 * 256 + 128], pcv[0:1, 0, 0:128],
                    1.0 / 2048.0, 4e-5, ALU.mult, ALU.add))(cb2),
                    deps=[rs_last], name=mk("r1", c))
                prev_r1 = r1
                r2 = p.op("scalar", (lambda cb2: lambda s: s.activation(
                    stg[0:1, cb2 * 256 + 128:cb2 * 256 + 256],
                    stg[0:1, cb2 * 256:cb2 * 256 + 128], AF.Ln))(cb2),
                          deps=[r1], name=mk("r2", c))
                r3 = p.op("scalar", (lambda cb2: lambda s: s.activation(
                    stg[0:1, cb2 * 256:cb2 * 256 + 128],
                    stg[0:1, cb2 * 256 + 128:cb2 * 256 + 256],
                    AF.Exp, scale=-0.5))(cb2),
                          deps=[r2], name=mk("r3", c))
                rb = p.op("tensor", (lambda cb2: lambda te: te.matmul(
                    pcv[:, 1, 0:128], onesf[:], stg[0:1, cb2 * 256:cb2 * 256 + 128],
                    start=True, stop=True))(cb2),
                    deps=[r3, prev_yt] + cm, name=mk("rb", c))
                yt = p.op("vector", (lambda c4, yfc: lambda v: v.tensor_tensor(
                    ytil[:, :, (c4 % 2) * 128:(c4 % 2) * 128 + 128],
                    yfc[:].rearrange("p (a b) -> p a b", a=16),
                    pcv[:, 1, 0:128].unsqueeze(1).broadcast_to([128, 16, 128]),
                    ALU.mult))(c4, yfc), deps=[rb, gmul] + wout_users, name=mk("yt", c))
                yt_by_c[c] = [yt]
                prev_yt = yt
                chunk_mark = p.marker("vector",
                                      [yt, r1, zs2, gsqo, sadd] + ye2s + ct_ops,
                                      name=mk("cmark2", c))
                if c4 % 2 == 1:
                    hf = c4 // 2
                    op_evs = []
                    wout_users = []
                    for kt in range(16):
                        for m in range(8):
                            st_ = (kt == 0)
                            sp_ = (kt == 15)
                            kdeps = []
                            if m == 0:
                                kdeps = [yt, chunk_mark] + (wnscs if (t2 == 0 and hf == 0) else [])
                                if kt == 0:
                                    kdeps += op_free
                            lastk = p.op("tensor", (lambda m, kt, st_, sp_: lambda te: te.matmul(
                                pbanks[m][:, 0:256],
                                woutf[:, kt, m * 128:(m + 1) * 128],
                                ytil[:, kt, :],
                                start=st_, stop=sp_))(m, kt, st_, sp_),
                                deps=kdeps, name=mk("op", t2, hf, kt, m))
                            if kt == 15:
                                wout_users.append(lastk)
                                ev = p.op("vector", (lambda m, hf: lambda v: v.scalar_tensor_tensor(
                                    uln[:, m, hf * 256:(hf + 1) * 256], pbanks[m][:, 0:256], 1.0,
                                    xb[:, m, hf * 256:(hf + 1) * 256],
                                    ALU.mult, ALU.add))(m, hf), deps=[lastk, xldt] + uln_free,
                                    name=mk("ope", t2, hf, m))
                                op_evs.append(ev)
                    xb2_users[t2].extend(op_evs)
                    op_free = op_evs
                    chunk_mark = p.marker("vector", [chunk_mark] + op_evs, name=mk("omark", t2, hf))
                    if hf == 0:
                        uln_half0 = op_evs
                    else:
                        uln_free = []
                        lns_last = None
                        lnq_last = None
                        for m in range(8):
                            sq_ = p.op("scalar", (lambda m: lambda s: s.activation(
                                tmpZ[:, 0, 0:512], uln[:, m, :], AF.Square))(m),
                                deps=[op_evs[m], uln_half0[m]] + ([lnq_last] if lnq_last else []) +
                                     ((rs_by_c.get(t2 * 4 + 2, []) + rs_by_c.get(t2 * 4 + 3, []))
                                      if m == 0 else []),
                                name=mk("usq", t2, m))
                            lns_last = p.op("tensor", (lambda m: lambda te: te.matmul(
                                pcv[0:1, 0, 0:512], ones128[:], uln[:, m, :],
                                start=(m == 0), stop=(m == 7)))(m),
                                deps=[op_evs[m], uln_half0[m]] + (op_evs if m == 0 else []),
                                name=mk("lns", t2, m))
                            lnq_last = p.op("tensor", (lambda m: lambda te: te.matmul(
                                pcv[0:1, 1, 0:512], ones128[:], tmpZ[:, 0, 0:512],
                                start=(m == 0), stop=(m == 7)))(m), deps=[sq_], name=mk("lnq", t2, m))
                        l1 = p.op("vector", lambda v: v.tensor_scalar_mul(lrow[:], pcv[0:1, 0, 0:512], 1.0 / 1024.0),
                                  deps=[lns_last], name=mk("l1", t2))
                        l2 = p.op("vector", lambda v: v.tensor_scalar_mul(lrow2[:], pcv[0:1, 1, 0:512], 1.0 / 1024.0),
                                  deps=[lnq_last], name=mk("l2", t2))
                        l3 = p.op("vector", lambda v: v.tensor_mul(stg[0:1, :], lrow[:], lrow[:]),
                                  deps=[l1], name=mk("l3", t2))
                        l4 = p.op("vector", lambda v: v.tensor_scalar_add(
                            lrow2[:], lrow2[:], 1e-5),
                            deps=[l2], name=mk("l4e", t2))
                        l4 = p.op("vector", lambda v: v.tensor_sub(lrow2[:], lrow2[:], stg[0:1, :]),
                                  deps=[l4, l3], name=mk("l4", t2))
                        l5 = p.op("scalar", lambda s: s.activation(stg[0:1, :], lrow2[:], AF.Ln),
                                  deps=[l4], name=mk("l5", t2))
                        l6 = p.op("scalar", lambda s: s.activation(lrow2[:], stg[0:1, :], AF.Exp, scale=-0.5),
                                  deps=[l5], name=mk("l6", t2))
                        l7 = p.op("vector", lambda v: v.tensor_mul(lrow[:], lrow[:], lrow2[:]),
                                  deps=[l6], name=mk("l7", t2))
                        b1 = p.op("tensor", lambda te: te.matmul(
                            pcv[:, 0, :], onesf[:], lrow2[:], start=True, stop=True),
                            deps=[l7, l1, lns_last], name=mk("b1", t2))
                        b2 = p.op("tensor", lambda te: te.matmul(
                            pcv[:, 1, :], onesf[:], lrow[:], start=True, stop=True),
                            deps=[l7, l2, lnq_last], name=mk("b2", t2))
                        sts = []
                        prev_st = None
                        for m in range(8):
                            a1 = p.op("vector", (lambda m: lambda v: v.tensor_mul(
                                stg[:], uln[:, m, :], pcv[:, 0, :]))(m),
                                deps=[b1] + ([prev_st] if prev_st else []), name=mk("a1", t2, m))
                            a2 = p.op("vector", (lambda m: lambda v: v.tensor_sub(
                                stg[:], stg[:], pcv[:, 1, :]))(m),
                                deps=[a1, b2], name=mk("a2", t2, m))
                            x1s = p.dma("sync", (lambda t2, m: lambda e: e.dma_start(
                                out=x1_d[m * 128:(m + 1) * 128, t2 * 512:(t2 + 1) * 512], in_=stg[:]))(t2, m),
                                deps=[a2], name=mk("x1s", t2, m))
                            prev_st = x1s
                            sts.append(x1s)
                            uln_free.append(x1s)
                        prev_x1s = sts
                        tile_mark2 = p.marker("vector", sts + op_evs + [b1, b2], name=mk("tmark2", t2))
                        chunk_mark = tile_mark2

        p.op("sync", lambda e: e.nop(), deps=prev_x1s, name="fin")
    return p.phase_wrap(start_ix, gate, name=pre + "_done")


def build_mamba2(T=2048):
    nc = bass.Bass("TRN2", target_bir_lowering=False)
    p = Prog(nc)
    specs = tuple(s for s in _MAMBA_DRAM if s[0] != "x")
    dr = declare_dram(nc, specs, T)
    dr["x"] = nc.dram_tensor("x", [1024, T + 3], F32, kind="ExternalInput").ap()
    iso_d = nc.dram_tensor("iso", [128, 1], F32, kind="ExternalInput").ap()
    x1_d = nc.dram_tensor("x1", [1024, T], F32, kind="ExternalOutput").ap()
    from contextlib import ExitStack
    ses = ExitStack()
    with ses:
        sems, dma_pools = make_sems(nc, ses)
        emit_mamba2(nc, p, dr, x1_d, iso_d, T)
        with nc.Block() as block:
            p.emit(block, sems, dma_pools)
    return nc


def launch_mamba2(inputs):
    x = np.asarray(inputs["L0_attn_x"])
    w = prep_mamba_weights(inputs)
    if "mamba2" not in _cache:
        _cache["mamba2"] = build_mamba2(2048)
    nc = _cache["mamba2"]
    in_maps = []
    for core in range(8):
        b, half = core // 2, core % 2
        xe = np.zeros((1024, 2051), dtype=np.float32)
        xe[:, 3:] = x[b, half * 2048:(half + 1) * 2048, :].T
        if half == 1:
            xe[:, 0:3] = x[b, 2045:2048, :].T
        m = {"x": xe, "iso": np.full((128, 1), float(half), dtype=np.float32)}
        m.update(w)
        in_maps.append(m)
    res = run_bass_kernel_spmd(nc, in_maps, core_ids=list(range(8)))
    x1 = np.stack([np.concatenate([np.asarray(res.results[b * 2 + h]["x1"]).T for h in range(2)], axis=0)
                   for b in range(4)], axis=0)
    return x1


# version 06
# speedup vs baseline: 1.4787x; 1.0270x over previous
import numpy as np
import ml_dtypes
import concourse.bass as bass
import concourse.mybir as mybir
from concourse.bass_utils import run_bass_kernel_spmd

F32 = mybir.dt.float32
BF16 = mybir.dt.bfloat16
AF = mybir.ActivationFunctionType
ALU = mybir.AluOpType
AX = mybir.AxisListType
BF = ml_dtypes.bfloat16

ENGINES = ("tensor", "vector", "scalar", "gpsimd", "sync")


class Op:
    __slots__ = ("eng", "fn", "deps", "inc", "cum", "name", "collective")

    def __init__(self, eng, fn, deps, inc, name, collective=False):
        self.eng = eng
        self.fn = fn
        self.deps = [d for d in deps if d is not None]
        self.inc = inc
        self.cum = None
        self.name = name
        self.collective = collective


class Prog:

    def __init__(self, nc):
        self.nc = nc
        self.ops = []

    def op(self, eng, fn, deps=(), name=""):
        o = Op(eng, fn, deps, 1, name)
        self.ops.append(o)
        return o

    def dma(self, eng, fn, deps=(), name=""):
        o = Op(eng, fn, deps, 16, name)
        self.ops.append(o)
        return o

    def cc(self, fn, deps=(), name="cc"):
        o = Op("gpsimd", fn, deps, 1, name, collective=True)
        self.ops.append(o)
        return o

    def marker(self, eng, deps, name="mark"):
        return self.op(eng, lambda e: e.nop(), deps=deps, name=name)

    def phase_wrap(self, start, gate, name="pg"):
        ops = self.ops[start:]
        if gate is not None:
            seen = set()
            for o in ops:
                if o.eng not in seen:
                    seen.add(o.eng)
                    o.deps.append(gate)
                    if len(seen) == len(ENGINES):
                        break
        return self.marker("sync", list(ops), name=name)

    def emit(self, block, sems, dma_pools):
        needed = set()
        for o in self.ops:
            for d in o.deps:
                needed.add(id(d))
        per_eng = {e: [] for e in ENGINES}
        for o in self.ops:
            per_eng[o.eng].append(o)
        cum = {e: 0 for e in ENGINES}
        dma_ct = {e: 0 for e in ENGINES}
        cc_ct = 0
        for e in ENGINES:
            for o in per_eng[e]:
                if id(o) not in needed:
                    continue
                if o.collective:
                    o.cum = ("cc", cc_ct, 1)
                    cc_ct += 1
                elif o.inc == 16:
                    pool = dma_pools[e]
                    i = dma_ct[e]
                    o.cum = (e, i % len(pool), 16 * (i // len(pool) + 1))
                    dma_ct[e] += 1
                else:
                    cum[e] += 1
                    o.cum = (e, None, cum[e])
        self.n_cc = cc_ct

        def sem_of(tag):
            e, slot, val = tag
            if e == "cc":
                return dma_pools["cc"][slot], val
            if slot is None:
                return sems[e], val
            return dma_pools[e][slot], val

        def runner(eng_name):
            def run(eng):
                watermark = {}
                for o in per_eng[eng_name]:
                    waits = {}
                    for d in o.deps:
                        assert d.cum is not None, f"dep {d.name} of {o.name} unmarked"
                        sem, val = sem_of(d.cum)
                        key = id(sem)
                        if val > waits.get(key, (None, 0))[1]:
                            waits[key] = (sem, val)
                    for key in sorted(waits):
                        sem, val = waits[key]
                        if val > watermark.get(key, 0):
                            eng.wait_ge(sem, val)
                            watermark[key] = val
                    if o.cum is not None and o.inc == 16 and not o.collective:
                        sem, val = sem_of(o.cum)
                        if val > 16:
                            eng.wait_ge(sem, val - 16)
                            watermark[id(sem)] = max(watermark.get(id(sem), 0), val - 16)
                    inst = o.fn(_TeProxy(eng) if eng_name == "tensor" else eng)
                    if o.cum is not None:
                        sem, val = sem_of(o.cum)
                        if o.collective:
                            inst.then_inc(sem)
                        else:
                            inst.then_inc(sem, o.inc)
            return run

        block.tensor(runner("tensor"))
        block.vector(runner("vector"))
        block.scalar(runner("scalar"))
        block.gpsimd(runner("gpsimd"))
        block.sync(runner("sync"))


class _TeProxy:

    def __init__(self, te):
        self._te = te

    def matmul(self, *a, **k):
        k.setdefault("skip_group_check", True)
        return self._te.matmul(*a, **k)

    def __getattr__(self, n):
        return getattr(self._te, n)


def _bf(x):
    return np.ascontiguousarray(x, dtype=BF)


def _f32(x):
    return np.ascontiguousarray(x, dtype=np.float32)



def prep_mamba_weights(inp):
    w = {}
    w["w_inT"] = _f32(np.asarray(inp["L0_attn_in_proj_w"]).T)
    cw = np.asarray(inp["L0_attn_conv_w"])[:, 0, :]
    cd = np.zeros((128, 72, 32), dtype=BF)
    ii = np.arange(32)
    for t in range(18):
        for tau in range(4):
            for sub in range(4):
                cd[sub * 32 + ii, t * 4 + tau, ii] = cw[t * 128 + sub * 32 + ii, tau].astype(BF)
    w["cd"] = cd
    w["conv_b"] = _f32(np.asarray(inp["L0_attn_conv_b"]).reshape(18, 128).T)
    w["dtb"] = _f32(np.asarray(inp["L0_attn_dt_bias"]).reshape(32, 1))
    w["alog"] = _f32(np.asarray(inp["L0_attn_A_log"]).reshape(32, 1))
    w["drep"] = _f32(np.repeat(np.asarray(inp["L0_attn_D_param"]), 64).reshape(16, 128).T)
    w["nw"] = _f32(np.asarray(inp["L0_attn_norm_w"]).reshape(16, 128).T)
    w["w_outT"] = _f32(np.asarray(inp["L0_attn_out_proj_w"]).T)
    w["ones2"] = np.ones((2, 128), dtype=BF)
    w["onesf"] = np.ones((1, 128), dtype=np.float32)
    esel = np.zeros((65, 4096), dtype=np.float32)
    for q in range(8):
        for hl in range(4):
            k = 4 * q + hl
            esel[k, q * 512 + hl * 128: q * 512 + (hl + 1) * 128] = -1.0
            esel[k + 32, q * 512 + hl * 128: q * 512 + (hl + 1) * 128] = -1.0
    esel[64, :] = -1e4
    w["esel"] = esel.astype(BF)
    tri = np.zeros((128, 512), dtype=np.float32)
    for k in range(128):
        for hl in range(4):
            tri[k, hl * 128 + k:(hl + 1) * 128] = 1e4
    w["tri"] = tri.astype(BF)
    w["idtb"] = np.eye(128, dtype=BF)
    return w


_MAMBA_DRAM = (("x", [1024, -1], F32), ("w_inT", [1024, 4384], F32), ("cd", [128, 72, 32], BF16),
               ("conv_b", [128, 18], F32), ("dtb", [32, 1], F32), ("alog", [32, 1], F32),
               ("drep", [128, 16], F32), ("nw", [128, 16], F32), ("w_outT", [2048, 1024], F32),
               ("ones2", [2, 128], BF16), ("onesf", [1, 128], F32), ("esel", [65, 4096], BF16),
               ("tri", [128, 512], BF16), ("idtb", [128, 128], BF16))


def declare_dram(nc, specs, T, pre=""):
    dr = {}
    for nm, shape, dt in specs:
        shape = [T if s == -1 else s for s in shape]
        dr[nm] = nc.dram_tensor(pre + nm, shape, dt, kind="ExternalInput").ap()
    return dr


def make_sems(nc, ses):
    sems = {e: ses.enter_context(nc.semaphore("sem_" + e)) for e in ENGINES}
    dma_pools = {"sync": [ses.enter_context(nc.semaphore(f"dp_sy{i}")) for i in range(24)],
                 "scalar": [ses.enter_context(nc.semaphore(f"dp_sc{i}")) for i in range(20)],
                 "gpsimd": [ses.enter_context(nc.semaphore(f"dp_gp{i}")) for i in range(12)],
                 "vector": [ses.enter_context(nc.semaphore(f"dp_ve{i}")) for i in range(8)],
                 "tensor": [ses.enter_context(nc.semaphore("dp_te0"))],
                 "cc": [ses.enter_context(nc.semaphore(f"dp_cc{i}")) for i in range(4)]}
    return sems, dma_pools


def emit_mamba(nc, p, dr, x1_d, T=4096, gate=None, pre="m"):
    TT_ = T // 512
    start_ix = len(p.ops)
    x_d = dr["x"]
    winT_d = dr["w_inT"]
    cd_d = dr["cd"]
    convb_d = dr["conv_b"]
    dtb_d = dr["dtb"]
    alog_d = dr["alog"]
    drep_d = dr["drep"]
    nw_d = dr["nw"]
    woutT_d = dr["w_outT"]
    ones2_d = dr["ones2"]
    onesf_d = dr["onesf"]
    esel_d = dr["esel"]
    tri_d = dr["tri"]
    idtb_d = dr["idtb"]
    zdr = nc.dram_tensor(pre + "_zdr", [128, 16, T], BF16, kind="Internal").ap()
    csdr = nc.dram_tensor(pre + "_csdr", [2, 32, T], BF16, kind="Internal").ap()

    from contextlib import ExitStack
    es = ExitStack()
    sb = lambda name, shape, dt=F32: es.enter_context(nc.sbuf_tensor(pre + name, shape, dt))
    ps = lambda name, shape, dt=F32: es.enter_context(nc.psum_tensor(pre + name, shape, dt))

    w_in = sb("w_in", [128, 8 * 4384], BF16)
    cd = sb("cd_sb", [128, 72 * 32], BF16)
    convb = sb("convb_sb", [128, 18])
    dtb = sb("dtb_sb", [32, 1])
    alog = sb("alog_sb", [32, 1])
    negA = sb("negA", [32, 1])
    drep = sb("drep_sb", [128, 16])
    nw = sb("nw_sb", [128, 16])
    ones2 = sb("ones2_sb", [2, 128], BF16)
    onesf = sb("onesf_sb", [1, 128])
    ones128 = sb("ones128", [128, 1], BF16)
    esel = sb("esel_sb", [65, 4096], BF16)
    tri = sb("tri_sb", [128, 512], BF16)
    idtb = sb("idtb_sb", [128, 128], BF16)
    zeros32 = sb("zeros32", [32, 128])

    xb = sb("xb", [128, 8, 512], BF16)
    xBC = sb("xBC", [128, 18, 528], BF16)
    halo = sb("halo", [128, 18, 3], BF16)
    zst = sb("zst", [128, 2, 512], BF16)
    dtraw = sb("dtraw", [32, 512])
    dts1 = sb("dts1", [32, 512])
    dts2 = sb("dts2", [32, 512])
    csf = sb("csf", [32, 512])
    u2 = sb("u2", [65, 512], BF16)
    cshl = sb("cshl", [32, 2, 512], BF16)
    csfl = sb("csfl", [2, 4096], BF16)
    ecst = sb("ecst", [128, 512], BF16)
    dtot = sb("dtot", [128, 32], BF16)
    ltd = sb("ltd", [128, 4096], BF16)
    wsl = sb("wsl", [128, 32], BF16)
    ctil = sb("ctil", [128, 4096], BF16)
    xtm = sb("xtm", [128, 2048], BF16)
    x2t = sb("x2t", [128, 2048], BF16)
    btm = sb("btm", [128, 128], BF16)
    gt = sb("gt", [128, 128], BF16)
    S = sb("S", [128, 2048])
    Sbf = sb("Sbf", [128, 2048], BF16)
    zbk = sb("zbk", [128, 16, 128], BF16)
    tmpZ = sb("tmpZ", [128, 2048], BF16)
    yf = sb("yf", [128, 2048], BF16)
    ytil = sb("ytil", [128, 16, 256], BF16)
    wout = sb("wout", [128, 2, 1024], BF16)
    uln = sb("uln", [128, 8, 512], BF16)
    lrow = sb("lrow", [1, 512])
    lrow2 = sb("lrow2", [1, 512])
    stg = sb("stg", [128, 512])

    pip = ps("pip", [128, 2, 512])
    pcv = ps("pcv", [128, 2, 512])
    pdd = ps("pdd", [128, 2, 512])
    psm = ps("psm", [128, 512])
    pec = ps("pec", [128, 512])
    pbanks = [pip[:, 0, :], pip[:, 1, :], pcv[:, 0, :], pcv[:, 1, :],
              pdd[:, 0, :], pdd[:, 1, :], psm[:], pec[:]]

    with es:
        ld = []
        for kc in range(8):
            ld.append(p.dma("gpsimd", (lambda kc: lambda g: g.dma_start(
                out=w_in[:, kc * 4384:(kc + 1) * 4384],
                in_=winT_d[kc * 128:(kc + 1) * 128, :]))(kc), name=f"ld_win{kc}"))
        ld.append(p.dma("sync", lambda e: e.dma_start(
            out=cd[:].rearrange("p (t f) -> p t f", f=32), in_=cd_d), name="ld_cd"))
        for nm, s_, d_ in (("convb", convb, convb_d), ("dtb", dtb, dtb_d), ("alog", alog, alog_d),
                           ("drep", drep, drep_d), ("nw", nw, nw_d), ("ones2", ones2, ones2_d),
                           ("onesf", onesf, onesf_d), ("esel", esel, esel_d), ("tri", tri, tri_d),
                           ("idtb", idtb, idtb_d)):
            ld.append(p.dma("sync", (lambda s_, d_: lambda e: e.dma_start(out=s_[:], in_=d_))(s_, d_),
                            name="ld_" + nm))
        LD = list(ld)
        init0 = p.op("vector", lambda v: v.memset(zeros32[:], 0.0), name="z32")
        p.op("vector", lambda v: v.memset(ones128[:], 1.0), name="o128")
        initH = p.op("vector", lambda v: v.memset(xBC[:, :, 13:16], 0.0), name="halo0")
        initU = p.op("vector", lambda v: v.memset(u2[64:65, :], 1.0), name="u2o")
        initS = p.op("vector", lambda v: v.memset(S[:], 0.0), name="S0")
        initSb = p.op("vector", lambda v: v.memset(Sbf[:], 0.0), name="Sbf0")
        expA = p.op("scalar", lambda s: s.activation(negA[:], alog[:], AF.Exp), deps=LD, name="expA")
        negAo = p.op("vector", lambda v: v.tensor_scalar_mul(negA[:], negA[:], -1.0), deps=[expA], name="negA")

        ip_evac = {}
        cv_evac = {}
        xb_users = {}
        zdma = {}
        halo_restore = None
        wout_users = []
        wl_mms = [None, None]
        Sbf_ready = initSb
        S_ready = initS
        chunk_mark = initH
        tile_mark = None
        prev_x1s = []
        zsl_hist = {}
        csfl_users = []
        uln_free = []
        op_free = []
        uln_half0 = None

        def mk(*a):
            return "_".join(str(x) for x in a)

        for t in range(TT_):
            xldt = p.dma("gpsimd", (lambda t: lambda g: g.dma_start(
                out=xb[:],
                in_=x_d[:, t * 512:(t + 1) * 512].rearrange("(kc p) t -> p kc t", p=128)))(t),
                deps=xb_users.get(t - 1, []), name=mk("xld", t))
            xb_users[t] = []

            for m in range(35):
                bank = m % 2
                rows = 128 if m < 34 else 32
                mdeps = [xldt] + (LD[:8] if t == 0 else [])
                key = (t, m - 2) if m >= 2 else (t - 1, 32 + m)
                if key in ip_evac:
                    mdeps.append(ip_evac[key])
                if m < 2 and tile_mark is not None:
                    mdeps.append(tile_mark)
                lastmm = None
                for k in range(8):
                    lastmm = p.op("tensor", (lambda m, k, bank, rows: lambda te: te.matmul(
                        pip[0:rows, bank, :],
                        w_in[:, k * 4384 + m * 128: k * 4384 + m * 128 + rows],
                        xb[:, k, :], start=(k == 0), stop=(k == 7)))(m, k, bank, rows),
                        deps=mdeps if k == 0 else [], name=mk("ip", t, m, k))
                xb_users[t].append(lastmm)
                if m < 16:
                    g4 = m // 2
                    zkey = (t, g4 - 1) if g4 > 0 else (t - 1, 7)
                    zwar = [zdma[zkey]] if zkey in zdma else []
                    ev = p.op("scalar", (lambda m, bank: lambda s: s.copy(
                        zst[:, m % 2, :], pip[:, bank, :]))(m, bank),
                        deps=[lastmm] + zwar, name=mk("ipe", t, m))
                    ip_evac[(t, m)] = ev
                    if m % 2 == 1:
                        zdma[(t, g4)] = p.dma("sync", (lambda t, g4: lambda e: e.dma_start(
                            out=zdr[:, g4 * 2:(g4 + 1) * 2, t * 512:(t + 1) * 512], in_=zst[:]))(t, g4),
                            deps=[ip_evac[(t, mm)] for mm in range(g4 * 2, g4 * 2 + 2)],
                            name=mk("zd", t, g4))
                elif m < 34:
                    ft = m - 16
                    ev = p.op("vector", (lambda ft, bank: lambda v: v.tensor_copy(
                        xBC[:, ft, 16:528], pip[:, bank, :]))(ft, bank),
                        deps=[lastmm] + ([halo_restore] if halo_restore is not None else []),
                        name=mk("ipe", t, m))
                    ip_evac[(t, m)] = ev
                else:
                    ev = p.op("vector", (lambda bank: lambda v: v.tensor_copy(
                        dtraw[:], pip[0:32, bank, :]))(bank), deps=[lastmm], name=mk("ipe", t, m))
                    ip_evac[(t, m)] = ev

            hs = p.op("vector", lambda v: v.tensor_copy(halo[:], xBC[:, :, 525:528]),
                      deps=[ip_evac[(t, m)] for m in range(16, 34)], name=mk("hsave", t))
            silu_ev = {}
            last_tap = None
            for ft in range(18):
                bank = ft % 2
                cdeps = [ip_evac[(t, 16 + ft)], LD[8]]
                if ft < 2:
                    if halo_restore is not None:
                        cdeps.append(halo_restore)
                    if tile_mark is not None:
                        cdeps.append(tile_mark)
                key = (t, ft - 2)
                if key in cv_evac:
                    cdeps.append(cv_evac[key])
                for tau in range(4):
                    for sub2 in range(4):
                        mmv = p.op("tensor", (lambda ft, tau, sub2, bank: lambda te: te.matmul(
                            pcv[sub2 * 32:(sub2 + 1) * 32, bank, :],
                            cd[sub2 * 32:(sub2 + 1) * 32, (ft * 4 + tau) * 32:(ft * 4 + tau + 1) * 32],
                            xBC[sub2 * 32:(sub2 + 1) * 32, ft, 13 + tau:13 + tau + 512],
                            start=(tau == 0), stop=(tau == 3),
                            tile_position=(sub2 * 32, sub2 * 32)))(ft, tau, sub2, bank),
                            deps=cdeps if (tau == 0 and sub2 == 0) else [], name=mk("cv", t, ft, tau, sub2))
                last_tap = mmv
                ev = p.op("scalar", (lambda ft, bank: lambda s: s.activation(
                    xBC[:, ft, 16:528], pcv[:, bank, :], AF.Silu, bias=convb[:, ft:ft + 1]))(ft, bank),
                    deps=[mmv, hs], name=mk("cse", t, ft))
                cv_evac[(t, ft)] = ev
                silu_ev[ft] = ev
            halo_restore = p.op("vector", lambda v: v.tensor_copy(xBC[:, :, 13:16], halo[:]),
                                deps=[last_tap, hs], name=mk("hrest", t))

            dt_ev = ip_evac[(t, 34)]
            o1 = p.op("scalar", lambda s: s.activation(dts1[:], dtraw[:], AF.Abs, bias=dtb[:]),
                      deps=[dt_ev] + (LD if t == 0 else []), name=mk("dtabs", t))
            o2 = p.op("scalar", lambda s: s.activation(dts1[:], dts1[:], AF.Exp, scale=-1.0),
                      deps=[o1], name=mk("dtexp", t))
            o3 = p.op("scalar", lambda s: s.activation(dts1[:], dts1[:], AF.Ln, bias=1.0),
                      deps=[o2], name=mk("dtln1", t))
            o4 = p.op("vector", lambda v: v.tensor_scalar_add(dts2[:], dtraw[:], dtb[:]),
                      deps=[dt_ev], name=mk("dtadd", t))
            o5 = p.op("vector", lambda v: v.tensor_single_scalar(dts2[:], dts2[:], 0.0, ALU.max),
                      deps=[o4], name=mk("dtmax", t))
            o6 = p.op("vector", lambda v: v.tensor_add(dtraw[:], dts1[:], dts2[:]),
                      deps=[o3, o5], name=mk("dt", t))
            o7 = p.op("vector", lambda v: v.tensor_scalar_mul(dts2[:], dtraw[:], negA[:]),
                      deps=[o6, negAo], name=mk("a", t))
            cs_ops = []
            for c4 in range(4):
                cs_ops.append(p.op("vector", (lambda c4: lambda v: v.tensor_tensor_scan(
                    csf[:, c4 * 128:(c4 + 1) * 128], dts2[:, c4 * 128:(c4 + 1) * 128], zeros32[:],
                    0.0, ALU.add, ALU.add))(c4), deps=[o7, init0], name=mk("cs", t, c4)))
            lnd = p.op("scalar", lambda s: s.activation(dts1[:], dtraw[:], AF.Ln),
                       deps=[o6, o3], name=mk("lndt", t))
            uo = p.op("vector", lambda v: v.tensor_sub(dts1[:], csf[:], dts1[:]),
                      deps=[lnd] + cs_ops, name=mk("u", t))
            uhi = p.op("vector", lambda v: v.tensor_copy(u2[0:32, :], dts1[:]),
                       deps=[uo, initU, chunk_mark], name=mk("uhi", t))
            ulo = p.op("vector", lambda v: v.tensor_sub(u2[32:64, :], dts1[:], u2[0:32, :]),
                       deps=[uhi], name=mk("ulo", t))
            chi = p.op("vector", lambda v: v.tensor_copy(cshl[:, 0, :], csf[:]),
                       deps=cs_ops, name=mk("chi", t))
            clo = p.op("vector", lambda v: v.tensor_sub(cshl[:, 1, :], csf[:], cshl[:, 0, :]),
                       deps=[chi], name=mk("clo", t))
            bo1 = p.dma("sync", (lambda t: lambda e: e.dma_start(
                out=csdr[:, :, t * 512:(t + 1) * 512].rearrange("r h t -> h r t"), in_=cshl[:]))(t),
                deps=[chi, clo], name=mk("bo1", t))

            for c4 in range(4):
                c = t * 4 + c4
                col0 = 16 + c4 * 128
                cm = [chunk_mark]
                bif = p.dma("sync", (lambda t, c4: lambda e: e.dma_start(
                    out=csfl[:].rearrange("r (h i) -> r h i", h=32),
                    in_=csdr[:, :, t * 512 + c4 * 128: t * 512 + (c4 + 1) * 128]))(t, c4),
                    deps=[bo1] + csfl_users, name=mk("bif", c))
                csfl_users = []
                zrb = p.dma("sync", (lambda t, c4: lambda e: e.dma_start(
                    out=zbk[:],
                    in_=zdr[:, :, t * 512 + c4 * 128: t * 512 + (c4 + 1) * 128]))(t, c4),
                    deps=[zdma[(t, g)] for g in range(8)] +
                         ([zsl_hist[c - 1]] if (c - 1) in zsl_hist else []),
                    name=mk("zrb", c))

                exps = []
                for q in range(8):
                    bank = q % 2
                    ddeps = [bif, ulo]
                    if q >= 2:
                        ddeps.append(exps[q - 2])
                    else:
                        ddeps += cm
                    m1 = p.op("tensor", (lambda q, bank: lambda te: te.matmul(
                        pdd[:, bank, :], ones2[:], csfl[:, q * 512:(q + 1) * 512],
                        start=True, stop=False))(q, bank), deps=ddeps, name=mk("D1", c, q))
                    csfl_users.append(m1)
                    p.op("tensor", (lambda c4, q, bank: lambda te: te.matmul(
                        pdd[:, bank, :], u2[:, c4 * 128:(c4 + 1) * 128],
                        esel[:, q * 512:(q + 1) * 512], start=False, stop=False))(c4, q, bank),
                        name=mk("D2", c, q))
                    m3 = p.op("tensor", (lambda bank: lambda te: te.matmul(
                        pdd[:, bank, :], idtb[:], tri[:], start=False, stop=True))(bank),
                        name=mk("D3", c, q))
                    ex = p.op("scalar", (lambda q, bank: lambda s: s.activation(
                        ltd[:, q * 512:(q + 1) * 512], pdd[:, bank, :], AF.Exp))(q, bank),
                        deps=[m3] + cm, name=mk("Dx", c, q))
                    exps.append(ex)
                wso = p.op("vector", lambda v: v.tensor_copy(
                    wsl[:].unsqueeze(2), ltd[:].rearrange("p (h i) -> p h i", h=32)[:, :, 127:128]),
                    deps=exps + cm, name=mk("w", c))
                gmm = p.op("tensor", (lambda col0: lambda te: te.matmul(
                    psm[:, 0:128], xBC[:, 16, col0:col0 + 128], xBC[:, 17, col0:col0 + 128],
                    start=True, stop=True))(col0),
                    deps=[silu_ev[16], silu_ev[17]] + cm, name=mk("gt", c))
                gev = p.op("scalar", lambda s: s.copy(gt[:], psm[:, 0:128]), deps=[gmm], name=mk("gte", c))
                mt = p.op("vector", lambda v: v.tensor_tensor(
                    ltd[:].rearrange("p (h i) -> p h i", h=32),
                    ltd[:].rearrange("p (h i) -> p h i", h=32),
                    gt[:].unsqueeze(1).broadcast_to([128, 32, 128]), ALU.mult),
                    deps=[gev, wso] + exps, name=mk("mt", c))
                ctil_ops = []
                prev_ct = None
                for q in range(8):
                    emm = p.op("tensor", (lambda q: lambda te: te.matmul(
                        pec[:], ones2[:], csfl[:, q * 512:(q + 1) * 512],
                        start=True, stop=True))(q),
                        deps=([bif] + cm if q == 0 else [prev_ct]), name=mk("ecm", c, q))
                    csfl_users.append(emm)
                    exo = p.op("scalar", (lambda q: lambda s: s.activation(
                        ecst[:], pec[:], AF.Exp))(q), deps=[emm], name=mk("ecx", c, q))
                    dcp = p.op("vector", (lambda q: lambda v: v.tensor_copy(
                        dtot[:, 4 * q:4 * q + 4].unsqueeze(2),
                        ecst[:].rearrange("p (h i) -> p h i", h=4)[:, :, 127:128]))(q),
                        deps=[exo], name=mk("dcp", c, q))
                    ct = p.op("vector", (lambda q, col0: lambda v: v.tensor_tensor(
                        ctil[:, q * 512:(q + 1) * 512].rearrange("p (h i) -> p h i", h=4),
                        ecst[:].rearrange("p (h i) -> p h i", h=4),
                        xBC[:, 17, col0:col0 + 128].unsqueeze(1).broadcast_to([128, 4, 128]),
                        ALU.mult))(q, col0), deps=[exo, silu_ev[17]], name=mk("ct", c, q))
                    ctil_ops += [ct, dcp]
                    prev_ct = p.marker("scalar", [ct, dcp], name=mk("ctm", c, q))
                btmd = p.dma("scalar", (lambda col0: lambda s: s.dma_start_transpose(
                    btm[:], xBC[:, 16, col0:col0 + 128]))(col0),
                    deps=[silu_ev[16]] + cm, name=mk("btm", c))
                xtm_dmas = []
                for ft in range(16):
                    xtm_dmas.append(p.dma("scalar", (lambda ft, col0: lambda s: s.dma_start_transpose(
                        xtm[:, ft * 128:(ft + 1) * 128], xBC[:, ft, col0:col0 + 128]))(ft, col0),
                        deps=[silu_ev[ft]] + cm, name=mk("xtm", c, ft)))
                x2o = p.op("vector", lambda v: v.tensor_tensor(
                    x2t[:].rearrange("p (h q) -> p h q", h=32),
                    xtm[:].rearrange("p (h q) -> p h q", h=32),
                    wsl[:].unsqueeze(2).broadcast_to([128, 32, 64]), ALU.mult),
                    deps=[wso] + xtm_dmas, name=mk("x2", c))
                yevs = []
                for hp in range(16):
                    slot = hp % 2
                    c0_ = 128
                    ydeps = [mt, ctil_ops[2 * (hp // 2)], Sbf_ready, xtm_dmas[hp]] + exps
                    if hp >= 2:
                        ydeps.append(yevs[hp - 2])
                    else:
                        ydeps += cm
                    p.op("tensor", (lambda hp, slot, c0_: lambda te: te.matmul(
                        pdd[0:64, slot, c0_:c0_ + 128], xtm[:, hp * 128:hp * 128 + 64],
                        ltd[:, (2 * hp) * 128:(2 * hp + 1) * 128],
                        start=True, stop=False, tile_position=(0, 0)))(hp, slot, c0_),
                        deps=ydeps, name=mk("ya", c, hp))
                    p.op("tensor", (lambda hp, slot, c0_: lambda te: te.matmul(
                        pdd[64:128, slot, c0_:c0_ + 128], xtm[:, hp * 128 + 64:hp * 128 + 128],
                        ltd[:, (2 * hp + 1) * 128:(2 * hp + 2) * 128],
                        start=True, stop=False, tile_position=(0, 64)))(hp, slot, c0_),
                        name=mk("yb", c, hp))
                    p.op("tensor", (lambda hp, slot, c0_: lambda te: te.matmul(
                        pdd[0:64, slot, c0_:c0_ + 128], Sbf[:, hp * 128:hp * 128 + 64],
                        ctil[:, (2 * hp) * 128:(2 * hp + 1) * 128],
                        start=False, stop=True, tile_position=(0, 0)))(hp, slot, c0_),
                        name=mk("yc", c, hp))
                    ymd = p.op("tensor", (lambda hp, slot, c0_: lambda te: te.matmul(
                        pdd[64:128, slot, c0_:c0_ + 128], Sbf[:, hp * 128 + 64:hp * 128 + 128],
                        ctil[:, (2 * hp + 1) * 128:(2 * hp + 2) * 128],
                        start=False, stop=True, tile_position=(0, 64)))(hp, slot, c0_),
                        name=mk("yd", c, hp))
                    ev = p.op("vector", (lambda hp, slot, col0: lambda v: v.scalar_tensor_tensor(
                        yf[:, hp * 128:(hp + 1) * 128], xBC[:, hp, col0:col0 + 128], drep[:, hp:hp + 1],
                        pdd[:, slot, c0_:c0_ + 128], ALU.mult, ALU.add))(hp, slot, col0),
                        deps=[ymd], name=mk("ye", c, hp))
                    yevs.append(ev)
                sup = []
                for grp in range(4):
                    last = None
                    for hh in range(8):
                        h = grp * 8 + hh
                        gdeps = []
                        if hh == 0:
                            gdeps = [btmd, x2o, prev_ct]
                            if grp > 0:
                                gdeps.append(sup[grp - 1])
                            else:
                                gdeps += cm
                        last = p.op("tensor", (lambda h, hh: lambda te: te.matmul(
                            pec[:, hh * 64:(hh + 1) * 64], btm[:],
                            x2t[:, h * 64:(h + 1) * 64], start=(hh == 0), stop=(hh == 7)))(h, hh),
                            deps=gdeps, name=mk("sl", c, h))
                    s1_ = p.op("vector", (lambda grp: lambda v: v.tensor_tensor(
                        S[:, grp * 512:(grp + 1) * 512].rearrange("p (h q) -> p h q", h=8),
                        S[:, grp * 512:(grp + 1) * 512].rearrange("p (h q) -> p h q", h=8),
                        dtot[:, grp * 8:(grp + 1) * 8].unsqueeze(2).broadcast_to([128, 8, 64]),
                        ALU.mult))(grp), deps=[S_ready, last] + ctil_ops, name=mk("su1", c, grp))
                    s2_ = p.op("vector", (lambda grp: lambda v: v.tensor_add(
                        S[:, grp * 512:(grp + 1) * 512], S[:, grp * 512:(grp + 1) * 512],
                        pec[:]))(grp), deps=[s1_], name=mk("su2", c, grp))
                    sup.append(s2_)
                scast = p.op("vector", lambda v: v.tensor_copy(Sbf[:], S[:]),
                             deps=sup + [yevs[-1]], name=mk("scast", c))
                Sbf_ready = scast
                S_ready = sup[-1]

                zsl = p.op("scalar", lambda s: s.activation(
                    tmpZ[:], zbk[:].rearrange("p a b -> p (a b)"), AF.Tanh, scale=0.5),
                    deps=[zrb] + cm, name=mk("zth", c))
                zs2 = p.op("vector", lambda v: v.scalar_tensor_tensor(
                    tmpZ[:], tmpZ[:], 1.0, zbk[:].rearrange("p a b -> p (a b)"),
                    ALU.add, ALU.mult), deps=[zsl], name=mk("zs2", c))
                zsl_hist[c] = zs2
                gmul = p.op("vector", lambda v: v.tensor_mul(yf[:], yf[:], tmpZ[:]),
                            deps=[zs2] + yevs, name=mk("gmul", c))
                gsqo = p.op("scalar", lambda s: s.activation(tmpZ[:], yf[:], AF.Square),
                            deps=[gmul], name=mk("gsq", c))
                rs_last = None
                for m16 in range(16):
                    rs_last = p.op("tensor", (lambda m16: lambda te: te.matmul(
                        pcv[0:1, 0, 0:128], ones128[:], tmpZ[:, m16 * 128:(m16 + 1) * 128],
                        start=(m16 == 0), stop=(m16 == 15)))(m16),
                        deps=([gsqo, silu_ev[16], silu_ev[17]] + cm if m16 == 0 else []),
                        name=mk("rs", c, m16))
                r1 = p.op("vector", lambda v: v.tensor_scalar(
                    stg[0:1, 0:128], pcv[0:1, 0, 0:128], 1.0 / 2048.0, 4e-5, ALU.mult, ALU.add),
                    deps=[rs_last], name=mk("r1", c))
                r2 = p.op("scalar", lambda s: s.activation(stg[0:1, 128:256], stg[0:1, 0:128], AF.Ln),
                          deps=[r1], name=mk("r2", c))
                r3 = p.op("scalar", lambda s: s.activation(stg[0:1, 0:128], stg[0:1, 128:256],
                                                           AF.Exp, scale=-0.5),
                          deps=[r2], name=mk("r3", c))
                rb = p.op("tensor", lambda te: te.matmul(
                    pcv[:, 1, 0:128], onesf[:], stg[0:1, 0:128], start=True, stop=True),
                    deps=[r3, silu_ev[17]] + cm, name=mk("rb", c))
                yt = p.op("vector", (lambda c4: lambda v: v.tensor_tensor(
                    ytil[:, :, (c4 % 2) * 128:(c4 % 2) * 128 + 128],
                    yf[:].rearrange("p (a b) -> p a b", a=16),
                    pcv[:, 1, 0:128].unsqueeze(1).broadcast_to([128, 16, 128]),
                    ALU.mult))(c4), deps=[rb, gmul] + wout_users, name=mk("yt", c))

                chunk_mark = p.marker("vector",
                                      [yt, scast, r1, wso, mt, x2o, zs2, gsqo] + yevs + ctil_ops,
                                      name=mk("cmark", c))

                if c4 % 2 == 1:
                    hf = c4 // 2
                    op_evs = []
                    prev_users = list(wout_users)
                    wout_users = []
                    for kt in range(16):
                        wbuf = kt % 2
                        wdeps = [LD[0]]
                        if kt < 2:
                            wdeps += prev_users
                        else:
                            wdeps.append(wl_mms[wbuf])
                        wl = p.dma("gpsimd", (lambda kt, wbuf: lambda g: g.dma_start(
                            out=wout[:, wbuf],
                            in_=woutT_d[kt * 128:(kt + 1) * 128, :]))(kt, wbuf),
                            deps=wdeps, name=mk("wl", t, hf, kt))
                        nsc = p.op("vector", (lambda kt, wbuf: lambda v: v.tensor_scalar_mul(
                            wout[:, wbuf], wout[:, wbuf],
                            nw[:, kt:kt + 1]))(kt, wbuf), deps=[wl], name=mk("nsc", t, hf, kt))
                        for m in range(8):
                            st_ = (kt == 0)
                            sp_ = (kt == 15)
                            kdeps = []
                            if m == 0:
                                kdeps = [nsc, yt, chunk_mark]
                                if kt == 0:
                                    kdeps += op_free
                            lastk = p.op("tensor", (lambda m, kt, wbuf, st_, sp_: lambda te: te.matmul(
                                pbanks[m][:, 0:256],
                                wout[:, wbuf, m * 128:(m + 1) * 128],
                                ytil[:, kt, :],
                                start=st_, stop=sp_))(m, kt, wbuf, st_, sp_),
                                deps=kdeps, name=mk("op", t, hf, kt, m))
                            if kt == 15:
                                wout_users.append(lastk)
                                ev = p.op("vector", (lambda m, hf: lambda v: v.scalar_tensor_tensor(
                                    uln[:, m, hf * 256:(hf + 1) * 256], pbanks[m][:, 0:256], 1.0,
                                    xb[:, m, hf * 256:(hf + 1) * 256],
                                    ALU.mult, ALU.add))(m, hf), deps=[lastk] + uln_free,
                                    name=mk("ope", t, hf, m))
                                op_evs.append(ev)
                        wl_mms[wbuf] = lastk
                    xb_users[t].extend(op_evs)
                    op_free = op_evs
                    chunk_mark = p.marker("vector", [chunk_mark] + op_evs, name=mk("omark", t, hf))
                    if hf == 0:
                        uln_half0 = op_evs
                    else:
                        uln_free = []
                        lns_last = None
                        lnq_last = None
                        for m in range(8):
                            sq_ = p.op("scalar", (lambda m: lambda s: s.activation(
                                tmpZ[:, 0:512], uln[:, m, :], AF.Square))(m),
                                deps=[op_evs[m], uln_half0[m]] + ([lnq_last] if lnq_last else []),
                                name=mk("usq", t, m))
                            lns_last = p.op("tensor", (lambda m: lambda te: te.matmul(
                                pcv[0:1, 0, 0:512], ones128[:], uln[:, m, :],
                                start=(m == 0), stop=(m == 7)))(m),
                                deps=[op_evs[m], uln_half0[m]] + (op_evs if m == 0 else []),
                                name=mk("lns", t, m))
                            lnq_last = p.op("tensor", (lambda m: lambda te: te.matmul(
                                pcv[0:1, 1, 0:512], ones128[:], tmpZ[:, 0:512],
                                start=(m == 0), stop=(m == 7)))(m), deps=[sq_], name=mk("lnq", t, m))
                        l1 = p.op("vector", lambda v: v.tensor_scalar_mul(lrow[:], pcv[0:1, 0, 0:512], 1.0 / 1024.0),
                                  deps=[lns_last], name=mk("l1", t))
                        l2 = p.op("vector", lambda v: v.tensor_scalar_mul(lrow2[:], pcv[0:1, 1, 0:512], 1.0 / 1024.0),
                                  deps=[lnq_last], name=mk("l2", t))
                        l3 = p.op("vector", lambda v: v.tensor_mul(stg[0:1, :], lrow[:], lrow[:]),
                                  deps=[l1], name=mk("l3", t))
                        l4 = p.op("vector", lambda v: v.tensor_scalar_add(
                            lrow2[:], lrow2[:], 1e-5),
                            deps=[l2], name=mk("l4e", t))
                        l4 = p.op("vector", lambda v: v.tensor_sub(lrow2[:], lrow2[:], stg[0:1, :]),
                                  deps=[l4, l3], name=mk("l4", t))
                        l5 = p.op("scalar", lambda s: s.activation(stg[0:1, :], lrow2[:], AF.Ln),
                                  deps=[l4], name=mk("l5", t))
                        l6 = p.op("scalar", lambda s: s.activation(lrow2[:], stg[0:1, :], AF.Exp, scale=-0.5),
                                  deps=[l5], name=mk("l6", t))
                        l7 = p.op("vector", lambda v: v.tensor_mul(lrow[:], lrow[:], lrow2[:]),
                                  deps=[l6], name=mk("l7", t))
                        b1 = p.op("tensor", lambda te: te.matmul(
                            pcv[:, 0, :], onesf[:], lrow2[:], start=True, stop=True),
                            deps=[l7, l1, lns_last], name=mk("b1", t))
                        b2 = p.op("tensor", lambda te: te.matmul(
                            pcv[:, 1, :], onesf[:], lrow[:], start=True, stop=True),
                            deps=[l7, l2, lnq_last], name=mk("b2", t))
                        sts = []
                        prev_st = None
                        for m in range(8):
                            a1 = p.op("vector", (lambda m: lambda v: v.tensor_mul(
                                stg[:], uln[:, m, :], pcv[:, 0, :]))(m),
                                deps=[b1] + ([prev_st] if prev_st else []), name=mk("a1", t, m))
                            a2 = p.op("vector", (lambda m: lambda v: v.tensor_sub(
                                stg[:], stg[:], pcv[:, 1, :]))(m),
                                deps=[a1, b2], name=mk("a2", t, m))
                            x1s = p.dma("sync", (lambda t, m: lambda e: e.dma_start(
                                out=x1_d[m * 128:(m + 1) * 128, t * 512:(t + 1) * 512], in_=stg[:]))(t, m),
                                deps=[a2], name=mk("x1s", t, m))
                            prev_st = x1s
                            sts.append(x1s)
                            uln_free.append(x1s)
                        prev_x1s = sts
                        tile_mark = p.marker("vector", sts + op_evs + [b1, b2], name=mk("tmark", t))
                        chunk_mark = tile_mark
                        ip_evac[(t, 32)] = op_evs[0]
                        ip_evac[(t, 33)] = op_evs[1]
                        cv_evac[(t, 16)] = tile_mark
                        cv_evac[(t, 17)] = tile_mark

        p.op("sync", lambda e: e.nop(), deps=prev_x1s, name="fin")
    return p.phase_wrap(start_ix, gate, name=pre + "_done")


def build_mamba(T=4096):
    nc = bass.Bass("TRN2", target_bir_lowering=False)
    p = Prog(nc)
    dr = declare_dram(nc, _MAMBA_DRAM, T)
    x1_d = nc.dram_tensor("x1", [1024, T], F32, kind="ExternalOutput").ap()
    from contextlib import ExitStack
    ses = ExitStack()
    with ses:
        sems, dma_pools = make_sems(nc, ses)
        emit_mamba(nc, p, dr, x1_d, T)
        with nc.Block() as block:
            p.emit(block, sems, dma_pools)
    return nc


_cache = {}


def launch_mamba(inputs):
    x = np.asarray(inputs["L0_attn_x"])
    w = prep_mamba_weights(inputs)
    if "mamba" not in _cache:
        _cache["mamba"] = build_mamba(4096)
    nc = _cache["mamba"]
    in_maps = []
    for core in range(8):
        b = core % 4
        m = {"x": _f32(x[b].T)}
        m.update(w)
        in_maps.append(m)
    res = run_bass_kernel_spmd(nc, in_maps, core_ids=list(range(8)))
    x1 = np.stack([np.asarray(res.results[b]["x1"]).T for b in range(4)], axis=0)
    return x1


def kernel_unfused(**inputs):
    x1 = launch_mamba(inputs)
    x2 = launch_ffn0(inputs, x1)
    x3 = launch_attn1(inputs, x2)
    x4 = launch_ffn1(inputs, x3)
    return np.ascontiguousarray(x4, dtype=np.float32)


def kernel(**inputs):
    x4 = launch_fused(inputs)
    return np.ascontiguousarray(x4, dtype=np.float32)



def prep_ffn0_weights(inp):
    w = {}
    w["wq"] = _f32(np.asarray(inp["L0_ffn_Wq"]))
    w["bq2"] = _f32(np.asarray(inp["L0_ffn_bq"]).reshape(16, 128).T)
    w["gam"] = _f32(np.asarray(inp["L0_ffn_bn_gamma"]).reshape(2, 128).T)
    w["bet"] = _f32(np.asarray(inp["L0_ffn_bn_beta"]).reshape(2, 128).T)
    w["k0T"] = _f32(np.asarray(inp["L0_ffn_sub_key0"]).T)
    w["k1T"] = _f32(np.asarray(inp["L0_ffn_sub_key1"]).T)
    w["dnT"] = _f32(np.asarray(inp["L0_ffn_expert_down"]).T)
    w["upW"] = _f32(np.asarray(inp["L0_ffn_expert_up"]))
    w["onesf"] = np.ones((1, 128), dtype=np.float32)
    return w


_FFN0_DRAM = (("wq", [1024, 2048], F32), ("bq2", [128, 16], F32), ("gam", [128, 2], F32),
              ("bet", [128, 2], F32), ("k0T", [128, 32], F32), ("k1T", [128, 32], F32),
              ("dnT", [1024, 1024], F32), ("upW", [1024, 1024], F32), ("onesf", [1, 128], F32))


def emit_ffn0(nc, p, dr, x2_d, T=2048, total_samples=131072.0, gate=None, pre="f0",
              xfull=None, iso_d=None):
    NTT = T // 512
    start_ix = len(p.ops)
    wq_d = dr["wq"]
    bq2_d = dr["bq2"]
    gam_d = dr["gam"]
    bet_d = dr["bet"]
    k0_d = dr["k0T"]
    k1_d = dr["k1T"]
    dnT_d = dr["dnT"]
    up_d = dr["upW"]
    onesf_d = dr["onesf"]
    xf_d = dr.get("xf")
    qdr = nc.dram_tensor(pre + "_qdr", [128, 16, T], BF16, kind="Internal").ap()
    gehd = nc.dram_tensor(pre + "_gehd", [128, 8, T], BF16, kind="Internal").ap()
    nrmd = nc.dram_tensor(pre + "_nrmd", [16, T], BF16, kind="Internal").ap()
    ccin = nc.dram_tensor(pre + "_ccin", [128, 4], F32, kind="Internal").ap()
    ccout = nc.dram_tensor(pre + "_ccout", [128, 4], F32, kind="Internal").ap()

    from contextlib import ExitStack
    es = ExitStack()
    es1 = ExitStack()
    sb = lambda name, shape, dt=F32: es.enter_context(nc.sbuf_tensor(pre + name, shape, dt))
    sb1 = lambda name, shape, dt=F32: es1.enter_context(nc.sbuf_tensor(pre + name, shape, dt))
    ps = lambda name, shape, dt=F32: es.enter_context(nc.psum_tensor(pre + name, shape, dt))

    x1b = sb("x1b", [128, 8, T], BF16)
    upb = sb("upb", [128, 8, 1024], BF16)
    gst = sb("gst", [128, 8, 512], BF16)
    ghb = sb("ghb", [128, 8, 512], BF16)
    bq2 = sb("bq2_sb", [128, 16])
    gam = sb("gam_sb", [128, 2])
    bet = sb("bet_sb", [128, 2])
    k0 = sb("k0_sb", [128, 32])
    k1 = sb("k1_sb", [128, 32])
    k0b = sb("k0b", [128, 32], BF16)
    k1b = sb("k1b", [128, 32], BF16)
    onesf = sb("onesf_sb", [1, 128])
    ones128 = sb("ones128", [128, 1], BF16)
    onescf = sb("onescf", [128, 1])
    qst = sb("qst", [128, 16, 512], BF16)
    scr = sb("scr", [128, 512], BF16)
    qs_acc = sb("qs_acc", [128, 16, NTT])
    qq_acc = sb("qq_acc", [128, 16, NTT])
    stats = sb("stats", [128, 4])
    bnsc = sb("bnsc", [128, 2])
    bnbi = sb("bnbi", [128, 2])
    stsc = sb("stsc", [128, 16])
    stg = sb("stg", [128, 512])
    wqb = sb1("wqb", [128, 8, 2048], BF16)
    dnb = sb1("dnb", [128, 8, 1024], BF16)
    pA = ps("pA", [128, 2, 512])
    pB = ps("pB", [128, 2, 512])
    pS = ps("pS", [128, 2, 512])
    pN = ps("pN", [128, 2, 512])
    pbanks_f0 = [pA[:, 0, :], pA[:, 1, :], pS[:, 0, :], pS[:, 1, :],
                 pN[:, 0, :], pN[:, 1, :], pB[:, 0, :], pB[:, 1, :]]

    def mk(*a):
        return "_".join(str(x) for x in a)

    with es:
        LD = []
        if xfull is None:
            LD.append(p.dma("gpsimd", lambda g: g.dma_start(
                out=x1b[:], in_=xf_d.rearrange("(kc p) t -> p kc t", p=128)), name="ld_x"))
        else:
            iso_f0 = sb("iso", [128, 1])
            ldi = p.dma("sync", lambda e: e.dma_start(out=iso_f0[:], in_=iso_d), name="ld_iso0")
            xr = xfull.rearrange("(kc p) t -> p kc t", p=128)
            sels = []
            prev_sel = {}
            for kc in range(8):
                qv = qn[:, (kc % 2) * 8:(kc % 2) * 8 + 8, :].rearrange("p k t -> p (k t)")[:, 0:T]
                la = p.dma("gpsimd", (lambda kc, qv: lambda g: g.dma_start(
                    out=qv, in_=xr[:, kc, 0:T]))(kc, qv),
                    deps=prev_sel.get(kc % 2, []), name=mk("selA", kc))
                lb = p.dma("gpsimd", (lambda kc: lambda g: g.dma_start(
                    out=x1b[:, kc, :], in_=xr[:, kc, T:2 * T]))(kc), name=mk("selB", kc))
                s1 = p.op("vector", (lambda kc, qv: lambda v: v.tensor_sub(
                    x1b[:, kc, :], x1b[:, kc, :], qv))(kc, qv), deps=[la, lb], name=mk("sel1", kc))
                s2 = p.op("vector", (lambda kc, qv: lambda v: v.scalar_tensor_tensor(
                    x1b[:, kc, :], x1b[:, kc, :], iso_f0[:], qv, ALU.mult, ALU.add))(kc, qv),
                    deps=[s1, ldi], name=mk("sel2", kc))
                prev_sel[kc % 2] = [s2]
                sels.append(s2)
            LD.append(p.marker("vector", sels, name="ld_x"))
        for k in range(8):
            LD.append(p.dma("gpsimd", (lambda k: lambda g: g.dma_start(
                out=wqb[:, k, :], in_=wq_d[k * 128:(k + 1) * 128, :]))(k), name=f"ld_wq{k}"))
        LD.append(p.dma("gpsimd", lambda g: g.dma_start(
            out=dnb[:], in_=dnT_d.rearrange("(k p) e -> p k e", p=128)), name="ld_dn"))
        LD.append(p.dma("gpsimd", lambda g: g.dma_start(
            out=upb[:], in_=up_d.rearrange("(k p) e -> p k e", p=128)), name="ld_up"))
        for nm, s_, d_ in (("bq2", bq2, bq2_d), ("gam", gam, gam_d), ("bet", bet, bet_d),
                           ("k0", k0, k0_d), ("k1", k1, k1_d), ("onesf", onesf, onesf_d)):
            LD.append(p.dma("sync", (lambda s_, d_: lambda e: e.dma_start(out=s_[:], in_=d_))(s_, d_),
                            name="ld_" + nm))
        o128 = p.op("vector", lambda v: v.memset(ones128[:], 1.0), name="o128")
        ocf = p.op("vector", lambda v: v.memset(onescf[:], 1.0), name="ocf")
        pn0 = p.op("vector", lambda v: v.memset(pN[:], 0.0), name="pn0")

        kh_ops = []
        for kk, (ksb, kbb) in enumerate(((k0, k0b), (k1, k1b))):
            sq = p.op("scalar", (lambda ksb: lambda s: s.activation(stg[:, 0:32], ksb[:], AF.Square))(ksb),
                      deps=LD + kh_ops, name=mk("ksq", kk))
            mm = p.op("tensor", lambda te: te.matmul(pN[0:1, 0, 0:32], onescf[:], stg[:, 0:32],
                      start=True, stop=True), deps=[sq, ocf, pn0], name=mk("knm", kk))
            e1 = p.op("vector", lambda v: v.tensor_single_scalar(stg[0:1, 32:64], pN[0:1, 0, 0:32], 1e-30, ALU.max),
                      deps=[mm], name=mk("kmx", kk))
            e2 = p.op("scalar", lambda s: s.activation(stg[0:1, 64:96], stg[0:1, 32:64], AF.Ln),
                      deps=[e1], name=mk("kln", kk))
            e3 = p.op("scalar", lambda s: s.activation(stg[0:1, 32:64], stg[0:1, 64:96], AF.Exp, scale=-0.5),
                      deps=[e2], name=mk("kex", kk))
            bmm = p.op("tensor", lambda te: te.matmul(pN[:, 1, 0:32], onesf[:], stg[0:1, 32:64],
                       start=True, stop=True), deps=[e3], name=mk("kbc", kk))
            sc = p.op("vector", (lambda ksb, kbb: lambda v: v.tensor_mul(kbb[:], ksb[:], pN[:, 1, 0:32]))(ksb, kbb),
                      deps=[bmm], name=mk("ksc", kk))
            kh_ops.append(sc)

        q_dma = {}
        qst_free = []
        prev_ipe = {}
        prev_sqa = None
        geh_done = {}
        for tt in range(NTT):
            for m in range(16):
                bank = m % 2
                mdeps = LD[:9] if tt == 0 and m < 2 else []
                key = m - 2 if m >= 2 else None
                if key is not None and (tt, key) in prev_ipe:
                    mdeps = [prev_ipe[(tt, key)]]
                elif m < 2 and (tt - 1, 14 + m) in prev_ipe:
                    mdeps = [prev_ipe[(tt - 1, 14 + m)]]
                lastmm = None
                for k in range(8):
                    lastmm = p.op("tensor", (lambda tt, m, k, bank: lambda te: te.matmul(
                        pA[:, bank, :], wqb[:, k, m * 128:(m + 1) * 128],
                        x1b[:, k, tt * 512:(tt + 1) * 512], start=(k == 0), stop=(k == 7)))(tt, m, k, bank),
                        deps=(mdeps + LD[0:1]) if k == 0 else [], name=mk("qm", tt, m, k))
                ev = p.op("scalar", (lambda tt, m, bank: lambda s: s.activation(
                    qst[:, m, :], pA[:, bank, :], AF.Identity, bias=bq2[:, m:m + 1],
                    accum_out=qs_acc[:, m, tt:tt + 1]))(tt, m, bank),
                    deps=[lastmm] + ([q_dma[tt - 1]] if (tt - 1) in q_dma else []), name=mk("qe", tt, m))
                prev_ipe[(tt, m)] = ev
                sqa = p.op("vector", (lambda tt, m: lambda v: v.scalar_tensor_tensor(
                    scr[:], qst[:, m, :], 1.0, qst[:, m, :], ALU.mult, ALU.mult,
                    accum_out=qq_acc[:, m, tt:tt + 1]))(tt, m),
                    deps=[ev] + ([prev_sqa] if prev_sqa is not None else []), name=mk("qq", tt, m))
                prev_sqa = sqa
                prev_ipe[(tt, m)] = sqa
            q_dma[tt] = p.dma("sync", (lambda tt: lambda e: e.dma_start(
                out=qdr[:, :, tt * 512:(tt + 1) * 512], in_=qst[:]))(tt),
                deps=[prev_ipe[(tt, m)] for m in range(16)], name=mk("qpark", tt))

        gelu_evs = []
        prev_de = {}
        g_dma = {}
        for tt in range(NTT):
            for m8 in range(8):
                bank = m8 % 2
                ddeps = []
                key = (tt, m8 - 2) if m8 >= 2 else (tt - 1, 6 + m8)
                if key in prev_de:
                    ddeps.append(prev_de[key])
                if tt == 0 and m8 < 2:
                    ddeps += [LD[0], LD[9]]
                lastmm = None
                for k in range(8):
                    lastmm = p.op("tensor", (lambda tt, m8, k, bank: lambda te: te.matmul(
                        pB[:, bank, :], dnb[:, k, m8 * 128:(m8 + 1) * 128],
                        x1b[:, k, tt * 512:(tt + 1) * 512], start=(k == 0), stop=(k == 7)))(tt, m8, k, bank),
                        deps=ddeps if k == 0 else [], name=mk("dm", tt, m8, k))
                ev = p.op("scalar", (lambda tt, m8, bank: lambda s: s.activation(
                    gst[:, m8, :], pB[:, bank, :], AF.Gelu))(tt, m8, bank),
                    deps=[lastmm] + ([g_dma[tt - 1]] if (tt - 1) in g_dma else []), name=mk("ge", tt, m8))
                prev_de[(tt, m8)] = ev
                gelu_evs.append(ev)
            g_dma[tt] = p.dma("sync", (lambda tt: lambda e: e.dma_start(
                out=gehd[:, :, tt * 512:(tt + 1) * 512], in_=gst[:]))(tt),
                deps=[prev_de[(tt, m8b_)] for m8b_ in range(8)], name=mk("gpark", tt))

        all_acc = [prev_ipe[(tt2, m2)] for tt2 in range(NTT) for m2 in range(16)]
        rops = []
        for i, (acc, par) in enumerate(((qs_acc, 0), (qs_acc, 1), (qq_acc, 0), (qq_acc, 1))):
            ra = p.op("vector", (lambda acc, par, i: lambda v: v.tensor_reduce(
                stsc[:, i * NTT:(i + 1) * NTT].unsqueeze(2), acc[:, par::2, :].transpose([0, 2, 1]),
                AX.X, ALU.add))(acc, par, i), deps=all_acc, name=f"sta{i}")
            rb = p.op("vector", (lambda i: lambda v: v.tensor_reduce(
                stats[:, i:i + 1].unsqueeze(2), stsc[:, i * NTT:(i + 1) * NTT].unsqueeze(1),
                AX.X, ALU.add))(i), deps=[ra], name=f"stb{i}")
            rops.append(rb)
        r1, r2, r3, r4 = rops
        cpy = p.dma("gpsimd", lambda g: g.dma_start(out=ccin, in_=stats[:]),
                    deps=[r1, r2, r3, r4], name="cc_out")
        cc = p.cc(lambda g: g.collective_compute(
            "AllReduce", ALU.add, replica_groups=[[0, 1, 2, 3, 4, 5, 6, 7]],
            ins=[ccin.opt()], outs=[ccout.opt()]), deps=[cpy], name="allreduce")
        cbk = p.dma("gpsimd", lambda g: g.dma_start(out=stats[:], in_=ccout), deps=[cc], name="cc_in")
        b1_ = p.op("vector", lambda v: v.tensor_scalar_mul(stats[:, 0:2], stats[:, 0:2], 1.0 / total_samples),
                   deps=[cbk], name="bn1")
        b2_ = p.op("vector", lambda v: v.tensor_scalar_mul(stats[:, 2:4], stats[:, 2:4], 1.0 / total_samples),
                   deps=[cbk], name="bn2")
        b3_ = p.op("vector", lambda v: v.tensor_mul(bnsc[:], stats[:, 0:2], stats[:, 0:2]),
                   deps=[b1_], name="bn3")
        b4_ = p.op("vector", lambda v: v.tensor_sub(bnsc[:], stats[:, 2:4], bnsc[:]),
                   deps=[b2_, b3_], name="bn4")
        b5_ = p.op("vector", lambda v: v.tensor_scalar_add(bnsc[:], bnsc[:], 1e-5), deps=[b4_], name="bn5")
        b6_ = p.op("scalar", lambda s: s.activation(bnbi[:], bnsc[:], AF.Ln), deps=[b5_], name="bn6")
        b7_ = p.op("scalar", lambda s: s.activation(bnsc[:], bnbi[:], AF.Exp, scale=-0.5), deps=[b6_], name="bn7")
        b8_ = p.op("vector", lambda v: v.tensor_mul(bnsc[:], bnsc[:], gam[:]), deps=[b7_] + LD, name="bn8")
        b9_ = p.op("vector", lambda v: v.tensor_mul(bnbi[:], stats[:, 0:2], bnsc[:]), deps=[b8_, b1_], name="bn9")
        b10 = p.op("vector", lambda v: v.tensor_sub(bnbi[:], bet[:], bnbi[:]), deps=[b9_], name="bn10")

        es1.close()
        qn = sb("qn", [128, 16, 512], BF16)
        nrm_bf = sb("nrm_bf", [128, 2, 2, 512], BF16)
        nrmT = sb("nrmT", [128, 2, 16], BF16)
        invn = sb("invn", [128, 2, 16])
        un = sb("un", [128, 2, 1024])
        wtmb = sb("wtmb", [128, 2, 1024])
        m8b = sb("m8b", [128, 2, 2, 8, 8])
        kept = sb("kept", [128, 2, 2, 256], BF16)
        ssum = sb("ssum", [128, 2, 16])
        pbf = sb("pbf", [128, 2, 2, 256], BF16)
        Fb = sb("Fb", [128, 8192], BF16)
        Wtb = sb("Wtb", [128, 1024], BF16)
        Wfm = sb("Wfm", [128, 8, 512], BF16)
        rhsb = sb("rhsb", [128, 8, 512], BF16)
        uln = sb("uln", [128, 8, 512], BF16)
        usq = sb("usq", [128, 512], BF16)
        lrow = sb("lrow", [1, 512])
        lrow2 = sb("lrow2", [1, 512])
        s_sb_ = [un[:, par, 0:512].rearrange("p (s a) -> p s a", s=2) for par in range(2)]
        sexp_ = [un[:, par, 512:1024].rearrange("p (s a) -> p s a", s=2) for par in range(2)]
        Wtm_ = [wtmb[:, par, :] for par in range(2)]

        prev_q_users = []
        prev_sl_stage = {}
        wfm_users = []
        uln_free = []
        up_users = []
        prev_x2s = []
        ln_free = []
        for tt in range(NTT):
            qbk = p.dma("sync", (lambda tt: lambda e: e.dma_start(
                out=qst[:], in_=qdr[:, :, tt * 512:(tt + 1) * 512]))(tt),
                deps=[q_dma[tt]] + prev_q_users + ([q_dma[NTT - 1]] if tt == 0 else []), name=mk("qbk", tt))
            prev_q_users = []
            bn_ops = []
            for m in range(16):
                o = p.op("vector", (lambda m: lambda v: v.tensor_scalar(
                    qn[:, m, :], qst[:, m, :], bnsc[:, m % 2:m % 2 + 1], bnbi[:, m % 2:m % 2 + 1],
                    ALU.mult, ALU.add))(m), deps=[qbk, b10, b8_], name=mk("bna", tt, m))
                bn_ops.append(o)
                prev_q_users.append(o)
            nrm_copies = []
            for w in range(2):
                wave_mms = []
                for m0 in range(8):
                    m = w * 8 + m0
                    bnk = m0 // 4
                    base = 32 * (m0 % 4)
                    sq = p.op("vector", (lambda m: lambda v: v.scalar_tensor_tensor(
                        scr[:], qn[:, m, :], 1.0, qn[:, m, :], ALU.mult, ALU.mult))(m),
                        deps=[bn_ops[m], prev_sqa] + nrm_copies + wave_mms[-1:], name=mk("nsq", tt, m))
                    prev_sqa = sq
                    mm = p.op("tensor", (lambda bnk, base: lambda te: te.matmul(
                        pN[base:base + 1, bnk, :], ones128[:], scr[:],
                        start=True, stop=True, tile_position=(0, base)))(bnk, base),
                        deps=[sq, pn0], name=mk("nmm", tt, m))
                    wave_mms.append(mm)
                for bnk in range(2):
                    nc_ = p.op("vector", (lambda w, bnk: lambda v: v.tensor_copy(
                        nrm_bf[:, w, bnk, :], pN[:, bnk, :]))(w, bnk),
                        deps=wave_mms, name=mk("nev", tt, w, bnk))
                    nrm_copies.append(nc_)
            ndma = p.dma("sync", (lambda tt: lambda e: e.dma_start(
                out=nrmd[:, tt * 512:(tt + 1) * 512].rearrange("(w b i) t -> i w b t", w=2, b=2),
                in_=nrm_bf[0:128:32, :, :, :]))(tt), deps=nrm_copies, name=mk("ndma", tt))

            for sl in range(4):
                par = (tt * 4 + sl) % 2
                s_sb = s_sb_[par]
                sexp = sexp_[par]
                Wtm = Wtm_[par]
                sdeps = [kh_ops[0], kh_ops[1]]
                smms = []
                for st in range(2):
                    for h in range(8):
                        m = h * 2 + st
                        mm = p.op("tensor", (lambda st, h, m, sl: lambda te: te.matmul(
                            pS[:, st, h * 32:(h + 1) * 32], qn[:, m, sl * 128:(sl + 1) * 128],
                            k0b[:] if st == 0 else k1b[:],
                            start=(h == 0), stop=(h == 7)))(st, h, m, sl),
                            deps=(bn_ops + sdeps + (prev_sl_stage.get(("s", st), []))) if h == 0 else [],
                            name=mk("sc", tt, sl, st, h))
                        smms.append(mm)
                ev0 = p.op("vector", (lambda s_sb: lambda v: v.tensor_copy(
                    s_sb[:, 0, :], pS[:, 0, 0:256]))(s_sb),
                    deps=smms + prev_sl_stage.get(("su", par), []), name=mk("sev0", tt, sl))
                ev1 = p.op("vector", (lambda s_sb: lambda v: v.tensor_copy(
                    s_sb[:, 1, :], pS[:, 1, 0:256]))(s_sb),
                    deps=smms + prev_sl_stage.get(("su", par), []), name=mk("sev1", tt, sl))
                prev_sl_stage[("s", 0)] = [ev0]
                prev_sl_stage[("s", 1)] = [ev1]
                ntr = p.dma("sync", (lambda tt, sl, par: lambda s: s.dma_start_transpose(
                    nrmT[:, par, :], nrmd[:, tt * 512 + sl * 128: tt * 512 + (sl + 1) * 128]))(tt, sl, par),
                    deps=[ndma] + prev_sl_stage.get(("ninv", par), []), name=mk("ntr", tt, sl))
                iv1 = p.op("vector", (lambda par: lambda v: v.tensor_single_scalar(
                    invn[:, par, :], nrmT[:, par, :], 1e-30, ALU.max))(par),
                    deps=[ntr], name=mk("iv1", tt, sl))
                iv2 = p.op("scalar", (lambda par: lambda s: s.activation(
                    invn[:, par, :], invn[:, par, :], AF.Ln))(par),
                    deps=[iv1], name=mk("iv2", tt, sl))
                iv3 = p.op("scalar", (lambda par: lambda s: s.activation(
                    invn[:, par, :], invn[:, par, :], AF.Exp, scale=-0.5))(par),
                    deps=[iv2], name=mk("iv3", tt, sl))
                prev_sl_stage[("ninv", par)] = [iv3]
                ssc = p.op("vector", (lambda s_sb, par: lambda v: v.tensor_tensor(
                    s_sb.rearrange("p s (h a) -> p s h a", h=8),
                    s_sb.rearrange("p s (h a) -> p s h a", h=8),
                    invn[:, par, :].rearrange("p (h s) -> p s h", s=2).unsqueeze(3).broadcast_to([128, 2, 8, 32]),
                    ALU.mult))(s_sb, par), deps=[iv3, ev0, ev1], name=mk("ssc", tt, sl))
                m8ops = []
                for st in range(2):
                    for h in range(8):
                        m8ops.append(p.op("vector", (lambda st, h, s_sb, par: lambda v: v.max(
                            out=m8b[:, par, st, h, :], in_=s_sb[:, st, h * 32:(h + 1) * 32]))(st, h, s_sb, par),
                            deps=[ssc], name=mk("m8", tt, sl, st, h)))
                kp = p.op("vector", (lambda s_sb, par: lambda v: v.tensor_tensor(
                    kept[:, par].rearrange("p s (h a) -> p s h a", h=8),
                    s_sb.rearrange("p s (h a) -> p s h a", h=8),
                    m8b[:, par, :, :, 3:4].broadcast_to([128, 2, 8, 32]), ALU.is_ge))(s_sb, par),
                    deps=m8ops, name=mk("kp", tt, sl))
                ex = p.op("scalar", (lambda s_sb, sexp: lambda s: s.activation(
                    sexp.rearrange("p s a -> p (s a)"), s_sb.rearrange("p s a -> p (s a)"),
                    AF.Exp))(s_sb, sexp), deps=[ssc] + prev_sl_stage.get(("se", par), []),
                    name=mk("ex", tt, sl))
                pe = p.op("vector", (lambda sexp, par: lambda v: v.tensor_mul(
                    sexp.rearrange("p s a -> p (s a)"), sexp.rearrange("p s a -> p (s a)"),
                    kept[:, par].rearrange("p s a -> p (s a)")))(sexp, par),
                    deps=[ex, kp], name=mk("pe", tt, sl))
                sm = p.op("vector", (lambda sexp, par: lambda v: v.tensor_reduce(
                    ssum[:, par, :].rearrange("p (s h) -> p s h", s=2).unsqueeze(3),
                    sexp.rearrange("p s (h a) -> p s h a", h=8), AX.X, ALU.add))(sexp, par),
                    deps=[pe], name=mk("sm", tt, sl))
                rc = p.op("vector", (lambda par: lambda v: v.reciprocal(
                    ssum[:, par, :], ssum[:, par, :]))(par), deps=[sm], name=mk("rc", tt, sl))
                pn = p.op("vector", (lambda sexp, par: lambda v: v.tensor_tensor(
                    pbf[:, par].rearrange("p s (h a) -> p s h a", h=8),
                    sexp.rearrange("p s (h a) -> p s h a", h=8),
                    ssum[:, par, :].rearrange("p (s h) -> p s h", s=2).unsqueeze(3).broadcast_to([128, 2, 8, 32]),
                    ALU.mult))(sexp, par), deps=[rc], name=mk("pn", tt, sl))
                prev_sl_stage[("su", par)] = [kp, ex]
                prev_sl_stage[("se", par)] = [pn]
                fb = p.op("vector", (lambda par: lambda v: v.tensor_tensor(
                    Fb[:].rearrange("p (h a b) -> p h a b", h=8, a=32),
                    pbf[:, par, 0, :].rearrange("p (h a) -> p h a", h=8).unsqueeze(3).broadcast_to([128, 8, 32, 32]),
                    pbf[:, par, 1, :].rearrange("p (h b) -> p h b", h=8).unsqueeze(2).broadcast_to([128, 8, 32, 32]),
                    ALU.mult))(par), deps=[pn] + prev_sl_stage.get("fb", []), name=mk("fb", tt, sl))
                hr = p.op("vector", (lambda Wtm: lambda v: v.tensor_reduce(
                    Wtm.unsqueeze(2),
                    Fb[:].rearrange("p (h e) -> p h e", h=8).transpose([0, 2, 1]), AX.X, ALU.add))(Wtm),
                    deps=[fb] + prev_sl_stage.get(("wtm", par), []), name=mk("hr", tt, sl))
                prev_sl_stage["fb"] = [hr]
                wc = p.op("vector", (lambda Wtm: lambda v: v.tensor_copy(Wtb[:], Wtm))(Wtm),
                          deps=[hr] + prev_sl_stage.get("wc", []), name=mk("wc", tt, sl))
                prev_sl_stage[("wtm", par)] = [wc]
                trs = []
                for eb in range(8):
                    tq = "scalar" if eb % 2 == 0 else "sync"
                    trs.append(p.dma(tq, (lambda eb, sl: lambda s: s.dma_start_transpose(
                        Wfm[:, eb, sl * 128:(sl + 1) * 128], Wtb[:, eb * 128:(eb + 1) * 128]))(eb, sl),
                        deps=[wc] + prev_sl_stage.get("wc", []) + wfm_users, name=mk("wtr", tt, sl, eb)))
                wfm_users = []
                prev_sl_stage["wc"] = trs
                wfm_tt = trs if sl == 0 else wfm_tt + trs

            gbk = p.dma("sync", (lambda tt: lambda e: e.dma_start(
                out=ghb[:], in_=gehd[:, :, tt * 512:(tt + 1) * 512]))(tt),
                deps=[g_dma[tt]] + up_users, name=mk("gbk", tt))
            rhs_ops = []
            for eb in range(8):
                o = p.op("vector", (lambda eb: lambda v: v.tensor_mul(
                    rhsb[:, eb, :], Wfm[:, eb, :], ghb[:, eb, :]))(eb),
                    deps=wfm_tt + [gbk] + up_users, name=mk("rhs", tt, eb))
                rhs_ops.append(o)
            up_users = []
            wfm_users = rhs_ops
            op_evs = []
            for m in range(8):
                lastk = None
                for k in range(8):
                    kdeps = []
                    if k == 0:
                        kdeps = [rhs_ops[m2] for m2 in range(8)] + LD[10:11] + ln_free
                    lastk = p.op("tensor", (lambda m, k: lambda te: te.matmul(
                        pbanks_f0[m],
                        upb[:, k, m * 128:(m + 1) * 128], rhsb[:, k, :],
                        start=(k == 0), stop=(k == 7)))(m, k),
                        deps=kdeps, name=mk("up", tt, m, k))
                up_users.append(lastk)
                ev = p.op("vector", (lambda m, tt: lambda v: v.scalar_tensor_tensor(
                    uln[:, m, :], pbanks_f0[m], 1.0, x1b[:, m, tt * 512:(tt + 1) * 512],
                    ALU.mult, ALU.add))(m, tt), deps=[lastk] + uln_free, name=mk("upe", tt, m))
                op_evs.append(ev)
            uln_free = []
            lns_last = None
            lnq_last = None
            for m in range(8):
                sq_ = p.op("scalar", (lambda m: lambda s: s.activation(
                    usq[:], uln[:, m, :], AF.Square))(m),
                    deps=[op_evs[m]] + ([lnq_last] if lnq_last else []), name=mk("usq", tt, m))
                lns_last = p.op("tensor", (lambda m: lambda te: te.matmul(
                    pB[0:1, 0, 0:512], ones128[:], uln[:, m, :],
                    start=(m == 0), stop=(m == 7)))(m),
                    deps=[op_evs[m]] + (op_evs + [gelu_evs[-1]] if m == 0 else []), name=mk("lns", tt, m))
                lnq_last = p.op("tensor", (lambda m: lambda te: te.matmul(
                    pB[0:1, 1, 0:512], ones128[:], usq[:],
                    start=(m == 0), stop=(m == 7)))(m), deps=[sq_], name=mk("lnq", tt, m))
            l1 = p.op("vector", lambda v: v.tensor_scalar_mul(lrow[:], pB[0:1, 0, 0:512], 1.0 / 1024.0),
                      deps=[lns_last], name=mk("l1", tt))
            l2 = p.op("vector", lambda v: v.tensor_scalar_mul(lrow2[:], pB[0:1, 1, 0:512], 1.0 / 1024.0),
                      deps=[lnq_last], name=mk("l2", tt))
            l3 = p.op("vector", lambda v: v.tensor_mul(stg[0:1, :], lrow[:], lrow[:]),
                      deps=[l1], name=mk("l3", tt))
            l4e = p.op("vector", lambda v: v.tensor_scalar_add(lrow2[:], lrow2[:], 1e-5),
                       deps=[l2], name=mk("l4e", tt))
            l4 = p.op("vector", lambda v: v.tensor_sub(lrow2[:], lrow2[:], stg[0:1, :]),
                      deps=[l4e, l3], name=mk("l4", tt))
            l5 = p.op("scalar", lambda s: s.activation(stg[0:1, :], lrow2[:], AF.Ln),
                      deps=[l4], name=mk("l5", tt))
            l6 = p.op("scalar", lambda s: s.activation(lrow2[:], stg[0:1, :], AF.Exp, scale=-0.5),
                      deps=[l5], name=mk("l6", tt))
            l7 = p.op("vector", lambda v: v.tensor_mul(lrow[:], lrow[:], lrow2[:]),
                      deps=[l6], name=mk("l7", tt))
            bb1 = p.op("tensor", lambda te: te.matmul(
                pB[:, 0, :], onesf[:], lrow2[:], start=True, stop=True),
                deps=[l7, l1, lns_last], name=mk("bb1", tt))
            bb2 = p.op("tensor", lambda te: te.matmul(
                pB[:, 1, :], onesf[:], lrow[:], start=True, stop=True),
                deps=[l7, l2, lnq_last], name=mk("bb2", tt))
            sts = []
            prev_st = None
            for m in range(8):
                a1 = p.op("vector", (lambda m: lambda v: v.tensor_mul(
                    stg[:], uln[:, m, :], pB[:, 0, :]))(m),
                    deps=[bb1] + ([prev_st] if prev_st else []), name=mk("a1", tt, m))
                a2 = p.op("vector", (lambda m: lambda v: v.tensor_sub(
                    stg[:], stg[:], pB[:, 1, :]))(m),
                    deps=[a1, bb2], name=mk("a2", tt, m))
                x2s = p.dma("sync", (lambda tt, m: lambda e: e.dma_start(
                    out=x2_d[m * 128:(m + 1) * 128, tt * 512:(tt + 1) * 512], in_=stg[:]))(tt, m),
                    deps=[a2], name=mk("x2s", tt, m))
                prev_st = x2s
                sts.append(x2s)
                uln_free.append(x2s)
            prev_x2s = sts
            ln_free = [bb1, bb2] + sts

        p.op("sync", lambda e: e.nop(), deps=prev_x2s, name="fin")
    return p.phase_wrap(start_ix, gate, name=pre + "_done")


def build_ffn0(T=2048, total_samples=131072.0):
    nc = bass.Bass("TRN2", target_bir_lowering=False)
    p = Prog(nc)
    dr = declare_dram(nc, _FFN0_DRAM, T)
    dr["xf"] = nc.dram_tensor("xf", [1024, T], F32, kind="ExternalInput").ap()
    x2_d = nc.dram_tensor("x2", [1024, T], F32, kind="ExternalOutput").ap()
    from contextlib import ExitStack
    ses = ExitStack()
    with ses:
        sems, dma_pools = make_sems(nc, ses)
        emit_ffn0(nc, p, dr, x2_d, T, total_samples)
        with nc.Block() as block:
            p.emit(block, sems, dma_pools)
    return nc


def launch_ffn0(inputs, x1):
    w = prep_ffn0_weights(inputs)
    if "ffn0" not in _cache:
        _cache["ffn0"] = build_ffn0(2048, 131072.0)
    nc = _cache["ffn0"]
    in_maps = []
    for core in range(8):
        b, half = core // 2, core % 2
        m = {"xf": _f32(x1[b, half * 2048:(half + 1) * 2048, :].T)}
        m.update(w)
        in_maps.append(m)
    res = run_bass_kernel_spmd(nc, in_maps, core_ids=list(range(8)))
    x2 = np.stack([np.concatenate([np.asarray(res.results[b * 2 + h]["x2"]).T for h in range(2)], axis=0)
                   for b in range(4)], axis=0)
    return x2



def prep_attn1_weights(inp):
    w = {}
    for nm, key in (("wqT", "L1_attn_W_Q"), ("wkT", "L1_attn_W_K"), ("wvT", "L1_attn_W_V"),
                    ("wpT", "L1_attn_W_Kp"), ("woT", "L1_attn_W_O")):
        w[nm] = _f32(np.asarray(inp[key]))
    w["lga"] = _f32(np.asarray(inp["L1_attn_ln_gamma"]).reshape(8, 128).T)
    w["lgb"] = _f32(np.asarray(inp["L1_attn_ln_beta"]).reshape(8, 128).T)
    mask = np.full((32, 1024), -1e5, dtype=np.float32)
    for t in range(32):
        for j in range(t + 1):
            mask[t, j * 32: j * 32 + t + 1] = 0.0
    w["amask"] = mask
    w["onesf"] = np.ones((1, 128), dtype=np.float32)
    w["idt"] = _bf(np.eye(128, dtype=np.float32))
    return w


_ATTN1_DRAM = (("wqT", [1024, 1024], F32), ("wkT", [1024, 1024], F32), ("wvT", [1024, 1024], F32),
               ("wpT", [1024, 1024], F32), ("woT", [1024, 1024], F32), ("lga", [128, 8], F32),
               ("lgb", [128, 8], F32), ("amask", [32, 1024], F32), ("onesf", [1, 128], F32),
               ("idt", [128, 128], BF16))


def emit_attn1(nc, p, dr, xf_d, iso_d, x3_d, T=2048, gate=None, pre="a1"):
    NTT = T // 512
    start_ix = len(p.ops)
    wd = {nm: dr[nm] for nm in ("wqT", "wkT", "wvT", "wpT", "woT")}
    lga_d = dr["lga"]
    lgb_d = dr["lgb"]
    amask_d = dr["amask"]
    onesf_d = dr["onesf"]
    idt_d = dr["idt"]

    from contextlib import ExitStack
    es = ExitStack()
    sb = lambda name, shape, dt=F32: es.enter_context(nc.sbuf_tensor(pre + name, shape, dt))
    ps = lambda name, shape, dt=F32: es.enter_context(nc.psum_tensor(pre + name, shape, dt))

    x2b = sb("x2b", [128, 8, T], BF16)
    wsb = {nm: sb(nm + "_sb", [128, 8, 1024], BF16) for nm in ("wqT", "wkT", "wvT", "wpT", "woT")}
    lga = sb("lga_sb", [128, 8])
    lgb = sb("lgb_sb", [128, 8])
    iso = sb("iso_sb", [128, 1])
    amask = sb("amask_sb", [32, 1024])
    onesf = sb("onesf_sb", [1, 128])
    ones128 = sb("ones128", [128, 1], BF16)
    onescf = sb("onescf", [128, 1])
    ones64f = sb("ones64f", [128, 64])
    q4 = {nm: sb("q4" + nm, [128, 8, 128]) for nm in "qkpv"}
    qsq = sb("qsq", [128, 128])
    qh = {nm: sb("qh" + nm, [128, 8, 128], BF16) for nm in "qkpv"}
    nrow = sb("nrow", [128, 2, 128])
    tk = sb("tk", [128, 1024], BF16)
    tv = sb("tv", [128, 1024], BF16)
    asb = sb("asb", [32, 1024])
    aex = sb("aex", [32, 1024])
    ars = sb("ars", [32, 1])
    pbt = sb("pbt", [32, 1024], BF16)
    ptr = sb("ptr", [128, 8, 32], BF16)
    utr = sb("utr", [128, 8, 64], BF16)
    zf = sb("zf", [128, 8, 32], BF16)
    ztail = sb("ztail", [128, 8, 32])
    ubf = sb("ubf", [128, 8, 512], BF16)
    w1b = sb("w1b", [128, 8, 512], BF16)
    vbf = sb("vbf", [128, 8, 512], BF16)
    usq = sb("usq", [128, 512], BF16)
    lrow = sb("lrow", [1, 512])
    lrow2 = sb("lrow2", [1, 512])
    stg = sb("stg", [128, 512])

    idt_sb = sb("idt_sb", [128, 128], BF16)
    pA = ps("pA", [128, 2, 512])
    pB = ps("pB", [128, 2, 512])
    pT = ps("pT", [128, 1024], BF16)

    def mk(*a):
        return "_".join(str(x) for x in a)

    with es:
        LD = [p.dma("gpsimd", lambda g: g.dma_start(
            out=x2b[:], in_=xf_d.rearrange("(kc p) t -> p kc t", p=128)), name="ld_x")]
        for nm in ("wqT", "wkT", "wvT", "wpT", "woT"):
            LD.append(p.dma("gpsimd", (lambda nm: lambda g: g.dma_start(
                out=wsb[nm][:], in_=wd[nm].rearrange("(k p) e -> p k e", p=128)))(nm), name="ld_" + nm))
        for nm, s_, d_ in (("lga", lga, lga_d), ("lgb", lgb, lgb_d), ("iso", iso, iso_d),
                           ("amask", amask, amask_d), ("onesf", onesf, onesf_d),
                           ("idt", idt_sb, idt_d)):
            LD.append(p.dma("sync", (lambda s_, d_: lambda e: e.dma_start(out=s_[:], in_=d_))(s_, d_),
                            name="ld_" + nm))
        o128 = p.op("vector", lambda v: v.memset(ones128[:], 1.0), name="o128")
        ocf = p.op("vector", lambda v: v.memset(onescf[:], 1.0), name="ocf")
        o64 = p.op("vector", lambda v: v.memset(ones64f[:], 1.0), name="o64")
        pb0 = p.op("vector", lambda v: v.memset(pB[:], 0.0), name="pb0")

        mats = (("q", "wqT"), ("k", "wkT"), ("p", "wpT"), ("v", "wvT"))
        ev4 = {}
        prev_ev = None
        for mi, (qn_, wn) in enumerate(mats):
            for m in range(8):
                bank = m % 2
                lastmm = None
                for k in range(8):
                    lastmm = p.op("tensor", (lambda qn_, wn, m, k, bank: lambda te: te.matmul(
                        pA[:, bank, :128], wsb[wn][:, k, m * 128:(m + 1) * 128],
                        x2b[:, k, T - 128:T], start=(k == 0), stop=(k == 7)))(qn_, wn, m, k, bank),
                        deps=(LD + ([prev_ev] if prev_ev else [])) if k == 0 else [], name=mk("pm", qn_, m, k))
                ev = p.op("vector", (lambda qn_, m, bank: lambda v: v.tensor_copy(
                    q4[qn_][:, m, :], pA[:, bank, :128]))(qn_, m, bank),
                    deps=[lastmm], name=mk("pe", qn_, m))
                ev4[(qn_, m)] = ev
                prev_ev = ev

        rows = [(qn_, h) for qn_ in "qkp" for h in range(16)]
        qh_ops = {}
        prev_nmm = None
        prev_nsc = [None, None]
        vcp = p.op("vector", lambda v: v.tensor_copy(
            qh["v"][:].rearrange("p a b -> p (a b)"), q4["v"][:].rearrange("p a b -> p (a b)")),
            deps=[ev4[("v", m)] for m in range(8)], name="vcp")
        prev_sq = None
        for wv in range(6):
            wave_rows = rows[wv * 8: wv * 8 + 8]
            wave_mms = []
            for ri, (qn_, h) in enumerate(wave_rows):
                bnk = ri // 4
                base = 32 * (ri % 4)
                sq = p.op("scalar", (lambda qn_, h: lambda s: s.activation(
                    qsq[:], q4[qn_][:, h // 2, :], AF.Square))(qn_, h),
                    deps=[ev4[(qn_, h // 2)]] + ([prev_sq] if prev_sq else []) +
                         ([prev_nmm] if prev_nmm else []), name=mk("nsq", wv, ri))
                prev_sq = sq
                mm = p.op("tensor", (lambda h, bnk, base: lambda te: te.matmul(
                    pB[base:base + 1, bnk, 0:128], onescf[(h % 2) * 64:(h % 2) * 64 + 64, :],
                    qsq[(h % 2) * 64:(h % 2) * 64 + 64, :],
                    start=True, stop=True, tile_position=((h % 2) * 64, base)))(h, bnk, base),
                    deps=[sq, pb0], name=mk("nmm", wv, ri))
                prev_nmm = mm
                wave_mms.append(mm)
            for bnk in range(2):
                c1 = p.op("vector", (lambda wv, bnk: lambda v: v.tensor_single_scalar(
                    nrow[:, bnk, :], pB[:, bnk, 0:128], 1e-38, ALU.max))(wv, bnk),
                    deps=wave_mms, name=mk("nc1", wv, bnk))
                c2 = p.op("scalar", (lambda bnk: lambda s: s.activation(
                    nrow[:, bnk, :], nrow[:, bnk, :], AF.Ln))(bnk), deps=[c1], name=mk("nc2", wv, bnk))
                c3 = p.op("scalar", (lambda bnk: lambda s: s.activation(
                    nrow[:, bnk, :], nrow[:, bnk, :], AF.Exp, scale=0.5))(bnk),
                    deps=[c2], name=mk("nc3", wv, bnk))
                c4 = p.op("vector", (lambda bnk: lambda v: v.tensor_scalar_add(
                    nrow[:, bnk, :], nrow[:, bnk, :], 1e-7))(bnk), deps=[c3], name=mk("nc4", wv, bnk))
                c5 = p.op("vector", (lambda bnk: lambda v: v.reciprocal(
                    nrow[:, bnk, :], nrow[:, bnk, :]))(bnk), deps=[c4], name=mk("nc5", wv, bnk))
                for ri, (qn_, h) in enumerate(wave_rows):
                    if ri // 4 != bnk:
                        continue
                    base = 32 * (ri % 4)
                    bc = p.op("tensor", (lambda bnk, base: lambda te: te.matmul(
                        pB[64:128, bnk, 128:256], ones64f[base:base + 1, :],
                        nrow[base:base + 1, bnk, :], start=True, stop=True,
                        tile_position=(base, 64)))(bnk, base),
                        deps=[c5, o64] + ([prev_nsc[bnk]] if prev_nsc[bnk] else []), name=mk("nbc", wv, ri))
                    sc = p.op("vector", (lambda qn_, h, bnk: lambda v: v.tensor_mul(
                        qh[qn_][(h % 2) * 64:(h % 2) * 64 + 64, h // 2, :],
                        q4[qn_][(h % 2) * 64:(h % 2) * 64 + 64, h // 2, :],
                        pB[64:128, bnk, 128:256][0:64, :]))(qn_, h, bnk),
                        deps=[bc], name=mk("nsc", wv, ri))
                    prev_nsc[bnk] = sc
                    qh_ops[(qn_, h)] = sc

        z_evs = []
        prev_h = {}
        for h in range(16):
            b64 = (h % 2) * 64
            tl = h // 2
            tkop = p.op("vector", (lambda b64, tl: lambda v: v.tensor_tensor(
                tk[b64:b64 + 64, :].rearrange("p (j k) -> p j k", j=32),
                qh["k"][b64:b64 + 64, tl, 96:128].unsqueeze(2).broadcast_to([64, 32, 32]),
                qh["p"][b64:b64 + 64, tl, 96:128].unsqueeze(1).broadcast_to([64, 32, 32]),
                ALU.mult))(b64, tl),
                deps=[qh_ops[("k", h)], qh_ops[("p", h)]] + prev_h.get("tk", []), name=mk("tk", h))
            tvop = p.op("vector", (lambda b64, tl: lambda v: v.tensor_tensor(
                tv[b64:b64 + 64, :].rearrange("p (j k) -> p j k", j=32),
                qh["v"][b64:b64 + 64, tl, 96:128].unsqueeze(2).broadcast_to([64, 32, 32]),
                qh["v"][b64:b64 + 64, tl, 96:128].unsqueeze(1).broadcast_to([64, 32, 32]),
                ALU.mult))(b64, tl), deps=[vcp] + prev_h.get("tk", []), name=mk("tv", h))
            amms = []
            for half in range(2):
                amms.append(p.op("tensor", (lambda b64, tl, half: lambda te: te.matmul(
                    pA[0:32, half, :], qh["q"][b64:b64 + 64, tl, 96:128],
                    tk[b64:b64 + 64, half * 512:(half + 1) * 512],
                    start=True, stop=True, tile_position=(b64, 0)))(b64, tl, half),
                    deps=[qh_ops[("q", h)], tkop] + prev_h.get("a", []), name=mk("amm", h, half)))
            am = p.op("vector", lambda v: v.tensor_add(
                asb[:, 0:512], pA[0:32, 0, :], amask[:, 0:512]), deps=amms + LD, name=mk("am0", h))
            am2 = p.op("vector", lambda v: v.tensor_add(
                asb[:, 512:1024], pA[0:32, 1, :], amask[:, 512:1024]), deps=amms, name=mk("am1", h))
            exo = p.op("scalar", lambda s: s.activation(
                aex[:], asb[:], AF.Exp, scale=0.125, accum_out=ars[:]),
                deps=[am, am2], name=mk("aex", h))
            rc = p.op("vector", lambda v: v.reciprocal(ars[:], ars[:]), deps=[exo], name=mk("arc", h))
            pno = p.op("vector", lambda v: v.tensor_scalar_mul(pbt[:], aex[:], ars[:]),
                       deps=[rc], name=mk("apn", h))
            prev_h["a"] = [pno]
            trs = []
            tps = []
            for ch in range(8):
                tp1 = p.op("tensor", (lambda ch, b64: lambda te: te.transpose(
                    pT[:, ch * 64:(ch + 1) * 64], tv[b64:b64 + 64, ch * 128:(ch + 1) * 128],
                    idt_sb[b64:b64 + 64, b64:b64 + 64], tile_position=(b64, 0)))(ch, b64),
                    deps=[tvop] + prev_h.get("tre", []), name=mk("ttu", h, ch))
                tps += [tp1]
                ev1 = p.op("vector", (lambda ch: lambda v: v.tensor_copy(
                    utr[:, ch, :], pT[:, ch * 64:(ch + 1) * 64]))(ch),
                    deps=[tp1] + prev_h.get("ztr", []), name=mk("utr", h, ch))
                ev2 = p.dma("sync", (lambda ch: lambda s: s.dma_start_transpose(
                    ptr[:, ch, :], pbt[:, ch * 128:(ch + 1) * 128]))(ch),
                    deps=[pno] + prev_h.get("ztr", []), name=mk("ptr", h, ch))
                trs += [ev1, ev2]
            prev_h["tre"] = trs
            zmm = None
            for ch in range(8):
                zmm = p.op("tensor", (lambda ch: lambda te: te.matmul(
                    pB[0:64, 0, 256:288], utr[:, ch, :], ptr[:, ch, :],
                    start=(ch == 0), stop=(ch == 7)))(ch),
                    deps=(trs + prev_h.get("zf", [])) if ch == 0 else [], name=mk("zmm", h, ch))
            zev = p.op("vector", (lambda b64, tl: lambda v: v.tensor_copy(
                zf[b64:b64 + 64, tl, :], pB[0:64, 0, 256:288]))(b64, tl),
                deps=[zmm], name=mk("zev", h))
            prev_h["zf"] = [zev]
            prev_h["tk"] = [am, am2, tvop, zev] + trs
            prev_h["ztr"] = [zmm]
            z_evs.append(zev)

        zt_ops = []
        for m in range(8):
            bank = m % 2
            lastmm = None
            for k in range(8):
                lastmm = p.op("tensor", (lambda m, k, bank: lambda te: te.matmul(
                    pA[:, bank, 0:32], wsb["woT"][:, k, m * 128:(m + 1) * 128],
                    zf[:, k, :], start=(k == 0), stop=(k == 7)))(m, k, bank),
                    deps=(z_evs + zt_ops) if k == 0 else [], name=mk("wo", m, k))
            ev = p.op("vector", (lambda m, bank: lambda v: v.tensor_scalar_mul(
                ztail[:, m, :], pA[:, bank, 0:32], iso[:]))(m, bank),
                deps=[lastmm, LD[8]], name=mk("zt", m))
            zt_ops.append(ev)

        prev_st = None
        prev_x3 = []
        ln_free = []
        for tt in range(NTT):
            uop = p.op("vector", (lambda tt: lambda v: v.tensor_copy(
                ubf[:], x2b[:, :, tt * 512:(tt + 1) * 512]))(tt),
                deps=[LD[0]] + ln_free, name=mk("u", tt))
            uops = [uop]
            if tt == NTT - 1:
                uz = p.op("vector", lambda v: v.tensor_add(
                    ubf[:, :, 480:512], ubf[:, :, 480:512], ztail[:]),
                    deps=[uop] + zt_ops, name=mk("uz", tt))
                uops.append(uz)
            for phase in range(2):
                src_buf = ubf if phase == 0 else vbf
                lns_last = None
                lnq_last = None
                for m in range(8):
                    sq_ = p.op("scalar", (lambda m, src_buf: lambda s: s.activation(
                        usq[:], src_buf[:, m, :], AF.Square))(m, src_buf),
                        deps=uops + ([lnq_last] if lnq_last else []), name=mk("usq", tt, phase, m))
                    lns_last = p.op("tensor", (lambda m, src_buf: lambda te: te.matmul(
                        pB[0:1, 0, 0:512], ones128[:], src_buf[:, m, :],
                        start=(m == 0), stop=(m == 7)))(m, src_buf),
                        deps=uops + (ln_free if m == 0 else []), name=mk("lns", tt, phase, m))
                    lnq_last = p.op("tensor", (lambda m: lambda te: te.matmul(
                        pB[0:1, 1, 0:512], ones128[:], usq[:],
                        start=(m == 0), stop=(m == 7)))(m), deps=[sq_], name=mk("lnq", tt, phase, m))
                l1 = p.op("vector", (lambda: lambda v: v.tensor_scalar_mul(
                    lrow[:], pB[0:1, 0, 0:512], 1.0 / 1024.0))(), deps=[lns_last], name=mk("l1", tt, phase))
                l2 = p.op("vector", lambda v: v.tensor_scalar_mul(lrow2[:], pB[0:1, 1, 0:512], 1.0 / 1024.0),
                          deps=[lnq_last], name=mk("l2", tt, phase))
                l3 = p.op("vector", lambda v: v.tensor_mul(stg[0:1, :], lrow[:], lrow[:]),
                          deps=[l1], name=mk("l3", tt, phase))
                l4e = p.op("vector", lambda v: v.tensor_scalar_add(lrow2[:], lrow2[:], 1e-5),
                           deps=[l2], name=mk("l4e", tt, phase))
                l4 = p.op("vector", lambda v: v.tensor_sub(lrow2[:], lrow2[:], stg[0:1, :]),
                          deps=[l4e, l3], name=mk("l4", tt, phase))
                l5 = p.op("scalar", lambda s: s.activation(stg[0:1, :], lrow2[:], AF.Ln),
                          deps=[l4], name=mk("l5", tt, phase))
                l6 = p.op("scalar", lambda s: s.activation(lrow2[:], stg[0:1, :], AF.Exp, scale=-0.5),
                          deps=[l5], name=mk("l6", tt, phase))
                l7 = p.op("vector", lambda v: v.tensor_mul(lrow[:], lrow[:], lrow2[:]),
                          deps=[l6], name=mk("l7", tt, phase))
                bb1 = p.op("tensor", lambda te: te.matmul(
                    pB[:, 0, :], onesf[:], lrow2[:], start=True, stop=True),
                    deps=[l7, l1, lns_last], name=mk("bb1", tt, phase))
                bb2 = p.op("tensor", lambda te: te.matmul(
                    pB[:, 1, :], onesf[:], lrow[:], start=True, stop=True),
                    deps=[l7, l2, lnq_last], name=mk("bb2", tt, phase))
                sts = []
                for m in range(8):
                    a1 = p.op("vector", (lambda m, src_buf: lambda v: v.tensor_mul(
                        stg[:], src_buf[:, m, :], pB[:, 0, :]))(m, src_buf),
                        deps=[bb1] + ([prev_st] if prev_st else []), name=mk("a1", tt, phase, m))
                    a2 = p.op("vector", (lambda m: lambda v: v.tensor_sub(
                        stg[:], stg[:], pB[:, 1, :]))(m),
                        deps=[a1, bb2], name=mk("a2", tt, phase, m))
                    if phase == 0:
                        a3 = p.op("vector", (lambda m, tt: lambda v: v.scalar_tensor_tensor(
                            vbf[:, m, :], stg[:], lga[:, m:m + 1],
                            x2b[:, m, tt * 512:(tt + 1) * 512], ALU.mult, ALU.add))(m, tt),
                            deps=[a2], name=mk("a3", tt, m))
                        a4 = p.op("vector", (lambda m: lambda v: v.tensor_scalar_add(
                            vbf[:, m, :], vbf[:, m, :], lgb[:, m:m + 1]))(m),
                            deps=[a3], name=mk("a4", tt, m))
                        prev_st = a4
                        sts.append(a4)
                    else:
                        x3s = p.dma("sync", (lambda tt, m: lambda e: e.dma_start(
                            out=x3_d[m * 128:(m + 1) * 128, tt * 512:(tt + 1) * 512], in_=stg[:]))(tt, m),
                            deps=[a2], name=mk("x3s", tt, m))
                        prev_st = x3s
                        sts.append(x3s)
                if phase == 0:
                    uops = sts
                else:
                    prev_x3 = sts
                    ln_free = sts + [bb1, bb2]

        p.op("sync", lambda e: e.nop(), deps=prev_x3, name="fin")
    return p.phase_wrap(start_ix, gate, name=pre + "_done")


def build_attn1(T=2048):
    nc = bass.Bass("TRN2", target_bir_lowering=False)
    p = Prog(nc)
    dr = declare_dram(nc, _ATTN1_DRAM, T)
    xf_d = nc.dram_tensor("xf", [1024, T], F32, kind="ExternalInput").ap()
    iso_d = nc.dram_tensor("iso", [128, 1], F32, kind="ExternalInput").ap()
    x3_d = nc.dram_tensor("x3", [1024, T], F32, kind="ExternalOutput").ap()
    from contextlib import ExitStack
    ses = ExitStack()
    with ses:
        sems, dma_pools = make_sems(nc, ses)
        emit_attn1(nc, p, dr, xf_d, iso_d, x3_d, T)
        with nc.Block() as block:
            p.emit(block, sems, dma_pools)
    return nc


def launch_attn1(inputs, x2):
    w = prep_attn1_weights(inputs)
    if "attn1" not in _cache:
        _cache["attn1"] = build_attn1(2048)
    nc = _cache["attn1"]
    in_maps = []
    for core in range(8):
        b, half = core // 2, core % 2
        m = {"xf": _f32(x2[b, half * 2048:(half + 1) * 2048, :].T),
             "iso": np.full((128, 1), float(half), dtype=np.float32)}
        m.update(w)
        in_maps.append(m)
    res = run_bass_kernel_spmd(nc, in_maps, core_ids=list(range(8)))
    x3 = np.stack([np.concatenate([np.asarray(res.results[b * 2 + h]["x3"]).T for h in range(2)], axis=0)
                   for b in range(4)], axis=0)
    return x3



def prep_ffn1_weights(inp):
    w = {}
    wc = np.asarray(inp["L1_ffn_Wc"]).reshape(1024, 128)
    w["wc"] = _bf(wc.reshape(8, 128, 128).transpose(1, 0, 2))
    w["bcv"] = _f32(np.asarray(inp["L1_ffn_bc"]).reshape(128, 1))
    w1 = np.asarray(inp["L1_ffn_W1"]).reshape(128, 1024, 32)
    w["w1q"] = _bf(w1.reshape(32, 4, 8, 128, 32).transpose(0, 3, 2, 1, 4).reshape(32, 128, 8, 128))
    w2 = np.asarray(inp["L1_ffn_W2"]).reshape(128, 1024, 32)
    w["w2q"] = _bf(w2.reshape(32, 4, 1024, 32).transpose(0, 1, 3, 2).reshape(32, 128, 1024))
    b1 = np.asarray(inp["L1_ffn_b1"]).reshape(128, 32)
    w["b1t"] = _f32(b1.reshape(32, 4, 32).transpose(1, 2, 0).reshape(128, 32))
    w["b2f"] = _bf(np.asarray(inp["L1_ffn_b2"]).reshape(128, 1024))
    band = np.zeros((128, 4), dtype=np.float32)
    for t in range(128):
        band[t, t // 32] = 1.0
    w["band"] = _bf(band)
    return w


_FFN1_DRAM = (("wc", [128, 8, 128], BF16), ("bcv", [128, 1], F32), ("w1q", [32, 128, 8, 128], BF16),
              ("w2q", [32, 128, 1024], BF16), ("b1t", [128, 32], F32), ("b2f", [128, 1024], BF16),
              ("band", [128, 4], BF16))


def emit_ffn1(nc, p, dr, xf_d, x4_d, T=2048, gate=None, pre="f1"):
    NG = T // 32
    NCK = T // 128
    NTL = T // 512
    start_ix = len(p.ops)
    wc_d = dr["wc"]
    bc_d = dr["bcv"]
    w1q_d = dr["w1q"]
    w2q_d = dr["w2q"]
    b1t_d = dr["b1t"]
    b2f_d = dr["b2f"]
    band_d = dr["band"]
    rdr = nc.dram_tensor(pre + "_rbounce", [NG, 128, 1024], BF16, kind="Internal").ap()

    from contextlib import ExitStack
    es = ExitStack()
    es1 = ExitStack()
    sb = lambda name, shape, dt=F32: es.enter_context(nc.sbuf_tensor(pre + name, shape, dt))
    sb1 = lambda name, shape, dt=F32: es1.enter_context(nc.sbuf_tensor(pre + name, shape, dt))
    ps = lambda name, shape, dt=F32: es.enter_context(nc.psum_tensor(pre + name, shape, dt))

    wc_sb = sb("wc_sb", [128, 8, 128], BF16)
    bc_sb = sb("bc_sb", [128, 1])
    b1t_sb = sb("b1t_sb", [128, 32])
    b2f_sb = sb("b2f_sb", [128, 1024], BF16)
    band_sb = sb("band_sb", [128, 4], BF16)
    perm = sb("perm_sb", [128, T], BF16)
    gmax = sb("gmax", [128, 64])
    inner = sb("inner_sb", [128, 32, 64], BF16)
    rsb = sb("rsb", [128, 2, 1024], BF16)
    ones128 = sb("ones128", [128, 1], BF16)
    onesf = sb("onesf_sb", [1, 128])
    x3b = sb1("x3b", [128, 8, T], BF16)
    ptm = sb1("ptm", [128, 2, 128], BF16)
    bdt = sb1("bdt", [128, 2, 512], BF16)
    xtm = sb1("xtm", [128, 2, 8, 128], BF16)
    ph = sb1("ph", [128, 8, NG * 128], BF16)
    w1s = sb1("w1s", [128, 4, 8, 128], BF16)
    w2s = sb1("w2s", [128, 4, 1024], BF16)

    pS = ps("pS", [128, 8, 512])

    def mk(*a):
        return "_".join(str(x) for x in a)

    with es:
        LDC = [p.dma("gpsimd", lambda g: g.dma_start(
                   out=x3b[:], in_=xf_d.rearrange("(kc p) t -> p kc t", p=128)), name="ldxb"),
               p.dma("gpsimd", lambda g: g.dma_start(out=wc_sb[:], in_=wc_d), name="ldwc"),
               p.dma("gpsimd", lambda g: g.dma_start(out=bc_sb[:], in_=bc_d), name="ldbc"),
               p.dma("gpsimd", lambda g: g.dma_start(out=b1t_sb[:], in_=b1t_d), name="ldb1"),
               p.dma("gpsimd", lambda g: g.dma_start(out=b2f_sb[:], in_=b2f_d), name="ldb2"),
               p.dma("gpsimd", lambda g: g.dma_start(out=band_sb[:], in_=band_d), name="ldband")]
        o128 = p.op("vector", lambda v: v.memset(ones128[:], 1.0), name="o128")
        onf = p.op("vector", lambda v: v.memset(onesf[:], 1.0), name="onf")

        pge = {}
        for tl in range(NTL):
            mm = None
            for kc in range(8):
                mm = p.op("tensor", (lambda tl, kc: lambda te: te.matmul(
                    pS[:, tl % 2, :], wc_sb[:, kc, :], x3b[:, kc, tl * 512:(tl + 1) * 512],
                    start=(kc == 0), stop=(kc == 7)))(tl, kc),
                    deps=(LDC + ([pge[tl - 2]] if tl >= 2 else [])) if kc == 0 else [],
                    name=mk("lg", tl, kc))
            bca = p.op("vector", (lambda tl: lambda v: v.tensor_scalar_add(
                pS[:, tl % 2, :], pS[:, tl % 2, :], bc_sb[:]))(tl), deps=[mm], name=mk("bca", tl))
            gmx = p.op("vector", (lambda tl: lambda v: v.tensor_reduce(
                gmax[:, tl * 16:(tl + 1) * 16].unsqueeze(2),
                pS[:, tl % 2, :].rearrange("p (g t) -> p g t", t=32), AX.X, ALU.max))(tl),
                deps=[bca], name=mk("gmx", tl))
            pge[tl] = p.op("vector", (lambda tl: lambda v: v.tensor_tensor(
                perm[:, tl * 512:(tl + 1) * 512].rearrange("p (g t) -> p g t", t=32),
                pS[:, tl % 2, :].rearrange("p (g t) -> p g t", t=32),
                gmax[:, tl * 16:(tl + 1) * 16].unsqueeze(2).broadcast_to([128, 16, 32]),
                ALU.is_ge))(tl), deps=[gmx], name=mk("pge", tl))
        pev = {}
        prev_ck = {}
        for ck in range(NCK):
            cb = ck % 2
            tq = "scalar" if ck % 2 == 0 else "sync"
            ptr = p.dma(tq, (lambda ck, cb: lambda s: s.dma_start_transpose(
                ptm[:, cb, :], perm[:, ck * 128:(ck + 1) * 128]))(ck, cb),
                deps=[pge[ck // 4]] + prev_ck.get(("bd", cb), []), name=mk("ptr", ck))
            bdo = p.op("vector", (lambda cb: lambda v: v.tensor_tensor(
                bdt[:, cb, :].rearrange("p (g e) -> p g e", e=128),
                ptm[:, cb, :].unsqueeze(1).broadcast_to([128, 4, 128]),
                band_sb[:].unsqueeze(2).broadcast_to([128, 4, 128]),
                ALU.mult))(cb), deps=[ptr, LDC[5]] + prev_ck.get(("pm", cb), []),
                name=mk("bd", ck))
            prev_ck[("bd", cb)] = [bdo]
            pmms = []
            for kc in range(8):
                pb = 2 + ((ck * 8 + kc) % 4)
                xtq = "scalar" if kc % 2 == 0 else "sync"
                xtr = p.dma(xtq, (lambda ck, cb, kc: lambda s: s.dma_start_transpose(
                    xtm[:, cb, kc, :], x3b[:, kc, ck * 128:(ck + 1) * 128]))(ck, cb, kc),
                    deps=[LDC[0]] + prev_ck.get(("pm", cb, kc), []), name=mk("xtr", ck, kc))
                ps_war = pev.get((ck, kc - 4), []) if kc >= 4 else pev.get((ck - 1, kc + 4), [])
                pmm = p.op("tensor", (lambda cb, kc, pb: lambda te: te.matmul(
                    pS[:, pb, :], xtm[:, cb, kc, :], bdt[:, cb, :],
                    start=True, stop=True))(cb, kc, pb),
                    deps=[bdo, xtr] + ps_war, name=mk("pmm", ck, kc))
                pmms.append(pmm)
                prev_ck[("pm", cb, kc)] = [pmm]
                if kc % 2 == 0:
                    ev = p.op("vector", (lambda ck, kc, pb: lambda v: v.tensor_copy(
                        ph[:, kc, ck * 512:(ck + 1) * 512], pS[:, pb, :]))(ck, kc, pb),
                        deps=[pmm], name=mk("pev", ck, kc))
                else:
                    ev = p.op("scalar", (lambda ck, kc, pb: lambda s: s.activation(
                        ph[:, kc, ck * 512:(ck + 1) * 512], pS[:, pb, :], AF.Identity))(ck, kc, pb),
                        deps=[pmm], name=mk("pev", ck, kc))
                pev[(ck, kc)] = [ev]
            prev_ck[("pm", cb)] = pmms
        p_done = p.marker("vector", [e for v in pev.values() for e in v], name="p_done")
        rdmas = []
        prev_q = {}
        for t in range(32):
            tb = t % 4
            lw1 = p.dma("sync", (lambda t, tb: lambda e: e.dma_start(
                out=w1s[:, tb, :, :], in_=w1q_d[t]))(t, tb),
                deps=prev_q.get(("w1mm", tb), []), name=mk("lw1", t))
            lw2 = p.dma("gpsimd", (lambda t, tb: lambda e: e.dma_start(
                out=w2s[:, tb, :], in_=w2q_d[t]))(t, tb),
                deps=prev_q.get(("w2mm", tb), []), name=mk("lw2", t))
            w1mms = []
            for j in range(4):
                mm = None
                for kc in range(8):
                    mm = p.op("tensor", (lambda t, tb, j, kc: lambda te: te.matmul(
                        pS[32 * j:32 * j + 32, 2 + (t % 2), 0:64],
                        w1s[:, tb, kc, 32 * j:32 * j + 32],
                        ph[:, kc, :].rearrange("p (c e) -> p c e", e=128)[:, :, 4 * t + j],
                        start=(kc == 0), stop=(kc == 7), tile_position=(0, 32 * j)))(t, tb, j, kc),
                        deps=([lw1, p_done] + prev_q.get(("rel", t % 2), [])) if kc == 0 else [],
                        name=mk("w1", t, j, kc))
                w1mms.append(mm)
            prev_q[("w1mm", tb)] = w1mms
            rel = p.op("scalar", (lambda t: lambda s: s.activation(
                inner[:, t, :], pS[:, 2 + (t % 2), 0:64], AF.Relu, bias=b1t_sb[:, t:t + 1]))(t),
                deps=w1mms + [LDC[3]], name=mk("rel", t))
            prev_q[("rel", t % 2)] = [rel]
            w2mms = []
            for j in range(4):
                qp = 64 * (j % 2)
                bk = 4 + 2 * (t % 2) if j < 2 else 6 - 2 * (t % 2)
                for n in range(2):
                    w2mms.append(p.op("tensor", (lambda t, tb, j, n, qp, bk: lambda te: te.matmul(
                        pS[qp:qp + 64, bk + n, :],
                        inner[32 * j:32 * j + 32, t, :],
                        w2s[32 * j:32 * j + 32, tb, n * 512:(n + 1) * 512],
                        start=True, stop=True, tile_position=(32 * j, qp)))(t, tb, j, n, qp, bk),
                        deps=[rel, lw2] + prev_q.get(("eva", t % 2, j // 2), []),
                        name=mk("w2", t, j, n)))
            prev_q[("w2mm", tb)] = w2mms
            for pp in range(2):
                bk = 4 + 2 * (t % 2) if pp == 0 else 6 - 2 * (t % 2)
                eng = "scalar" if (t + pp) % 2 == 0 else "vector"
                if eng == "scalar":
                    eva = p.op(eng, (lambda pp, bk: lambda s: s.activation(
                        rsb[:, pp, :], pS[:, bk:bk + 2, :].rearrange("p b f -> p (b f)"),
                        AF.Identity))(pp, bk), deps=w2mms[4 * pp:4 * pp + 4] +
                        prev_q.get(("rdma", pp), []), name=mk("eva", t, pp))
                else:
                    eva = p.op(eng, (lambda pp, bk: lambda v: v.tensor_copy(
                        rsb[:, pp, :], pS[:, bk:bk + 2, :].rearrange("p b f -> p (b f)")))(pp, bk),
                        deps=w2mms[4 * pp:4 * pp + 4] + prev_q.get(("rdma", pp), []),
                        name=mk("eva", t, pp))
                prev_q[("eva", t % 2, pp)] = [eva]
                rd = p.dma("scalar", (lambda t, pp: lambda s: s.dma_start(
                    out=rdr.rearrange("g e d -> e g d")[4 * t + 2 * pp:4 * t + 2 * pp + 2, :, :],
                    in_=rsb[:, pp, :]))(t, pp), deps=[eva], name=mk("rdma", t, pp))
                prev_q[("rdma", pp)] = [rd]
                rdmas.append(rd)
        r_done = p.marker("sync", rdmas, name="r_done")
        es1.close()
        rg = sb("rg", [128, 8, 1024], BF16)
        x3t = sb("x3t", [128, 2, 8, 512])
        ubf = sb("ubf1", [128, 8, 512], BF16)
        usq = sb("usq1", [128, 512], BF16)
        lrow = sb("lrow_f", [1, 512])
        lrow2 = sb("lrow2_f", [1, 512])
        stg = sb("stg_f", [128, 512])
        tile_done = None
        rgls = {}
        sc_by_g = {}
        x3lds = {}
        for tl in range(NTL):
            x3lds[tl] = p.dma("gpsimd", (lambda tl: lambda g: g.dma_start(
                out=x3t[:, tl % 2, :, :],
                in_=xf_d.rearrange("(kc p) t -> p kc t", p=128)[:, :, tl * 512:(tl + 1) * 512]))(tl),
                deps=[r_done] + ([tile_done] if tile_done else []), name=mk("x3ld", tl))
            td_dep = [tile_done] if tile_done else []
            b2m = {}
            for ds in range(8):
                b2m[ds] = p.op("tensor", (lambda tl, ds: lambda te: te.matmul(
                    pS[:, ds, :], b2f_sb[:, ds * 128:(ds + 1) * 128],
                    perm[:, tl * 512:(tl + 1) * 512], start=True, stop=False))(tl, ds),
                    deps=[r_done, LDC[4]] + td_dep, name=mk("b2m", tl, ds))
            last_sc = {}
            for g16 in range(16):
                g = tl * 16 + g16
                gb = g % 8
                gq = "sync" if g % 2 == 0 else "gpsimd"
                rgls[g] = p.dma(gq, (lambda g, gb: lambda e: e.dma_start(
                    out=rg[:, gb, :], in_=rdr[g]))(g, gb),
                    deps=[r_done] + sc_by_g.get(g - 8, []) + td_dep, name=mk("rgl", g))
                scs = []
                for ds in range(8):
                    scs.append(p.op("tensor", (lambda tl, g16, g, gb, ds: lambda te: te.matmul(
                        pS[:, ds, g16 * 32:(g16 + 1) * 32], rg[:, gb, ds * 128:(ds + 1) * 128],
                        perm[:, tl * 512 + g16 * 32: tl * 512 + (g16 + 1) * 32],
                        start=False, stop=(g16 == 15)))(tl, g16, g, gb, ds),
                        deps=[rgls[g], b2m[ds]] + last_sc.get(ds, []), name=mk("sc", g, ds)))
                    last_sc[ds] = [scs[-1]]
                sc_by_g[g] = scs
            ubevs = {}
            for ds in range(8):
                ubevs[ds] = p.op("vector", (lambda tl, ds: lambda v: v.tensor_add(
                    ubf[:, ds, :], pS[:, ds, :], x3t[:, tl % 2, ds, :]))(tl, ds),
                    deps=last_sc[ds] + [x3lds[tl]], name=mk("ubev", tl, ds))
            lns_last = None
            lnq_last = None
            for m in range(8):
                sq_ = p.op("scalar", (lambda m: lambda s: s.activation(
                    usq[:], ubf[:, m, :], AF.Square))(m),
                    deps=[ubevs[m]] + ([lnq_last] if lnq_last else []), name=mk("usq", tl, m))
                lns_last = p.op("tensor", (lambda m: lambda te: te.matmul(
                    pS[0:1, 0, 0:512], ones128[:], ubf[:, m, :],
                    start=(m == 0), stop=(m == 7)))(m),
                    deps=[ubevs[m], ubevs[0], o128], name=mk("lns", tl, m))
                lnq_last = p.op("tensor", (lambda m: lambda te: te.matmul(
                    pS[0:1, 1, 0:512], ones128[:], usq[:],
                    start=(m == 0), stop=(m == 7)))(m), deps=[sq_, ubevs[1]], name=mk("lnq", tl, m))
            l1 = p.op("vector", lambda v: v.tensor_scalar_mul(
                lrow[:], pS[0:1, 0, 0:512], 1.0 / 1024.0), deps=[lns_last], name=mk("l1", tl))
            l2 = p.op("vector", lambda v: v.tensor_scalar_mul(lrow2[:], pS[0:1, 1, 0:512], 1.0 / 1024.0),
                      deps=[lnq_last], name=mk("l2", tl))
            l3 = p.op("vector", lambda v: v.tensor_mul(stg[0:1, :], lrow[:], lrow[:]),
                      deps=[l1], name=mk("l3", tl))
            l4e = p.op("vector", lambda v: v.tensor_scalar_add(lrow2[:], lrow2[:], 1e-5),
                       deps=[l2], name=mk("l4e", tl))
            l4 = p.op("vector", lambda v: v.tensor_sub(lrow2[:], lrow2[:], stg[0:1, :]),
                      deps=[l4e, l3], name=mk("l4", tl))
            l5 = p.op("scalar", lambda s: s.activation(stg[0:1, :], lrow2[:], AF.Ln),
                      deps=[l4], name=mk("l5", tl))
            l6 = p.op("scalar", lambda s: s.activation(lrow2[:], stg[0:1, :], AF.Exp, scale=-0.5),
                      deps=[l5], name=mk("l6", tl))
            l7 = p.op("vector", lambda v: v.tensor_mul(lrow[:], lrow[:], lrow2[:]),
                      deps=[l6], name=mk("l7", tl))
            bb1 = p.op("tensor", lambda te: te.matmul(
                pS[:, 0, :], onesf[:], lrow2[:], start=True, stop=True),
                deps=[l6, l7, l1, onf], name=mk("bb1", tl))
            bb2 = p.op("tensor", lambda te: te.matmul(
                pS[:, 1, :], onesf[:], lrow[:], start=True, stop=True),
                deps=[l7, l2], name=mk("bb2", tl))
            prev_st = None
            outs = []
            for m in range(8):
                a1 = p.op("vector", (lambda m: lambda v: v.tensor_mul(
                    stg[:], ubf[:, m, :], pS[:, 0, :]))(m),
                    deps=[bb1] + ([prev_st] if prev_st else []), name=mk("a1", tl, m))
                a2 = p.op("vector", (lambda m: lambda v: v.tensor_sub(
                    stg[:], stg[:], pS[:, 1, :]))(m), deps=[a1, bb2], name=mk("a2", tl, m))
                x4s = p.dma("sync", (lambda m, tl: lambda e: e.dma_start(
                    out=x4_d[m * 128:(m + 1) * 128, tl * 512:(tl + 1) * 512],
                    in_=stg[:]))(m, tl), deps=[a2], name=mk("x4s", tl, m))
                prev_st = x4s
                outs.append(x4s)
            tile_done = p.marker("vector", outs + [bb1, bb2], name=mk("tdone", tl))
        p.op("sync", lambda e: e.nop(), deps=[tile_done], name="fin")
    return p.phase_wrap(start_ix, gate, name=pre + "_done")


def build_ffn1(T=2048):
    nc = bass.Bass("TRN2", target_bir_lowering=False)
    p = Prog(nc)
    dr = declare_dram(nc, _FFN1_DRAM, T)
    xf_d = nc.dram_tensor("xf", [1024, T], F32, kind="ExternalInput").ap()
    x4_d = nc.dram_tensor("x4", [1024, T], F32, kind="ExternalOutput").ap()
    from contextlib import ExitStack
    ses = ExitStack()
    with ses:
        sems, dma_pools = make_sems(nc, ses)
        emit_ffn1(nc, p, dr, xf_d, x4_d, T)
        with nc.Block() as block:
            p.emit(block, sems, dma_pools)
    return nc


def launch_ffn1(inputs, x3):
    w = prep_ffn1_weights(inputs)
    if "ffn1" not in _cache:
        _cache["ffn1"] = build_ffn1(2048)
    nc = _cache["ffn1"]
    in_maps = []
    for core in range(8):
        b, half = core // 2, core % 2
        m = {"xf": _f32(x3[b, half * 2048:(half + 1) * 2048, :].T)}
        m.update(w)
        in_maps.append(m)
    res = run_bass_kernel_spmd(nc, in_maps, core_ids=list(range(8)))
    x4 = np.stack([np.concatenate([np.asarray(res.results[b * 2 + h]["x4"]).T for h in range(2)], axis=0)
                   for b in range(4)], axis=0)
    return x4



def build_fused():
    nc = bass.Bass("TRN2", target_bir_lowering=False)
    p = Prog(nc)
    drm = declare_dram(nc, tuple(s for s in _MAMBA_DRAM if s[0] != "x"), 2048, pre="m_")
    drm["x"] = nc.dram_tensor("m_x", [1024, 2051], F32, kind="ExternalInput").ap()
    dr0 = declare_dram(nc, _FFN0_DRAM, 2048, pre="f0_")
    dra = declare_dram(nc, _ATTN1_DRAM, 2048, pre="a1_")
    dr1 = declare_dram(nc, _FFN1_DRAM, 2048, pre="f1_")
    iso_d = nc.dram_tensor("iso", [128, 1], F32, kind="ExternalInput").ap()
    x1_d = nc.dram_tensor("x1i", [1024, 2048], F32, kind="Internal").ap()
    x2_d = nc.dram_tensor("x2i", [1024, 2048], F32, kind="Internal").ap()
    x3_d = nc.dram_tensor("x3i", [1024, 2048], F32, kind="Internal").ap()
    x4_d = nc.dram_tensor("x4", [1024, 2048], F32, kind="ExternalOutput").ap()
    dr0["xf"] = x1_d
    from contextlib import ExitStack
    ses = ExitStack()
    with ses:
        sems, dma_pools = make_sems(nc, ses)
        g1 = emit_mamba2(nc, p, drm, x1_d, iso_d, 2048, None, pre="m")
        g2 = emit_ffn0(nc, p, dr0, x2_d, 2048, 131072.0, g1, pre="f0")
        g3 = emit_attn1(nc, p, dra, x2_d, iso_d, x3_d, 2048, g2, pre="a1")
        g4 = emit_ffn1(nc, p, dr1, x3_d, x4_d, 2048, g3, pre="f1")
        p.op("sync", lambda e: e.nop(), deps=[g4], name="endfin")
        with nc.Block() as block:
            p.emit(block, sems, dma_pools)
    return nc


def launch_fused(inputs):
    if "fused" not in _cache:
        _cache["fused"] = build_fused()
    nc = _cache["fused"]
    x = np.asarray(inputs["L0_attn_x"])
    wsets = (("m_", prep_mamba_weights(inputs)), ("f0_", prep_ffn0_weights(inputs)),
             ("a1_", prep_attn1_weights(inputs)), ("f1_", prep_ffn1_weights(inputs)))
    in_maps = []
    for core in range(8):
        b, half = core // 2, core % 2
        m = {}
        for pre, w in wsets:
            for k, v in w.items():
                m[pre + k] = v
        xe = np.zeros((1024, 2051), dtype=np.float32)
        xe[:, 3:] = x[b, half * 2048:(half + 1) * 2048, :].T
        if half == 1:
            xe[:, 0:3] = x[b, 2045:2048, :].T
        m["m_x"] = xe
        m["iso"] = np.full((128, 1), float(half), dtype=np.float32)
        in_maps.append(m)
    res = run_bass_kernel_spmd(nc, in_maps, core_ids=list(range(8)))
    x4 = np.stack([np.concatenate([np.asarray(res.results[b * 2 + h]["x4"]).T for h in range(2)], axis=0)
                   for b in range(4)], axis=0)
    return x4



def emit_mamba2(nc, p, dr, x1_d, iso_d, T=2048, gate=None, pre="m"):
    TT_ = T // 512
    TC = T // 128
    start_ix = len(p.ops)
    x_d = dr["x"]
    winT_d = dr["w_inT"]
    cd_d = dr["cd"]
    convb_d = dr["conv_b"]
    dtb_d = dr["dtb"]
    alog_d = dr["alog"]
    drep_d = dr["drep"]
    nw_d = dr["nw"]
    woutT_d = dr["w_outT"]
    ones2_d = dr["ones2"]
    onesf_d = dr["onesf"]
    esel_d = dr["esel"]
    tri_d = dr["tri"]
    idtb_d = dr["idtb"]
    zdr = nc.dram_tensor(pre + "_zdr", [128, 16, T], BF16, kind="Internal").ap()
    csdr = nc.dram_tensor(pre + "_csdr", [2, 32, T], BF16, kind="Internal").ap()
    yfdr = nc.dram_tensor(pre + "_yfdr", [128, TC, 2048], BF16, kind="Internal").ap()
    ctdr = nc.dram_tensor(pre + "_ctdr", [128, TC, 4096], BF16, kind="Internal").ap()
    ldr = nc.dram_tensor(pre + "_ldr", [128, TC, 2048], BF16, kind="Internal").ap()
    ccin = nc.dram_tensor(pre + "_ccin", [128, 2048], F32, kind="Internal").ap()
    ccout = nc.dram_tensor(pre + "_ccout", [256, 2048], F32, kind="Internal").ap()

    from contextlib import ExitStack
    es = ExitStack()
    es1 = ExitStack()
    sb = lambda name, shape, dt=F32: es.enter_context(nc.sbuf_tensor(pre + name, shape, dt))
    sb1 = lambda name, shape, dt=F32: es1.enter_context(nc.sbuf_tensor(pre + name, shape, dt))
    ps = lambda name, shape, dt=F32: es.enter_context(nc.psum_tensor(pre + name, shape, dt))

    drep = sb("drep_sb", [128, 16])
    nw = sb("nw_sb", [128, 16])
    ones2 = sb("ones2_sb", [2, 128], BF16)
    onesf = sb("onesf_sb", [1, 128])
    ones128 = sb("ones128", [128, 1], BF16)
    iso = sb("iso_sb", [128, 1])
    xb = sb("xb", [128, 8, 512], BF16)
    xBC = sb("xBC", [128, 18, 528], BF16)
    zst = sb("zst", [128, 2, 512], BF16)
    csfl = sb("csfl", [2, 4, 4096], BF16)
    ecst = sb("ecst", [128, 512], BF16)
    dtot_all = sb("dtot", [128, TC, 32], BF16)
    pfxall = sb("pfx", [128, TC, 32])
    ltd = sb("ltd", [128, 4096], BF16)
    wsl = sb("wsl", [128, 32], BF16)
    ctil = sb("ctil", [128, 4096], BF16)
    S = sb("S", [128, 2048])
    Sbf = sb("Sbf", [128, 2048], BF16)
    yf = sb("yf", [128, 2048], BF16)

    w_in = sb1("w_in", [128, 8 * 4384], BF16)
    cd = sb1("cd_sb", [128, 72 * 32], BF16)
    convb = sb1("convb_sb", [128, 18])
    dtb = sb1("dtb_sb", [32, 1])
    alog = sb1("alog_sb", [32, 1])
    negA = sb1("negA", [32, 1])
    esel = sb1("esel_sb", [65, 4096], BF16)
    tri = sb1("tri_sb", [128, 512], BF16)
    idtb = sb1("idtb_sb", [128, 128], BF16)
    zeros32 = sb1("zeros32", [32, 128])
    xh = sb1("xh", [128, 8, 3], BF16)
    halo = sb1("halo", [128, 18, 3], BF16)
    dtraw = sb1("dtraw", [32, 512])
    dts1 = sb1("dts1", [32, 512])
    dts2 = sb1("dts2", [32, 512])
    csf = sb1("csf", [32, 512])
    u2 = sb1("u2", [65, 512], BF16)
    cshl = sb1("cshl", [32, 2, 512], BF16)
    xtm = sb1("xtm", [128, 2048], BF16)
    x2t = sb1("x2t", [128, 2048], BF16)
    btm = sb1("btm", [128, 128], BF16)
    gt = sb1("gt", [128, 128], BF16)

    pip = ps("pip", [128, 2, 512])
    pcv = ps("pcv", [128, 2, 512])
    pdd = ps("pdd", [128, 2, 512])
    psm = ps("psm", [128, 512])
    pec = ps("pec", [128, 512])
    pbanks = [pip[:, 0, :], pip[:, 1, :], pcv[:, 0, :], pcv[:, 1, :],
              pdd[:, 0, :], pdd[:, 1, :], psm[:], pec[:]]

    with es:
        ld = []
        for kc in range(8):
            ld.append(p.dma("gpsimd", (lambda kc: lambda g: g.dma_start(
                out=w_in[:, kc * 4384:(kc + 1) * 4384],
                in_=winT_d[kc * 128:(kc + 1) * 128, :]))(kc), name=f"ld_win{kc}"))
        ld.append(p.dma("sync", lambda e: e.dma_start(
            out=cd[:].rearrange("p (t f) -> p t f", f=32), in_=cd_d), name="ld_cd"))
        for nm, s_, d_ in (("convb", convb, convb_d), ("dtb", dtb, dtb_d), ("alog", alog, alog_d),
                           ("drep", drep, drep_d), ("nw", nw, nw_d), ("ones2", ones2, ones2_d),
                           ("onesf", onesf, onesf_d), ("esel", esel, esel_d), ("tri", tri, tri_d),
                           ("idtb", idtb, idtb_d), ("iso", iso, iso_d)):
            ld.append(p.dma("sync", (lambda s_, d_: lambda e: e.dma_start(out=s_[:], in_=d_))(s_, d_),
                            name="ld_" + nm))
        LD = list(ld)
        init0 = p.op("vector", lambda v: v.memset(zeros32[:], 0.0), name="z32")
        p.op("vector", lambda v: v.memset(ones128[:], 1.0), name="o128")
        initU = p.op("vector", lambda v: v.memset(u2[64:65, :], 1.0), name="u2o")
        initS = p.op("vector", lambda v: v.memset(S[:], 0.0), name="S0")
        initSb = p.op("vector", lambda v: v.memset(Sbf[:], 0.0), name="Sbf0")
        expA = p.op("scalar", lambda s: s.activation(negA[:], alog[:], AF.Exp), deps=LD, name="expA")
        negAo = p.op("vector", lambda v: v.tensor_scalar_mul(negA[:], negA[:], -1.0), deps=[expA], name="negA")

        def mk(*a):
            return "_".join(str(x) for x in a)

        xhld = p.dma("gpsimd", lambda g: g.dma_start(
            out=xh[:], in_=x_d[:, 0:3].rearrange("(kc p) t -> p kc t", p=128)), name="xhld")
        hevs = []
        hev_by_bank = {}
        for ft in range(18):
            m = 16 + ft
            bank = ft % 2
            hdeps = [xhld] + (LD[:8] if ft < 2 else []) + hev_by_bank.get(bank, [])
            lastmm = None
            for k in range(8):
                lastmm = p.op("tensor", (lambda m, k, bank: lambda te: te.matmul(
                    pip[:, bank, 0:3],
                    w_in[:, k * 4384 + m * 128: k * 4384 + m * 128 + 128],
                    xh[:, k, :], start=(k == 0), stop=(k == 7)))(m, k, bank),
                    deps=hdeps if k == 0 else [], name=mk("hip", ft, k))
            hev = p.op("vector", (lambda ft, bank: lambda v: v.tensor_copy(
                xBC[:, ft, 13:16], pip[:, bank, 0:3]))(ft, bank), deps=[lastmm], name=mk("hev", ft))
            hev_by_bank[bank] = [hev]
            hevs.append(hev)
        halo_restore = p.marker("vector", hevs, name="halo_init")

        ip_evac = {(-1, 32): hev_by_bank[0][0], (-1, 33): hev_by_bank[1][0]}
        cv_evac = {}
        xb_users = {}
        zdma = {}
        Sbf_ready = initSb
        S_ready = initS
        chunk_mark = halo_restore
        tile_mark = None
        zsl_hist = {}
        csfl_users = {}
        silu_ev_by_t = {}
        dcp_by_c = {}
        ctp_by_c = {}
        yfps = []
        lparks = []
        bo1s = []
        cpks = {}

        for t in range(TT_):
            xldt = p.dma("gpsimd", (lambda t: lambda g: g.dma_start(
                out=xb[:],
                in_=x_d[:, 3 + t * 512:3 + (t + 1) * 512].rearrange("(kc p) t -> p kc t", p=128)))(t),
                deps=xb_users.get(t - 1, []) + ([xhld] if t == 0 else []), name=mk("xld", t))
            xb_users[t] = []

            for m in range(35):
                bank = m % 2
                rows = 128 if m < 34 else 32
                mdeps = [xldt] + (LD[:8] if t == 0 else [])
                key = (t, m - 2) if m >= 2 else (t - 1, 32 + m)
                if key in ip_evac:
                    mdeps.append(ip_evac[key])
                if m < 2 and tile_mark is not None:
                    mdeps.append(tile_mark)
                lastmm = None
                for k in range(8):
                    lastmm = p.op("tensor", (lambda m, k, bank, rows: lambda te: te.matmul(
                        pip[0:rows, bank, :],
                        w_in[:, k * 4384 + m * 128: k * 4384 + m * 128 + rows],
                        xb[:, k, :], start=(k == 0), stop=(k == 7)))(m, k, bank, rows),
                        deps=mdeps if k == 0 else [], name=mk("ip", t, m, k))
                xb_users[t].append(lastmm)
                if m < 16:
                    g4 = m // 2
                    zkey = (t, g4 - 1) if g4 > 0 else (t - 1, 7)
                    zwar = [zdma[zkey]] if zkey in zdma else []
                    ev = p.op("scalar", (lambda m, bank: lambda s: s.copy(
                        zst[:, m % 2, :], pip[:, bank, :]))(m, bank),
                        deps=[lastmm] + zwar, name=mk("ipe", t, m))
                    ip_evac[(t, m)] = ev
                    if m % 2 == 1:
                        zdma[(t, g4)] = p.dma("sync", (lambda t, g4: lambda e: e.dma_start(
                            out=zdr[:, g4 * 2:(g4 + 1) * 2, t * 512:(t + 1) * 512], in_=zst[:]))(t, g4),
                            deps=[ip_evac[(t, mm)] for mm in range(g4 * 2, g4 * 2 + 2)],
                            name=mk("zd", t, g4))
                elif m < 34:
                    ft = m - 16
                    ev = p.op("vector", (lambda ft, bank: lambda v: v.tensor_copy(
                        xBC[:, ft, 16:528], pip[:, bank, :]))(ft, bank),
                        deps=[lastmm] + ([halo_restore] if halo_restore is not None else []),
                        name=mk("ipe", t, m))
                    ip_evac[(t, m)] = ev
                else:
                    ev = p.op("vector", (lambda bank: lambda v: v.tensor_copy(
                        dtraw[:], pip[0:32, bank, :]))(bank), deps=[lastmm], name=mk("ipe", t, m))
                    ip_evac[(t, m)] = ev

            hs = p.op("vector", lambda v: v.tensor_copy(halo[:], xBC[:, :, 525:528]),
                      deps=[ip_evac[(t, m)] for m in range(16, 34)], name=mk("hsave", t))
            silu_ev = {}
            last_tap = None
            for ft in range(18):
                bank = ft % 2
                cdeps = [ip_evac[(t, 16 + ft)], LD[8]]
                if ft < 2:
                    if halo_restore is not None:
                        cdeps.append(halo_restore)
                    if tile_mark is not None:
                        cdeps.append(tile_mark)
                key = (t, ft - 2)
                if key in cv_evac:
                    cdeps.append(cv_evac[key])
                for tau in range(4):
                    for sub2 in range(4):
                        mmv = p.op("tensor", (lambda ft, tau, sub2, bank: lambda te: te.matmul(
                            pcv[sub2 * 32:(sub2 + 1) * 32, bank, :],
                            cd[sub2 * 32:(sub2 + 1) * 32, (ft * 4 + tau) * 32:(ft * 4 + tau + 1) * 32],
                            xBC[sub2 * 32:(sub2 + 1) * 32, ft, 13 + tau:13 + tau + 512],
                            start=(tau == 0), stop=(tau == 3),
                            tile_position=(sub2 * 32, sub2 * 32)))(ft, tau, sub2, bank),
                            deps=cdeps if (tau == 0 and sub2 == 0) else [], name=mk("cv", t, ft, tau, sub2))
                last_tap = mmv
                ev = p.op("scalar", (lambda ft, bank: lambda s: s.activation(
                    xBC[:, ft, 16:528], pcv[:, bank, :], AF.Silu, bias=convb[:, ft:ft + 1]))(ft, bank),
                    deps=[mmv, hs], name=mk("cse", t, ft))
                cv_evac[(t, ft)] = ev
                silu_ev[ft] = ev
            silu_ev_by_t[t] = silu_ev
            halo_restore = p.op("vector", lambda v: v.tensor_copy(xBC[:, :, 13:16], halo[:]),
                                deps=[last_tap, hs], name=mk("hrest", t))

            dt_ev = ip_evac[(t, 34)]
            o1 = p.op("scalar", lambda s: s.activation(dts1[:], dtraw[:], AF.Abs, bias=dtb[:]),
                      deps=[dt_ev] + (LD if t == 0 else []), name=mk("dtabs", t))
            o2 = p.op("scalar", lambda s: s.activation(dts1[:], dts1[:], AF.Exp, scale=-1.0),
                      deps=[o1], name=mk("dtexp", t))
            o3 = p.op("scalar", lambda s: s.activation(dts1[:], dts1[:], AF.Ln, bias=1.0),
                      deps=[o2], name=mk("dtln1", t))
            o4 = p.op("vector", lambda v: v.tensor_scalar_add(dts2[:], dtraw[:], dtb[:]),
                      deps=[dt_ev], name=mk("dtadd", t))
            o5 = p.op("vector", lambda v: v.tensor_single_scalar(dts2[:], dts2[:], 0.0, ALU.max),
                      deps=[o4], name=mk("dtmax", t))
            o6 = p.op("vector", lambda v: v.tensor_add(dtraw[:], dts1[:], dts2[:]),
                      deps=[o3, o5], name=mk("dt", t))
            o7 = p.op("vector", lambda v: v.tensor_scalar_mul(dts2[:], dtraw[:], negA[:]),
                      deps=[o6, negAo], name=mk("a", t))
            cs_ops = []
            for c4 in range(4):
                cs_ops.append(p.op("vector", (lambda c4: lambda v: v.tensor_tensor_scan(
                    csf[:, c4 * 128:(c4 + 1) * 128], dts2[:, c4 * 128:(c4 + 1) * 128], zeros32[:],
                    0.0, ALU.add, ALU.add))(c4), deps=[o7, init0], name=mk("cs", t, c4)))
            lnd = p.op("scalar", lambda s: s.activation(dts1[:], dtraw[:], AF.Ln),
                       deps=[o6, o3], name=mk("lndt", t))
            uo = p.op("vector", lambda v: v.tensor_sub(dts1[:], csf[:], dts1[:]),
                      deps=[lnd] + cs_ops, name=mk("u", t))
            uhi = p.op("vector", lambda v: v.tensor_copy(u2[0:32, :], dts1[:]),
                       deps=[uo, initU, chunk_mark], name=mk("uhi", t))
            ulo = p.op("vector", lambda v: v.tensor_sub(u2[32:64, :], dts1[:], u2[0:32, :]),
                       deps=[uhi], name=mk("ulo", t))
            chi = p.op("vector", lambda v: v.tensor_copy(cshl[:, 0, :], csf[:]),
                       deps=cs_ops, name=mk("chi", t))
            clo = p.op("vector", lambda v: v.tensor_sub(cshl[:, 1, :], csf[:], cshl[:, 0, :]),
                       deps=[chi], name=mk("clo", t))
            bo1 = p.dma("sync", (lambda t: lambda e: e.dma_start(
                out=csdr[:, :, t * 512:(t + 1) * 512].rearrange("r h t -> h r t"), in_=cshl[:]))(t),
                deps=[chi, clo], name=mk("bo1", t))
            bo1s.append(bo1)

            bifs = {}
            for c4 in range(4):
                bifs[c4] = p.dma("sync", (lambda t, c4: lambda e: e.dma_start(
                    out=csfl[:, c4, :].rearrange("r (h i) -> r h i", h=32),
                    in_=csdr[:, :, t * 512 + c4 * 128: t * 512 + (c4 + 1) * 128]))(t, c4),
                    deps=[bo1] + csfl_users.get(c4, []), name=mk("bif", t, c4))
            csfl_users = {}
            for c4 in range(4):
                c = t * 4 + c4
                col0 = 16 + c4 * 128
                cm = [chunk_mark]
                bif = bifs[c4]
                exps = []
                for q in range(8):
                    bank = q % 2
                    ddeps = [bif, ulo]
                    if q >= 2:
                        ddeps.append(exps[q - 2])
                    else:
                        ddeps += cm
                    m1 = p.op("tensor", (lambda q, bank, c4: lambda te: te.matmul(
                        pdd[:, bank, :], ones2[:], csfl[:, c4, q * 512:(q + 1) * 512],
                        start=True, stop=False))(q, bank, c4), deps=ddeps, name=mk("D1", c, q))
                    csfl_users.setdefault(c4, []).append(m1)
                    p.op("tensor", (lambda c4, q, bank: lambda te: te.matmul(
                        pdd[:, bank, :], u2[:, c4 * 128:(c4 + 1) * 128],
                        esel[:, q * 512:(q + 1) * 512], start=False, stop=False))(c4, q, bank),
                        name=mk("D2", c, q))
                    m3 = p.op("tensor", (lambda bank: lambda te: te.matmul(
                        pdd[:, bank, :], idtb[:], tri[:], start=False, stop=True))(bank),
                        name=mk("D3", c, q))
                    ex = p.op("scalar", (lambda q, bank: lambda s: s.activation(
                        ltd[:, q * 512:(q + 1) * 512], pdd[:, bank, :], AF.Exp))(q, bank),
                        deps=[m3] + cm, name=mk("Dx", c, q))
                    exps.append(ex)
                wso = p.op("vector", lambda v: v.tensor_copy(
                    wsl[:].unsqueeze(2), ltd[:].rearrange("p (h i) -> p h i", h=32)[:, :, 127:128]),
                    deps=exps + cm, name=mk("w", c))
                gmm = p.op("tensor", (lambda col0: lambda te: te.matmul(
                    psm[:, 0:128], xBC[:, 16, col0:col0 + 128], xBC[:, 17, col0:col0 + 128],
                    start=True, stop=True))(col0),
                    deps=[silu_ev[16], silu_ev[17]] + cm, name=mk("gt", c))
                gev = p.op("scalar", lambda s: s.copy(gt[:], psm[:, 0:128]), deps=[gmm], name=mk("gte", c))
                mt = p.op("vector", lambda v: v.tensor_tensor(
                    ltd[:].rearrange("p (h i) -> p h i", h=32),
                    ltd[:].rearrange("p (h i) -> p h i", h=32),
                    gt[:].unsqueeze(1).broadcast_to([128, 32, 128]), ALU.mult),
                    deps=[gev, wso] + exps, name=mk("mt", c))
                dcp_ops = []
                ct_ops = []
                prev_ct = None
                for q in range(8):
                    emm = p.op("tensor", (lambda q, c4: lambda te: te.matmul(
                        pec[:], ones2[:], csfl[:, c4, q * 512:(q + 1) * 512],
                        start=True, stop=True))(q, c4),
                        deps=([bif] + cm if q == 0 else [prev_ct]), name=mk("ecm", c, q))
                    csfl_users.setdefault(c4, []).append(emm)
                    exo = p.op("scalar", (lambda q: lambda s: s.activation(
                        ecst[:], pec[:], AF.Exp))(q), deps=[emm], name=mk("ecx", c, q))
                    dcp = p.op("vector", (lambda q, c: lambda v: v.tensor_copy(
                        dtot_all[:, c, 4 * q:4 * q + 4].unsqueeze(2),
                        ecst[:].rearrange("p (h i) -> p h i", h=4)[:, :, 127:128]))(q, c),
                        deps=[exo], name=mk("dcp", c, q))
                    ct = p.op("vector", (lambda q, col0: lambda v: v.tensor_tensor(
                        ctil[:, q * 512:(q + 1) * 512].rearrange("p (h i) -> p h i", h=4),
                        ecst[:].rearrange("p (h i) -> p h i", h=4),
                        xBC[:, 17, col0:col0 + 128].unsqueeze(1).broadcast_to([128, 4, 128]),
                        ALU.mult))(q, col0),
                        deps=[exo, silu_ev[17]] + ctp_by_c.get(c - 1, []), name=mk("ct1", c, q))
                    dcp_ops.append(dcp)
                    ct_ops.append(ct)
                    prev_ct = p.marker("scalar", [dcp, ct], name=mk("ctm", c, q))
                ctp = p.dma("gpsimd", (lambda c: lambda e: e.dma_start(
                    out=ctdr[:, c, :], in_=ctil[:]))(c),
                    deps=ct_ops, name=mk("ctp", c))
                ctp_by_c[c] = [ctp]
                dcp_by_c[c] = dcp_ops
                btmd = p.dma("sync", (lambda col0: lambda s: s.dma_start_transpose(
                    btm[:], xBC[:, 16, col0:col0 + 128]))(col0),
                    deps=[silu_ev[16]] + cm, name=mk("btm", c))
                xtm_dmas = []
                for ft in range(16):
                    xq = "scalar" if ft % 2 == 0 else "sync"
                    xtm_dmas.append(p.dma(xq, (lambda ft, col0: lambda s: s.dma_start_transpose(
                        xtm[:, ft * 128:(ft + 1) * 128], xBC[:, ft, col0:col0 + 128]))(ft, col0),
                        deps=[silu_ev[ft]] + cm, name=mk("xtm", c, ft)))
                x2o = p.op("vector", lambda v: v.tensor_tensor(
                    x2t[:].rearrange("p (h q) -> p h q", h=32),
                    xtm[:].rearrange("p (h q) -> p h q", h=32),
                    wsl[:].unsqueeze(2).broadcast_to([128, 32, 64]), ALU.mult),
                    deps=[wso] + xtm_dmas, name=mk("x2", c))
                yevs = []
                for hp in range(16):
                    slot = hp % 2
                    c0_ = 128
                    ydeps = [mt, xtm_dmas[hp]]
                    if hp >= 2:
                        ydeps.append(yevs[hp - 2])
                    else:
                        ydeps += cm
                    p.op("tensor", (lambda hp, slot, c0_: lambda te: te.matmul(
                        pdd[0:64, slot, c0_:c0_ + 128], xtm[:, hp * 128:hp * 128 + 64],
                        ltd[:, (2 * hp) * 128:(2 * hp + 1) * 128],
                        start=True, stop=True, tile_position=(0, 0)))(hp, slot, c0_),
                        deps=ydeps, name=mk("ya", c, hp))
                    ymd = p.op("tensor", (lambda hp, slot, c0_: lambda te: te.matmul(
                        pdd[64:128, slot, c0_:c0_ + 128], xtm[:, hp * 128 + 64:hp * 128 + 128],
                        ltd[:, (2 * hp + 1) * 128:(2 * hp + 2) * 128],
                        start=True, stop=True, tile_position=(0, 64)))(hp, slot, c0_),
                        name=mk("yb", c, hp))
                    ev = p.op("vector", (lambda hp, slot, col0, c0_: lambda v: v.scalar_tensor_tensor(
                        yf[:, hp * 128:(hp + 1) * 128], xBC[:, hp, col0:col0 + 128], drep[:, hp:hp + 1],
                        pdd[:, slot, c0_:c0_ + 128], ALU.mult, ALU.add))(hp, slot, col0, c0_),
                        deps=[ymd], name=mk("ye", c, hp))
                    yevs.append(ev)
                yfp = p.dma("sync", (lambda c: lambda e: e.dma_start(
                    out=yfdr[:, c, :], in_=yf[:]))(c), deps=yevs, name=mk("yfp", c))
                yfps.append(yfp)
                lpark = p.dma("sync", (lambda c: lambda e: e.dma_start(
                    out=ldr[:, c, :], in_=Sbf[:]))(c), deps=[Sbf_ready], name=mk("lpark", c))
                lparks.append(lpark)
                sup = []
                for grp in range(4):
                    last = None
                    for hh in range(8):
                        h = grp * 8 + hh
                        gdeps = []
                        if hh == 0:
                            gdeps = [btmd, x2o, prev_ct]
                            if grp > 0:
                                gdeps.append(sup[grp - 1])
                            else:
                                gdeps += cm
                        last = p.op("tensor", (lambda h, hh: lambda te: te.matmul(
                            pec[:, hh * 64:(hh + 1) * 64], btm[:],
                            x2t[:, h * 64:(h + 1) * 64], start=(hh == 0), stop=(hh == 7)))(h, hh),
                            deps=gdeps, name=mk("sl", c, h))
                    s1_ = p.op("vector", (lambda grp, c: lambda v: v.tensor_tensor(
                        S[:, grp * 512:(grp + 1) * 512].rearrange("p (h q) -> p h q", h=8),
                        S[:, grp * 512:(grp + 1) * 512].rearrange("p (h q) -> p h q", h=8),
                        dtot_all[:, c, grp * 8:(grp + 1) * 8].unsqueeze(2).broadcast_to([128, 8, 64]),
                        ALU.mult))(grp, c), deps=[S_ready, last] + dcp_ops, name=mk("su1", c, grp))
                    s2_ = p.op("vector", (lambda grp: lambda v: v.tensor_add(
                        S[:, grp * 512:(grp + 1) * 512], S[:, grp * 512:(grp + 1) * 512],
                        pec[:]))(grp), deps=[s1_], name=mk("su2", c, grp))
                    sup.append(s2_)
                scast = p.op("vector", lambda v: v.tensor_copy(Sbf[:], S[:]),
                             deps=sup + [lpark], name=mk("scast", c))
                Sbf_ready = scast
                S_ready = sup[-1]
                chunk_mark = p.marker("vector", [scast, wso, mt, x2o, yfp] + yevs + dcp_ops,
                                      name=mk("cmark", c))
            tile_mark = p.marker("vector", [chunk_mark], name=mk("tmark1", t))
            chunk_mark = tile_mark
            ip_evac[(t, 32)] = ip_evac[(t, 34)]
            cv_evac[(t, 16)] = tile_mark
            cv_evac[(t, 17)] = tile_mark

        l1done = p.marker("sync", [tile_mark, S_ready, Sbf_ready] + yfps + lparks +
                          [e for v in ctp_by_c.values() for e in v] +
                          list(zdma.values()) + bo1s, name="l1done")

        ccd = p.dma("sync", lambda e: e.dma_start(out=ccin, in_=S[:]), deps=[S_ready], name="ccd")
        ccop = p.cc(lambda g: g.collective_compute(
            "AllGather", ALU.bypass, replica_groups=[[0, 1], [2, 3], [4, 5], [6, 7]],
            ins=[ccin.opt()], outs=[ccout.opt()]), deps=[ccd], name="ccag")
        crb = p.dma("gpsimd", lambda g: g.dma_start(out=S[:], in_=ccout[0:128, :]),
                    deps=[ccop, ccd], name="crb")
        sio = p.op("vector", lambda v: v.tensor_scalar_mul(S[:], S[:], iso[:]),
                   deps=[crb, LD[-1]], name="sio")
        pf0 = p.op("vector", lambda v: v.memset(pfxall[:, 0, :], 1.0), name="pf0")
        pfo = pf0
        for c in range(1, TC):
            pfo = p.op("vector", (lambda c: lambda v: v.tensor_mul(
                pfxall[:, c, :], pfxall[:, c - 1, :], dtot_all[:, c - 1, :]))(c),
                deps=[pfo] + dcp_by_c[c - 1], name=mk("pfx", c))
        pfx_done = pfo

        es1.close()
        ctl2 = sb("ctl2", [128, 2, 4096], BF16)
        zbk = sb("zbk", [128, 2, 16, 128], BF16)
        tmpZ = sb("tmpZ", [128, 2, 2048], BF16)
        ytil = sb("ytil", [128, 16, 256], BF16)
        woutf = sb("woutf", [128, 16, 1024], BF16)
        yfb2 = sb("yfb2", [128, 2048], BF16)
        uln = sb("uln", [128, 8, 512], BF16)
        lrow = sb("lrow", [1, 512])
        lrow2 = sb("lrow2", [1, 512])
        stg = sb("stg", [128, 512])
        wnscs = []
        for kt in range(16):
            wlf = p.dma("gpsimd", (lambda kt: lambda g: g.dma_start(
                out=woutf[:, kt, :], in_=woutT_d[kt * 128:(kt + 1) * 128, :]))(kt),
                deps=[l1done], name=mk("wlf", kt))
            wnscs.append(p.op("vector", (lambda kt: lambda v: v.tensor_scalar_mul(
                woutf[:, kt, :], woutf[:, kt, :], nw[:, kt:kt + 1]))(kt),
                deps=[wlf, LD[13]], name=mk("nscf", kt)))

        xb2_users = {}
        prev_r1 = l1done
        prev_yt = l1done
        csfl_users = {0: [l1done], 1: [l1done], 2: [l1done], 3: [l1done]}
        prev_sadd = None
        y2_by_c = {}
        yt_by_c = {}
        gmul_by_c = {}
        zsl_by_c = {}
        rs_by_c = {}
        wout_users = []
        op_free = []
        uln_free = []
        uln_half0 = None
        prev_x1s = []
        chunk_mark = l1done
        tile_mark2 = None
        for t2 in range(TT_):
            xldt = p.dma("gpsimd", (lambda t2: lambda g: g.dma_start(
                out=xb[:],
                in_=x_d[:, 3 + t2 * 512:3 + (t2 + 1) * 512].rearrange("(kc p) t -> p kc t", p=128)))(t2),
                deps=xb2_users.get(t2 - 1, []) + (xb_users[TT_ - 1] if t2 == 0 else []),
                name=mk("xld2", t2))
            xb2_users[t2] = []
            for c4 in range(4):
                c = t2 * 4 + c4
                cb2 = c % 2
                cm = [chunk_mark]
                yfc = yf if cb2 == 0 else yfb2
                zrb = p.dma("sync", (lambda t2, c4, cb2: lambda e: e.dma_start(
                    out=zbk[:, cb2, :, :],
                    in_=zdr[:, :, t2 * 512 + c4 * 128: t2 * 512 + (c4 + 1) * 128]))(t2, c4, cb2),
                    deps=[l1done] + ([zsl_hist[c - 2]] if (c - 2) in zsl_hist else []),
                    name=mk("zrb2", c))
                ctrb = p.dma("sync", (lambda c, cb2: lambda e: e.dma_start(
                    out=ctl2[:, cb2, :], in_=ctdr[:, c, :]))(c, cb2),
                    deps=[l1done] + [o for q in range(8) for o in y2_by_c.get((c - 2, q), [])],
                    name=mk("ctrb", c))
                ct_ops = [ctrb] * 8
                ssc1 = p.op("vector", (lambda c: lambda v: v.tensor_tensor(
                    ltd[:, 0:2048].rearrange("p (h q) -> p h q", h=32),
                    S[:].rearrange("p (h q) -> p h q", h=32),
                    pfxall[:, c, :].unsqueeze(2).broadcast_to([128, 32, 64]), ALU.mult))(c),
                    deps=[sio, pfx_done] + ([prev_sadd] if prev_sadd else cm), name=mk("ssc", c))
                lrb = p.dma("sync", (lambda c: lambda e: e.dma_start(
                    out=Sbf[:], in_=ldr[:, c, :]))(c),
                    deps=[l1done] + cm, name=mk("lrb", c))
                sadd = p.op("vector", lambda v: v.tensor_add(Sbf[:], Sbf[:], ltd[:, 0:2048]),
                            deps=[lrb, ssc1], name=mk("sadd", c))
                prev_sadd = sadd
                yrb = p.dma("gpsimd", (lambda c, yfc: lambda e: e.dma_start(
                    out=yfc[:], in_=yfdr[:, c, :]))(c, yfc),
                    deps=[l1done] + yt_by_c.get(c - 2, []) + gmul_by_c.get(c - 2, []),
                    name=mk("yrb", c))
                ye2s = []
                for hp in range(16):
                    slot = hp % 2
                    c0_ = 128
                    ydeps = [sadd, ct_ops[(2 * hp) // 4]]
                    if hp >= 2:
                        ydeps.append(ye2s[hp - 2])
                    else:
                        ydeps += cm
                    ymc = p.op("tensor", (lambda hp, slot, c0_, cb2: lambda te: te.matmul(
                        pdd[0:64, slot, c0_:c0_ + 128], Sbf[:, hp * 128:hp * 128 + 64],
                        ctl2[:, cb2, (2 * hp) * 128:(2 * hp + 1) * 128],
                        start=True, stop=True, tile_position=(0, 0)))(hp, slot, c0_, cb2),
                        deps=ydeps, name=mk("y2a", c, hp))
                    ymd = p.op("tensor", (lambda hp, slot, c0_, cb2: lambda te: te.matmul(
                        pdd[64:128, slot, c0_:c0_ + 128], Sbf[:, hp * 128 + 64:hp * 128 + 128],
                        ctl2[:, cb2, (2 * hp + 1) * 128:(2 * hp + 2) * 128],
                        start=True, stop=True, tile_position=(0, 64)))(hp, slot, c0_, cb2),
                        deps=[ct_ops[(2 * hp + 1) // 4]], name=mk("y2b", c, hp))
                    ev = p.op("vector", (lambda hp, slot, c0_, yfc: lambda v: v.tensor_add(
                        yfc[:, hp * 128:(hp + 1) * 128], yfc[:, hp * 128:(hp + 1) * 128],
                        pdd[:, slot, c0_:c0_ + 128]))(hp, slot, c0_, yfc),
                        deps=[ymc, ymd, yrb], name=mk("ye2", c, hp))
                    ye2s.append(ev)
                    y2_by_c.setdefault((c, (2 * hp) // 4), []).append(ymc)
                    y2_by_c.setdefault((c, (2 * hp + 1) // 4), []).append(ymd)
                zsl = p.op("scalar", (lambda cb2: lambda s: s.activation(
                    tmpZ[:, cb2, :], zbk[:, cb2, :, :].rearrange("p a b -> p (a b)"),
                    AF.Tanh, scale=0.5))(cb2),
                    deps=[zrb] + rs_by_c.get(c - 2, []) + (cm if c4 == 0 else []),
                    name=mk("zth", c))
                zs2 = p.op("vector", (lambda cb2: lambda v: v.scalar_tensor_tensor(
                    tmpZ[:, cb2, :], tmpZ[:, cb2, :], 1.0,
                    zbk[:, cb2, :, :].rearrange("p a b -> p (a b)"),
                    ALU.add, ALU.mult))(cb2), deps=[zsl], name=mk("zs2", c))
                zsl_hist[c] = zs2
                gmul = p.op("vector", (lambda cb2, yfc: lambda v: v.tensor_mul(
                    yfc[:], yfc[:], tmpZ[:, cb2, :]))(cb2, yfc),
                    deps=[zs2] + ye2s, name=mk("gmul", c))
                gmul_by_c[c] = [gmul]
                gsqo = p.op("scalar", (lambda cb2, yfc: lambda s: s.activation(
                    tmpZ[:, cb2, :], yfc[:], AF.Square))(cb2, yfc),
                    deps=[gmul], name=mk("gsq", c))
                rs_last = None
                for m16 in range(16):
                    rs_last = p.op("tensor", (lambda m16, cb2: lambda te: te.matmul(
                        pcv[0:1, 0, 0:128], ones128[:], tmpZ[:, cb2, m16 * 128:(m16 + 1) * 128],
                        start=(m16 == 0), stop=(m16 == 15)))(m16, cb2),
                        deps=([gsqo, prev_r1] + cm if m16 == 0 else []),
                        name=mk("rs", c, m16))
                rs_by_c[c] = [rs_last]
                r1 = p.op("vector", (lambda cb2: lambda v: v.tensor_scalar(
                    stg[0:1, cb2 * 256:cb2 * 256 + 128], pcv[0:1, 0, 0:128],
                    1.0 / 2048.0, 4e-5, ALU.mult, ALU.add))(cb2),
                    deps=[rs_last], name=mk("r1", c))
                prev_r1 = r1
                r2 = p.op("scalar", (lambda cb2: lambda s: s.activation(
                    stg[0:1, cb2 * 256 + 128:cb2 * 256 + 256],
                    stg[0:1, cb2 * 256:cb2 * 256 + 128], AF.Ln))(cb2),
                          deps=[r1], name=mk("r2", c))
                r3 = p.op("scalar", (lambda cb2: lambda s: s.activation(
                    stg[0:1, cb2 * 256:cb2 * 256 + 128],
                    stg[0:1, cb2 * 256 + 128:cb2 * 256 + 256],
                    AF.Exp, scale=-0.5))(cb2),
                          deps=[r2], name=mk("r3", c))
                rb = p.op("tensor", (lambda cb2: lambda te: te.matmul(
                    pcv[:, 1, 0:128], onesf[:], stg[0:1, cb2 * 256:cb2 * 256 + 128],
                    start=True, stop=True))(cb2),
                    deps=[r3, prev_yt] + cm, name=mk("rb", c))
                yt = p.op("vector", (lambda c4, yfc: lambda v: v.tensor_tensor(
                    ytil[:, :, (c4 % 2) * 128:(c4 % 2) * 128 + 128],
                    yfc[:].rearrange("p (a b) -> p a b", a=16),
                    pcv[:, 1, 0:128].unsqueeze(1).broadcast_to([128, 16, 128]),
                    ALU.mult))(c4, yfc), deps=[rb, gmul] + wout_users, name=mk("yt", c))
                yt_by_c[c] = [yt]
                prev_yt = yt
                chunk_mark = p.marker("vector",
                                      [yt, r1, zs2, gsqo, sadd] + ye2s + ct_ops,
                                      name=mk("cmark2", c))
                if c4 % 2 == 1:
                    hf = c4 // 2
                    op_evs = []
                    wout_users = []
                    for kt in range(16):
                        for m in range(8):
                            st_ = (kt == 0)
                            sp_ = (kt == 15)
                            kdeps = []
                            if m == 0:
                                kdeps = [yt, chunk_mark] + (wnscs if (t2 == 0 and hf == 0) else [])
                                if kt == 0:
                                    kdeps += op_free
                            lastk = p.op("tensor", (lambda m, kt, st_, sp_: lambda te: te.matmul(
                                pbanks[m][:, 0:256],
                                woutf[:, kt, m * 128:(m + 1) * 128],
                                ytil[:, kt, :],
                                start=st_, stop=sp_))(m, kt, st_, sp_),
                                deps=kdeps, name=mk("op", t2, hf, kt, m))
                            if kt == 15:
                                wout_users.append(lastk)
                                ev = p.op("vector", (lambda m, hf: lambda v: v.scalar_tensor_tensor(
                                    uln[:, m, hf * 256:(hf + 1) * 256], pbanks[m][:, 0:256], 1.0,
                                    xb[:, m, hf * 256:(hf + 1) * 256],
                                    ALU.mult, ALU.add))(m, hf), deps=[lastk, xldt] + uln_free,
                                    name=mk("ope", t2, hf, m))
                                op_evs.append(ev)
                    xb2_users[t2].extend(op_evs)
                    op_free = op_evs
                    chunk_mark = p.marker("vector", [chunk_mark] + op_evs, name=mk("omark", t2, hf))
                    if hf == 0:
                        uln_half0 = op_evs
                    else:
                        uln_free = []
                        lns_last = None
                        lnq_last = None
                        for m in range(8):
                            sq_ = p.op("scalar", (lambda m: lambda s: s.activation(
                                tmpZ[:, 0, 0:512], uln[:, m, :], AF.Square))(m),
                                deps=[op_evs[m], uln_half0[m]] + ([lnq_last] if lnq_last else []) +
                                     ((rs_by_c.get(t2 * 4 + 2, []) + rs_by_c.get(t2 * 4 + 3, []))
                                      if m == 0 else []),
                                name=mk("usq", t2, m))
                            lns_last = p.op("tensor", (lambda m: lambda te: te.matmul(
                                pcv[0:1, 0, 0:512], ones128[:], uln[:, m, :],
                                start=(m == 0), stop=(m == 7)))(m),
                                deps=[op_evs[m], uln_half0[m]] + (op_evs if m == 0 else []),
                                name=mk("lns", t2, m))
                            lnq_last = p.op("tensor", (lambda m: lambda te: te.matmul(
                                pcv[0:1, 1, 0:512], ones128[:], tmpZ[:, 0, 0:512],
                                start=(m == 0), stop=(m == 7)))(m), deps=[sq_], name=mk("lnq", t2, m))
                        l1 = p.op("vector", lambda v: v.tensor_scalar_mul(lrow[:], pcv[0:1, 0, 0:512], 1.0 / 1024.0),
                                  deps=[lns_last], name=mk("l1", t2))
                        l2 = p.op("vector", lambda v: v.tensor_scalar_mul(lrow2[:], pcv[0:1, 1, 0:512], 1.0 / 1024.0),
                                  deps=[lnq_last], name=mk("l2", t2))
                        l3 = p.op("vector", lambda v: v.tensor_mul(stg[0:1, :], lrow[:], lrow[:]),
                                  deps=[l1], name=mk("l3", t2))
                        l4 = p.op("vector", lambda v: v.tensor_scalar_add(
                            lrow2[:], lrow2[:], 1e-5),
                            deps=[l2], name=mk("l4e", t2))
                        l4 = p.op("vector", lambda v: v.tensor_sub(lrow2[:], lrow2[:], stg[0:1, :]),
                                  deps=[l4, l3], name=mk("l4", t2))
                        l5 = p.op("scalar", lambda s: s.activation(stg[0:1, :], lrow2[:], AF.Ln),
                                  deps=[l4], name=mk("l5", t2))
                        l6 = p.op("scalar", lambda s: s.activation(lrow2[:], stg[0:1, :], AF.Exp, scale=-0.5),
                                  deps=[l5], name=mk("l6", t2))
                        l7 = p.op("vector", lambda v: v.tensor_mul(lrow[:], lrow[:], lrow2[:]),
                                  deps=[l6], name=mk("l7", t2))
                        b1 = p.op("tensor", lambda te: te.matmul(
                            pcv[:, 0, :], onesf[:], lrow2[:], start=True, stop=True),
                            deps=[l7, l1, lns_last], name=mk("b1", t2))
                        b2 = p.op("tensor", lambda te: te.matmul(
                            pcv[:, 1, :], onesf[:], lrow[:], start=True, stop=True),
                            deps=[l7, l2, lnq_last], name=mk("b2", t2))
                        sts = []
                        prev_st = None
                        for m in range(8):
                            a1 = p.op("vector", (lambda m: lambda v: v.tensor_mul(
                                stg[:], uln[:, m, :], pcv[:, 0, :]))(m),
                                deps=[b1] + ([prev_st] if prev_st else []), name=mk("a1", t2, m))
                            a2 = p.op("vector", (lambda m: lambda v: v.tensor_sub(
                                stg[:], stg[:], pcv[:, 1, :]))(m),
                                deps=[a1, b2], name=mk("a2", t2, m))
                            x1s = p.dma("sync", (lambda t2, m: lambda e: e.dma_start(
                                out=x1_d[m * 128:(m + 1) * 128, t2 * 512:(t2 + 1) * 512], in_=stg[:]))(t2, m),
                                deps=[a2], name=mk("x1s", t2, m))
                            prev_st = x1s
                            sts.append(x1s)
                            uln_free.append(x1s)
                        prev_x1s = sts
                        tile_mark2 = p.marker("vector", sts + op_evs + [b1, b2], name=mk("tmark2", t2))
                        chunk_mark = tile_mark2

        p.op("sync", lambda e: e.nop(), deps=prev_x1s, name="fin")
    return p.phase_wrap(start_ix, gate, name=pre + "_done")


def build_mamba2(T=2048):
    nc = bass.Bass("TRN2", target_bir_lowering=False)
    p = Prog(nc)
    specs = tuple(s for s in _MAMBA_DRAM if s[0] != "x")
    dr = declare_dram(nc, specs, T)
    dr["x"] = nc.dram_tensor("x", [1024, T + 3], F32, kind="ExternalInput").ap()
    iso_d = nc.dram_tensor("iso", [128, 1], F32, kind="ExternalInput").ap()
    x1_d = nc.dram_tensor("x1", [1024, T], F32, kind="ExternalOutput").ap()
    from contextlib import ExitStack
    ses = ExitStack()
    with ses:
        sems, dma_pools = make_sems(nc, ses)
        emit_mamba2(nc, p, dr, x1_d, iso_d, T)
        with nc.Block() as block:
            p.emit(block, sems, dma_pools)
    return nc


def launch_mamba2(inputs):
    x = np.asarray(inputs["L0_attn_x"])
    w = prep_mamba_weights(inputs)
    if "mamba2" not in _cache:
        _cache["mamba2"] = build_mamba2(2048)
    nc = _cache["mamba2"]
    in_maps = []
    for core in range(8):
        b, half = core // 2, core % 2
        xe = np.zeros((1024, 2051), dtype=np.float32)
        xe[:, 3:] = x[b, half * 2048:(half + 1) * 2048, :].T
        if half == 1:
            xe[:, 0:3] = x[b, 2045:2048, :].T
        m = {"x": xe, "iso": np.full((128, 1), float(half), dtype=np.float32)}
        m.update(w)
        in_maps.append(m)
    res = run_bass_kernel_spmd(nc, in_maps, core_ids=list(range(8)))
    x1 = np.stack([np.concatenate([np.asarray(res.results[b * 2 + h]["x1"]).T for h in range(2)], axis=0)
                   for b in range(4)], axis=0)
    return x1
